# version 02
# baseline (speedup 1.0000x reference)
import numpy as np
import ml_dtypes
import concourse.bass as bass
import concourse.mybir as mybir
from concourse.bass_utils import run_bass_kernel_spmd
from contextlib import ExitStack

F32 = mybir.dt.float32
BF16 = mybir.dt.bfloat16
I32 = mybir.dt.int32
AF = mybir.ActivationFunctionType
ALU = mybir.AluOpType
AXL = mybir.AxisListType

B, S, D = 16, 2048, 1024
NC_N = 8
BPC = B // NC_N
NCH = D // 128
NKC = S // 128
NTG = S // 512
NBIG = -3.0e38

_PROGRAM_CACHE = {}
DBG = set()
DBG_DUMP = [None]
DBG_ATT = [None]
DBG_DRAM = [None]
DBG_ATT2 = [None]
DBG_DRAM2 = [None]

_NAME_SEQ = [0]


def _u(name):
    _NAME_SEQ[0] += 1
    return f"{name}_{_NAME_SEQ[0]}"




class Ctx:
    def __init__(self, nc, block, stack):
        self.nc = nc
        self.block = block
        self.stack = stack
        self.sems = {}
        self.cnt = {}
        for name in ("d", "p", "v", "a", "g", "t", "o", "s"):
            self.sems[name] = stack.enter_context(nc.semaphore("sem_" + name))
            self.cnt[name] = 0

    def inc(self, inst, name, n=1):
        inst.then_inc(self.sems[name], n)
        self.cnt[name] += n
        return inst

    def dma(self, eng, out, in_, name="d"):
        return self.inc(eng.dma_start(out=out, in_=in_), name, 16)

    def wait(self, eng, name):
        eng.wait_ge(self.sems[name], self.cnt[name])

    def wait_all(self, eng):
        for name in ("d", "p", "v", "a", "g", "t", "o", "s"):
            if self.cnt[name]:
                eng.wait_ge(self.sems[name], self.cnt[name])


def _fence(ctx):
    block = ctx.block

    @block.tensor
    def _(e):
        ctx.inc(e.drain(), "p")

    @block.vector
    def _(e):
        ctx.inc(e.drain(), "v")

    @block.scalar
    def _(e):
        ctx.inc(e.drain(), "a")

    @block.gpsimd
    def _(e):
        ctx.inc(e.drain(), "g")

    @block.sync
    def _(e):
        ctx.inc(e.drain(), "s")

    for emit in (block.tensor, block.vector, block.scalar, block.gpsimd, block.sync):
        @emit
        def _(e):
            ctx.wait_all(e)


def _make_consts(nc):
    c = {}
    c["ident"] = nc.inline_tensor(np.eye(128, dtype=ml_dtypes.bfloat16), "c_ident")
    c["ones_row_f"] = nc.inline_tensor(np.ones((1, 128), np.float32), "c_onesrf")
    c["ones_col_f"] = nc.inline_tensor(np.ones((128, 1), np.float32), "c_onescf")
    mask = np.triu(np.ones((128, 128), np.float32), k=1) * (-1e9)
    c["causal"] = nc.inline_tensor(mask, "c_causal")
    c["expmask"] = nc.inline_tensor(np.full((128, 1), 0x7F800000, np.int32), "c_expmask")
    return c


def _load_consts(ctx, cts):
    nc, block = ctx.nc, ctx.block
    sb = {}
    sb["ident"] = ctx.stack.enter_context(nc.sbuf_tensor(_u("cs_ident"), [128, 128], BF16))
    sb["ones_row_f"] = ctx.stack.enter_context(nc.sbuf_tensor(_u("cs_onesrf"), [1, 128], F32))
    sb["ones_col_f"] = ctx.stack.enter_context(nc.sbuf_tensor(_u("cs_onescf"), [128, 1], F32))
    sb["causal"] = ctx.stack.enter_context(nc.sbuf_tensor(_u("cs_causal"), [128, 128], F32))
    sb["expmask"] = ctx.stack.enter_context(nc.sbuf_tensor(_u("cs_expmask"), [128, 1], I32))
    sb["eps"] = ctx.stack.enter_context(nc.sbuf_tensor(_u("cs_eps"), [128, 1], F32))

    @block.gpsimd
    def _(g):
        for k in ("ident", "ones_row_f", "ones_col_f", "causal", "expmask"):
            ctx.dma(g, sb[k][:, :], cts[k].ap()[:, :])

    @block.vector
    def _(v):
        v.memset(sb["eps"][:, :], 1e-5)
        ctx.inc(v.drain(), "v")
    return sb


def load_weight_bf16(ctx, w_dram, nin, c0, c1, name):
    nc, block = ctx.nc, ctx.block
    nchk = nin // 128
    ncol = c1 - c0
    wbf = ctx.stack.enter_context(nc.sbuf_tensor(_u(name + "_bf"), [128, nchk, ncol], BF16))
    sub = min(ncol, 512)
    with nc.sbuf_tensor(_u(name + "_f32"), [128, sub], F32) as wf:
        @block.gpsimd
        def _(g):
            ctx.wait_all(g)
            for ic in range(nchk):
                for s0 in range(0, ncol, sub):
                    ctx.dma(g, wf[:, :], w_dram[ic * 128:(ic + 1) * 128,
                                                c0 + s0:c0 + s0 + sub])
                    ctx.wait(g, "d")
                    i = g.tensor_copy(out=wbf[:, ic, s0:s0 + sub], in_=wf[:, :])
                    ctx.inc(i, "g")
                    ctx.wait(g, "g")
            ctx.inc(g.drain(), "g")
        _fence(ctx)
    return wbf


def emit_proj_featmajor(ctx, wbf, nin, nout, src_bf, evac, psname, oc0=0):
    nc, block = ctx.nc, ctx.block
    nic = nin // 128
    noc = nout // 128
    with nc.psum_tensor(_u(psname), [128, 2, 512], F32) as ps:
        for oc_ in range(noc):
            for tg in range(NTG):
                par = (oc_ * NTG + tg) % 2

                @block.tensor
                def _(te, oc=oc_, tg=tg, par=par):
                    if oc == 0 and tg == 0:
                        ctx.wait_all(te)
                    if oc * NTG + tg >= 2:
                        ctx.wait(te, "v")
                    for ic in range(nic):
                        i = te.matmul(ps[:, par, :], wbf[:, ic, oc * 128:(oc + 1) * 128],
                                      src_bf[:, ic, tg * 512:(tg + 1) * 512],
                                      start=(ic == 0), stop=(ic == nic - 1))
                    ctx.inc(i, "p")

                @block.vector
                def _(v, oc=oc_, tg=tg, par=par):
                    ctx.wait(v, "p")
                    evac(v, ps[:, par, :], oc0 + oc, tg)
                    ctx.inc(v.drain(), "v")
    _fence(ctx)


def emit_load_stream(ctx, x_dram, xT, xbf=None):
    nc, block = ctx.nc, ctx.block

    @block.gpsimd
    def _(g):
        ctx.wait_all(g)
        for c in range(NCH):
            ctx.dma(g, xT[:, c, :], x_dram[c * 128:(c + 1) * 128, :])
        if xbf is not None:
            ctx.wait(g, "d")
            for c in range(NCH):
                g.tensor_copy(out=xbf[:, c, :], in_=xT[:, c, :])
            ctx.inc(g.drain(), "g")
    _fence(ctx)


def emit_refresh_xbf(ctx, xT, xbf):
    @ctx.block.gpsimd
    def _(g):
        ctx.wait_all(g)
        for c in range(NCH):
            g.tensor_copy(out=xbf[:, c, :], in_=xT[:, c, :])
        ctx.inc(g.drain(), "g")
    _fence(ctx)


def emit_store_stream(ctx, xT, y_dram):
    nc, block = ctx.nc, ctx.block

    @block.gpsimd
    def _(g):
        ctx.wait_all(g)
        for c in range(NCH):
            ctx.dma(g, y_dram[c * 128:(c + 1) * 128, :], xT[:, c, :], name="o")
        ctx.wait(g, "o")
    _fence(ctx)


def emit_rope(ctx, xro_dram, tabs, xrbf):
    nc, block = ctx.nc, ctx.block
    with ExitStack() as st:
        xe = st.enter_context(nc.sbuf_tensor(_u("rp_xe"), [128, 2048], F32))
        xo = st.enter_context(nc.sbuf_tensor(_u("rp_xo"), [128, 2048], F32))
        tt = {k: st.enter_context(nc.sbuf_tensor(_u("rp_" + k), [128, 2048], F32))
              for k in ("ch", "cl", "sh", "sl")}
        m1 = st.enter_context(nc.sbuf_tensor(_u("rp_m1"), [128, 2048], F32))
        m2 = st.enter_context(nc.sbuf_tensor(_u("rp_m2"), [128, 2048], F32))
        p1 = st.enter_context(nc.sbuf_tensor(_u("rp_p1"), [128, 2048], F32))
        p2 = st.enter_context(nc.sbuf_tensor(_u("rp_p2"), [128, 2048], F32))
        for c in range(4):
            @block.gpsimd
            def _(g, c=c):
                if c == 0:
                    ctx.wait_all(g)
                else:
                    ctx.wait(g, "v")
                ctx.dma(g, xe[:, :], xro_dram[c * 128:(c + 1) * 128, :])
                ctx.dma(g, xo[:, :], xro_dram[512 + c * 128:512 + (c + 1) * 128, :])
                for k, nm in (("ch", "coshi"), ("cl", "coslo"), ("sh", "sinhi"), ("sl", "sinlo")):
                    ctx.dma(g, tt[k][:, :], tabs[nm][c * 128:(c + 1) * 128, :])

            @block.vector
            def _(v, c=c):
                ctx.wait(v, "d")
                v.tensor_tensor(out=m1[:, :], in0=xe[:, :], in1=tt["ch"][:, :], op=ALU.mult)
                v.tensor_tensor(out=m2[:, :], in0=xo[:, :], in1=tt["sh"][:, :], op=ALU.mult)
                v.tensor_tensor(out=p1[:, :], in0=m1[:, :], in1=m2[:, :], op=ALU.subtract)
                v.tensor_tensor(out=m1[:, :], in0=xe[:, :], in1=tt["cl"][:, :], op=ALU.mult)
                v.tensor_tensor(out=m2[:, :], in0=xo[:, :], in1=tt["sl"][:, :], op=ALU.mult)
                v.tensor_tensor(out=p2[:, :], in0=m1[:, :], in1=m2[:, :], op=ALU.subtract)
                v.tensor_tensor(out=p1[:, :], in0=p1[:, :], in1=p2[:, :], op=ALU.add)
                v.tensor_copy(out=xrbf[:, c, :], in_=p1[:, :])
                v.tensor_tensor(out=m1[:, :], in0=xe[:, :], in1=tt["sh"][:, :], op=ALU.mult)
                v.tensor_tensor(out=m2[:, :], in0=xo[:, :], in1=tt["ch"][:, :], op=ALU.mult)
                v.tensor_tensor(out=p1[:, :], in0=m1[:, :], in1=m2[:, :], op=ALU.add)
                v.tensor_tensor(out=m1[:, :], in0=xe[:, :], in1=tt["sl"][:, :], op=ALU.mult)
                v.tensor_tensor(out=m2[:, :], in0=xo[:, :], in1=tt["cl"][:, :], op=ALU.mult)
                v.tensor_tensor(out=p2[:, :], in0=m1[:, :], in1=m2[:, :], op=ALU.add)
                v.tensor_tensor(out=p1[:, :], in0=p1[:, :], in1=p2[:, :], op=ALU.add)
                v.tensor_copy(out=xrbf[:, 4 + c, :], in_=p1[:, :])
                ctx.inc(v.drain(), "v")
    _fence(ctx)


def emit_layernorm_inplace(ctx, consts, xT):
    nc, block = ctx.nc, ctx.block
    with ExitStack() as st:
        sq = st.enter_context(nc.sbuf_tensor(_u("ln_sq"), [128, 1024], F32))
        stats = st.enter_context(nc.sbuf_tensor(_u("ln_stats"), [1, 4, 1024], F32))
        bc = st.enter_context(nc.sbuf_tensor(_u("ln_bc"), [128, 2, 1024], F32))
        ps_mu = st.enter_context(nc.psum_tensor(_u("ln_psmu"), [1, 1024], F32))
        ps_sq = st.enter_context(nc.psum_tensor(_u("ln_pssq"), [1, 1024], F32))
        ps_bc = st.enter_context(nc.psum_tensor(_u("ln_psbc"), [128, 2, 1024], F32))
        for th in range(2):
            t0 = th * 1024

            for c in range(NCH):
                @block.scalar
                def _(a, th=th, t0=t0, c=c):
                    if th == 0 and c == 0:
                        ctx.wait_all(a)
                    elif c == 0:
                        ctx.wait(a, "v")
                    else:
                        ctx.wait(a, "p")
                    i = a.activation(out=sq[:, :], in_=xT[:, c, t0:t0 + 1024],
                                     func=AF.Square, bias=0.0, scale=1.0)
                    ctx.inc(i, "a")

                @block.tensor
                def _(te, th=th, t0=t0, c=c):
                    ctx.wait(te, "a")
                    for g2 in range(2):
                        te.matmul(ps_mu[:, g2 * 512:(g2 + 1) * 512], consts["ones_col_f"][:, :],
                                  xT[:, c, t0 + g2 * 512:t0 + (g2 + 1) * 512],
                                  start=(c == 0), stop=(c == NCH - 1))
                        i = te.matmul(ps_sq[:, g2 * 512:(g2 + 1) * 512], consts["ones_col_f"][:, :],
                                      sq[:, g2 * 512:(g2 + 1) * 512],
                                      start=(c == 0), stop=(c == NCH - 1))
                    ctx.inc(i, "p")

            @block.vector
            def _(v, th=th):
                ctx.wait(v, "p")
                v.tensor_scalar_mul(stats[0:1, 0, :], ps_mu[:, :], 1.0 / D)
                v.tensor_scalar_mul(stats[0:1, 1, :], ps_sq[:, :], 1.0 / D)
                v.tensor_tensor(out=stats[0:1, 3, :], in0=stats[0:1, 0, :],
                                in1=stats[0:1, 0, :], op=ALU.mult)
                v.tensor_tensor(out=stats[0:1, 1, :], in0=stats[0:1, 1, :],
                                in1=stats[0:1, 3, :], op=ALU.subtract)
                ctx.inc(v.drain(), "v")

            @block.scalar
            def _(a, th=th):
                ctx.wait(a, "v")
                i = a.activation(out=stats[0:1, 3, :], in_=stats[0:1, 1, :],
                                 func=AF.Sqrt, bias=consts["eps"][0:1, 0:1], scale=1.0)
                ctx.inc(i, "a")

            @block.vector
            def _(v, th=th):
                ctx.wait(v, "a")
                v.reciprocal(out=stats[0:1, 1, :], in_=stats[0:1, 3, :])
                v.tensor_tensor(out=stats[0:1, 2, :], in0=stats[0:1, 0, :],
                                in1=stats[0:1, 1, :], op=ALU.mult)
                v.tensor_scalar_mul(stats[0:1, 2, :], stats[0:1, 2, :], -1.0)
                ctx.inc(v.drain(), "v")

            @block.tensor
            def _(te, th=th):
                ctx.wait(te, "v")
                for g2 in range(2):
                    te.matmul(ps_bc[:, 0, g2 * 512:(g2 + 1) * 512], consts["ones_row_f"][0:1, :],
                              stats[0:1, 1, g2 * 512:(g2 + 1) * 512], start=True, stop=True)
                    i = te.matmul(ps_bc[:, 1, g2 * 512:(g2 + 1) * 512], consts["ones_row_f"][0:1, :],
                                  stats[0:1, 2, g2 * 512:(g2 + 1) * 512], start=True, stop=True)
                ctx.inc(i, "p")

            @block.vector
            def _(v, th=th, t0=t0):
                ctx.wait(v, "p")
                v.tensor_copy(out=bc[:, 0, :], in_=ps_bc[:, 0, :])
                v.tensor_copy(out=bc[:, 1, :], in_=ps_bc[:, 1, :])
                for c in range(NCH):
                    v.tensor_tensor(out=xT[:, c, t0:t0 + 1024], in0=xT[:, c, t0:t0 + 1024],
                                    in1=bc[:, 0, :], op=ALU.mult)
                    v.tensor_tensor(out=xT[:, c, t0:t0 + 1024], in0=xT[:, c, t0:t0 + 1024],
                                    in1=bc[:, 1, :], op=ALU.add)
                ctx.inc(v.drain(), "v")
    _fence(ctx)


def emit_negm(ctx, consts, qt_bf, kt_bf, layer, negm_bc, scr_dram, tag):
    nc, block = ctx.nc, ctx.block
    kch = 4 if layer == 0 else 8
    with ExitStack() as st:
        mq = st.enter_context(nc.sbuf_tensor(_u(tag + "_mq"), [128, 8], F32))
        mk = st.enter_context(nc.sbuf_tensor(_u(tag + "_mk"), [128, 8], F32))
        flat = st.enter_context(nc.sbuf_tensor(_u(tag + "_flat"), [1, 2048], F32))
        mrow = st.enter_context(nc.sbuf_tensor(_u(tag + "_mrow"), [1, 3, 16], F32))
        ps = st.enter_context(nc.psum_tensor(_u(tag + "_ps"), [128, 16], F32))
        if "dump_negm" in DBG and tag == "l0nm":
            DBG_ATT2[0] = st.enter_context(nc.sbuf_tensor(_u("nm_dbg"), [128, 2128], F32))

        @block.vector
        def _(v):
            ctx.wait_all(v)
            v.memset(mk[:, :], 0.0)
            for c in range(NCH):
                v.reduce_max(out=mq[:, c:c + 1], in_=qt_bf[:, c, :], axis=AXL.X,
                             apply_absolute_value=True)
            for c in range(kch):
                v.reduce_max(out=mk[:, c:c + 1], in_=kt_bf[:, c, :], axis=AXL.X,
                             apply_absolute_value=True)
            ctx.inc(v.drain(), "v")

        @block.gpsimd
        def _(g):
            ctx.wait(g, "v")
            ctx.dma(g, scr_dram[0:128, :], mq[:, :])
            ctx.dma(g, scr_dram[128:256, :], mk[:, :])

        @block.gpsimd
        def _(g):
            ctx.wait(g, "d")
            ctx.dma(g, flat[0:1, :], scr_dram.rearrange("(o p) c -> o (p c)", o=1)[0:1, :])

        @block.vector
        def _(v):
            ctx.wait(v, "d")
            for h in range(16):
                st_q = 64 * (h % 2) * 8 + (h // 2)
                v.reduce_max(out=mrow[0:1, 0, h:h + 1],
                             in_=flat[0:1, st_q:st_q + 505:8], axis=AXL.X)
            for h in range(16):
                if layer == 0:
                    gk = h // 4
                    st_k = 1024 + gk
                    v.reduce_max(out=mrow[0:1, 1, h:h + 1],
                                 in_=flat[0:1, st_k:st_k + 505:8], axis=AXL.X)
                else:
                    st_k = 1024 + 64 * (h % 2) * 8 + (h // 2)
                    v.reduce_max(out=mrow[0:1, 1, h:h + 1],
                                 in_=flat[0:1, st_k:st_k + 505:8], axis=AXL.X)
            ctx.inc(v.drain(), "v")
            ctx.wait(v, "v")
            v.scalar_tensor_tensor(out=mrow[0:1, 2, :], in0=mrow[0:1, 0, :],
                                   scalar=-64.0, in1=mrow[0:1, 1, :],
                                   op0=ALU.mult, op1=ALU.mult)
            ctx.inc(v.drain(), "v")

        @block.tensor
        def _(te):
            ctx.wait(te, "v")
            i = te.matmul(ps[:, :], consts["ones_row_f"][0:1, :], mrow[0:1, 2, :],
                          start=True, stop=True)
            ctx.inc(i, "p")

        @block.vector
        def _(v):
            ctx.wait(v, "p")
            v.tensor_copy(out=negm_bc[:, :], in_=ps[:, :])
            if "dump_negm" in DBG and tag == "l0nm":
                v.tensor_copy(out=DBG_ATT2[0][:, 0:8], in_=mq[:, :])
                v.tensor_copy(out=DBG_ATT2[0][:, 8:16], in_=mk[:, :])
                v.tensor_copy(out=DBG_ATT2[0][0:1, 16:2064], in_=flat[0:1, :])
                v.tensor_copy(out=DBG_ATT2[0][0:1, 2064:2112], in_=mrow[0:1, :, :])
                v.tensor_copy(out=DBG_ATT2[0][:, 2112:2128], in_=negm_bc[:, :])
            ctx.inc(v.drain(), "v")

        if "dump_negm" in DBG and tag == "l0nm":
            @block.gpsimd
            def _(g):
                ctx.wait(g, "v")
                ctx.dma(g, DBG_DRAM2[0][:, :], DBG_ATT2[0][:, :], name="o")
                ctx.wait(g, "o")
    _fence(ctx)


def emit_attention(ctx, consts, layer, qt_bf, kt_bf, v_bf, negm_bc, ctx_bf):
    nc, block = ctx.nc, ctx.block
    with ExitStack() as st:
        E = st.enter_context(nc.sbuf_tensor(_u("at_E"), [128, 2048], BF16))
        W = st.enter_context(nc.sbuf_tensor(_u("at_W"), [128, 2048], BF16))
        PT = st.enter_context(nc.sbuf_tensor(_u("at_PT"), [128, 16, 128], BF16))
        m8 = st.enter_context(nc.sbuf_tensor(_u("at_m8"), [128, 8], F32))
        thr = st.enter_context(nc.sbuf_tensor(_u("at_thr"), [128, 1], F32))
        rs = st.enter_context(nc.sbuf_tensor(_u("at_rs"), [128, 2], F32))
        rc = st.enter_context(nc.sbuf_tensor(_u("at_rc"), [128, 1], F32))
        rm = st.enter_context(nc.sbuf_tensor(_u("at_rm"), [128, 3], F32))
        if "dump_att" in DBG:
            DBG_ATT[0] = st.enter_context(nc.sbuf_tensor(_u("at_dbg"), [128, 4200], F32))
        ps_s = st.enter_context(nc.psum_tensor(_u("at_pss"), [128, 2, 1024], F32))
        ps_c = st.enter_context(nc.psum_tensor(_u("at_psc"), [128, 128], F32))

        for h in range(16):
            grp = h // 4
            qch, qba = h // 2, 64 * (h % 2)
            for qi in range(NKC):
                live = (qi + 1) if layer == 0 else NKC
                for half in range(2):
                    k0 = half * 8
                    nliv = min(8, max(0, live - k0))
                    if nliv == 0:
                        continue

                    @block.tensor
                    def _(te, h=h, qi=qi, half=half, k0=k0, nliv=nliv, qch=qch,
                          qba=qba, grp=grp):
                        if h == 0 and qi == 0 and half == 0:
                            ctx.wait_all(te)
                        ctx.wait(te, "a")
                        lhsT = qt_bf[qba:qba + 64, qch, 128 * qi:128 * (qi + 1)]
                        if layer == 0:
                            rhs_t = kt_bf[qba:qba + 64, grp, :]
                        else:
                            rhs_t = kt_bf[qba:qba + 64, qch, :]
                        for nb in range((nliv * 128 + 511) // 512):
                            n0 = k0 * 128 + nb * 512
                            nn = min(512, (k0 + nliv) * 128 - n0)
                            i = te.matmul(ps_s[:, half, nb * 512:nb * 512 + nn],
                                          lhsT, rhs_t[:, n0:n0 + nn],
                                          start=True, stop=True,
                                          tile_position=(qba, 0))
                        ctx.inc(i, "p")

                    if layer == 0 and k0 <= qi < k0 + 8:
                        @block.vector
                        def _(v, qi=qi, half=half, k0=k0):
                            ctx.wait(v, "p")
                            cp = (qi - k0) * 128
                            v.tensor_tensor(out=ps_s[:, half, cp:cp + 128],
                                            in0=ps_s[:, half, cp:cp + 128],
                                            in1=consts["causal"][:, :], op=ALU.add)
                            ctx.inc(v.drain(), "v")

                    @block.vector
                    def _(v, half=half, nliv=nliv):
                        ctx.wait(v, "p")
                        i = v.reduce_max(out=rm[:, half:half + 1],
                                         in_=ps_s[:, half, 0:nliv * 128],
                                         axis=AXL.X, negate=True)
                        ctx.inc(i, "v")

                nhalf = 1 if live <= 8 else 2

                @block.vector
                def _(v, nhalf=nhalf):
                    ctx.wait(v, "v")
                    if nhalf == 2:
                        i = v.tensor_tensor(out=rm[:, 2:3], in0=rm[:, 0:1],
                                            in1=rm[:, 1:2], op=ALU.min)
                    else:
                        i = v.tensor_copy(out=rm[:, 2:3], in_=rm[:, 0:1])
                    ctx.inc(i, "v")

                for half in range(2):
                    k0 = half * 8
                    nliv = min(8, max(0, live - k0))
                    if nliv == 0:
                        continue

                    @block.scalar
                    def _(a, h=h, half=half, k0=k0, nliv=nliv):
                        ctx.wait(a, "p")
                        ctx.wait(a, "v")
                        ctx.wait(a, "t")
                        i = a.activation(out=E[:, k0 * 128:(k0 + nliv) * 128],
                                         in_=ps_s[:, half, 0:nliv * 128], func=AF.Exp,
                                         bias=rm[:, 2:3], scale=1.0,
                                         accum_out=rs[:, half:half + 1])
                        ctx.inc(i, "a")

                nlive = live * 128
                if layer == 1:
                    @block.gpsimd
                    def _(gp):
                        ctx.wait(gp, "a")
                        i = gp.tensor_copy(out=W[:, :], in_=E[:, :])
                        ctx.inc(i, "g")

                    @block.vector
                    def _(v, h=h, qi=qi):
                        ctx.wait(v, "g")
                        for rnd in range(5):
                            i = v.max(out=m8[:, :], in_=W[:, :])
                            ctx.inc(i, "v")
                            ctx.wait(v, "v")
                            if rnd < 4:
                                v.tensor_scalar(out=PT[:, :, :], in0=W[:, :],
                                                scalar1=m8[:, 7:8], scalar2=NBIG,
                                                op0=ALU.is_ge, op1=ALU.mult)
                                v.tensor_tensor(out=W[:, :], in0=W[:, :],
                                                in1=PT[:, :, :], op=ALU.add)
                        i = v.tensor_copy(out=thr[:, 0:1], in_=m8[:, 5:6])
                        ctx.inc(i, "v")
                        ctx.wait(v, "v")
                        i = v.scalar_tensor_tensor(out=E[:, :], in0=E[:, :],
                                                   scalar=thr[:, 0:1], in1=E[:, :],
                                                   op0=ALU.is_ge, op1=ALU.mult,
                                                   accum_out=rs[:, 0:1])
                        ctx.inc(i, "v")
                        ctx.wait(v, "v")
                        i = v.reciprocal(out=rc[:, 0:1], in_=rs[:, 0:1])
                        ctx.inc(i, "v")
                        ctx.wait(v, "v")
                        v.tensor_scalar_mul(E[:, :], E[:, :], rc[:, 0:1])
                        ctx.inc(v.drain(), "v")
                else:
                    @block.vector
                    def _(v, h=h, qi=qi, nlive=nlive, live=live):
                        ctx.wait(v, "a")
                        if live > 8:
                            i = v.tensor_tensor(out=rs[:, 0:1], in0=rs[:, 0:1],
                                                in1=rs[:, 1:2], op=ALU.add)
                            ctx.inc(i, "v")
                            ctx.wait(v, "v")
                        i = v.reciprocal(out=rc[:, 0:1], in_=rs[:, 0:1])
                        ctx.inc(i, "v")
                        ctx.wait(v, "v")
                        if "dump_att" in DBG and h == 0 and qi == 2:
                            v.tensor_copy(out=DBG_ATT[0][:, 0:nlive], in_=E[:, 0:nlive])
                            v.tensor_copy(out=DBG_ATT[0][:, 2048:2049], in_=rs[:, 0:1])
                            v.tensor_copy(out=DBG_ATT[0][:, 2049:2050], in_=rc[:, 0:1])
                        v.tensor_scalar_mul(E[:, 0:nlive], E[:, 0:nlive], rc[:, 0:1])
                        ctx.inc(v.drain(), "v")

                @block.scalar
                def _(a, live=live):
                    ctx.wait(a, "v")
                    ctx.wait(a, "p")
                    ctx.wait(a, "t")
                    i = a.dma_start(out=PT[:, 0:live, :],
                                    in_=E[:, 0:live * 128], transpose=True)
                    ctx.inc(i, "t", 16)

                @block.tensor
                def _(te, h=h, qi=qi, grp=grp, live=live, qba=qba):
                    ctx.wait(te, "t")
                    ctx.wait(te, "v")
                    for c in range(live):
                        if layer == 0:
                            va = v_bf[:, c, 64 * grp:64 * grp + 64]
                        else:
                            va = v_bf[:, c, 64 * h:64 * h + 64]
                        i = te.matmul(ps_c[qba:qba + 64, :], va, PT[:, c, :],
                                      start=(c == 0), stop=(c == live - 1),
                                      tile_position=(0, qba), skip_group_check=True)
                    ctx.inc(i, "p")

                @block.vector
                def _(v, h=h, qi=qi, qba=qba):
                    ctx.wait(v, "p")
                    v.tensor_copy(out=ctx_bf[qba:qba + 64, h // 2, 128 * qi:128 * (qi + 1)],
                                  in_=ps_c[qba:qba + 64, :])
                    ctx.inc(v.drain(), "v")
        if "dump_att" in DBG:
            @block.gpsimd
            def _(g):
                ctx.wait_all(g)
                ctx.dma(g, DBG_DRAM[0][:, :], DBG_ATT[0][:, :], name="o")
                ctx.wait(g, "o")
        if "dump_ctx" in DBG:
            with nc.sbuf_tensor(_u("dmpc"), [128, 2048], F32) as dmpc:
                for c in range(NCH):
                    @block.gpsimd
                    def _(g, c=c):
                        ctx.wait_all(g)
                        i = g.tensor_copy(out=dmpc[:, :], in_=ctx_bf[:, c, :])
                        ctx.inc(i, "g")
                        ctx.wait(g, "g")
                        ctx.dma(g, DBG_DRAM[0][c * 128:(c + 1) * 128, 0:2048],
                                dmpc[:, :], name="o")
                        ctx.wait(g, "o")
    _fence(ctx)


def emit_l0_attn(ctx, consts, wd, tabs, xT, scr_dram):
    nc, block = ctx.nc, ctx.block
    with ExitStack() as st:
        xr_bf = st.enter_context(nc.sbuf_tensor(_u("l0_xr"), [128, 8, 2048], BF16))
        emit_rope(ctx, wd["xro"], tabs, xr_bf)

        qt_bf = st.enter_context(nc.sbuf_tensor(_u("l0_qt"), [128, 8, 2048], BF16))
        ktd_bf = st.enter_context(nc.sbuf_tensor(_u("l0_kt"), [128, 4, 2048], BF16))
        v_bf = st.enter_context(nc.sbuf_tensor(_u("l0_v"), [128, 16, 256], BF16))

        with ExitStack() as st2:
            saved, ctx.stack = ctx.stack, st2
            wq = load_weight_bf16(ctx, wd["l0wq"], 1024, 0, 1024, "l0wq")

            def evq(v, psap, oc, tg):
                v.tensor_copy(out=qt_bf[:, oc, tg * 512:(tg + 1) * 512], in_=psap)
            emit_proj_featmajor(ctx, wq, 1024, 1024, xr_bf, evq, "l0psq")
            ctx.stack = saved

        with ExitStack() as st2:
            saved, ctx.stack = ctx.stack, st2
            wk = load_weight_bf16(ctx, wd["l0wk"], 1024, 0, 256, "l0wk")

            def evk(v, psap, oc, tg):
                for gg in range(2):
                    gidx = 2 * oc + gg
                    for hf in range(2):
                        v.tensor_scalar(out=ktd_bf[64 * hf:64 * hf + 64, gidx,
                                                   tg * 512:(tg + 1) * 512],
                                        in0=psap[64 * gg:64 * gg + 64, :], scalar1=0.125,
                                        scalar2=None, op0=ALU.mult)
            emit_proj_featmajor(ctx, wk, 1024, 256, xr_bf, evk, "l0psk")
            ctx.stack = saved

        with ExitStack() as st2:
            saved, ctx.stack = ctx.stack, st2
            wv = load_weight_bf16(ctx, wd["l0wv"], 1024, 0, 256, "l0wv")
            with nc.psum_tensor(_u("l0psv"), [128, 2, 512], F32) as psv:
                for tc in range(NKC):
                    par = tc % 2

                    @block.tensor
                    def _(te, tc=tc, par=par):
                        if tc == 0:
                            ctx.wait_all(te)
                        if tc >= 2:
                            ctx.wait(te, "v")
                        for ic in range(NCH):
                            i = te.matmul(psv[:, par, 0:256],
                                          xr_bf[:, ic, tc * 128:(tc + 1) * 128],
                                          wv[:, ic, :], start=(ic == 0), stop=(ic == NCH - 1))
                        ctx.inc(i, "p")

                    @block.vector
                    def _(v, tc=tc, par=par):
                        ctx.wait(v, "p")
                        v.tensor_copy(out=v_bf[:, tc, :], in_=psv[:, par, 0:256])
                        ctx.inc(v.drain(), "v")
            _fence(ctx)
            ctx.stack = saved

        emit_attention(ctx, consts, 0, qt_bf, ktd_bf, v_bf, None, xr_bf)

        with ExitStack() as st2:
            saved, ctx.stack = ctx.stack, st2
            wo = load_weight_bf16(ctx, wd["l0wo"], 1024, 0, 1024, "l0wo")

            def evo(v, psap, oc, tg):
                v.tensor_tensor(out=xT[:, oc, tg * 512:(tg + 1) * 512],
                                in0=psap, in1=xT[:, oc, tg * 512:(tg + 1) * 512], op=ALU.add)
            emit_proj_featmajor(ctx, wo, 1024, 1024, xr_bf, evo, "l0pso")
            ctx.stack = saved
    emit_layernorm_inplace(ctx, consts, xT)


def emit_l0_ffn(ctx, consts, wd, xT):
    nc, block = ctx.nc, ctx.block
    with ExitStack() as st:
        q_bf = st.enter_context(nc.sbuf_tensor(_u("f0_q"), [128, 8, 2048], BF16))
        sc2 = st.enter_context(nc.sbuf_tensor(_u("f0_sc2"), [128, 2], F32))
        scs = st.enter_context(nc.sbuf_tensor(_u("f0_scs"), [1, 4], F32))
        bias = st.enter_context(nc.sbuf_tensor(_u("f0_bias"), [128, 8], F32))

        @block.gpsimd
        def _(g):
            ctx.wait_all(g)
            ctx.dma(g, scs[0:1, 0:2], wd["l0sc"][0:1, 0:2])
            ctx.dma(g, bias[:, :], wd["l0fb"][:, :])

        @block.vector
        def _(v):
            ctx.wait(v, "d")
            i = v.reciprocal(out=scs[0:1, 2:3], in_=scs[0:1, 0:1])
            ctx.inc(i, "v")
            ctx.wait(v, "v")
            v.tensor_tensor(out=scs[0:1, 3:4], in0=scs[0:1, 0:1],
                            in1=scs[0:1, 1:2], op=ALU.mult)
            ctx.inc(v.drain(), "v")

        with nc.psum_tensor(_u("f0psb"), [128, 2], F32) as psb:
            @block.tensor
            def _(te):
                ctx.wait(te, "v")
                i = te.matmul(psb[:, :], consts["ones_row_f"][0:1, :], scs[0:1, 2:4],
                              start=True, stop=True)
                ctx.inc(i, "p")

            @block.vector
            def _(v):
                ctx.wait(v, "p")
                v.tensor_copy(out=sc2[:, :], in_=psb[:, :])
                ctx.inc(v.drain(), "v")
        _fence(ctx)

        with ExitStack() as stq:
            t = stq.enter_context(nc.sbuf_tensor(_u("f0_t"), [128, 2048], F32))
            sg = stq.enter_context(nc.sbuf_tensor(_u("f0_sg"), [128, 2048], F32))
            aa = stq.enter_context(nc.sbuf_tensor(_u("f0_a"), [128, 2048], F32))
            ib = stq.enter_context(nc.sbuf_tensor(_u("f0_ib"), [128, 2048], I32))
            sp = stq.enter_context(nc.sbuf_tensor(_u("f0_sp"), [128, 2048], F32))
            uu = stq.enter_context(nc.sbuf_tensor(_u("f0_u"), [128, 2048], F32))
            vv = stq.enter_context(nc.sbuf_tensor(_u("f0_v"), [128, 2048], F32))
            bb = stq.enter_context(nc.sbuf_tensor(_u("f0_b"), [128, 2048], F32))
            for c in range(NCH):
                @block.vector
                def _(v, c=c):
                    if c == 0:
                        ctx.wait_all(v)
                    else:
                        ctx.wait(v, "g")
                    v.tensor_scalar(out=t[:, :], in0=xT[:, c, :], scalar1=sc2[:, 0:1],
                                    scalar2=448.0, op0=ALU.mult, op1=ALU.min)
                    v.tensor_scalar_max(t[:, :], t[:, :], -448.0)
                    ctx.inc(v.drain(), "v")

                @block.scalar
                def _(a, c=c):
                    ctx.wait(a, "v")
                    ctx.wait(a, "g")
                    a.activation(out=sg[:, :], in_=t[:, :], func=AF.Sign, bias=0.0, scale=1.0)
                    i = a.activation(out=aa[:, :], in_=t[:, :], func=AF.Abs, bias=0.0, scale=1.0)
                    ctx.inc(i, "a")

                @block.vector
                def _(v, c=c):
                    ctx.wait(v, "a")
                    v.tensor_scalar(out=ib[:, :], in0=aa[:, :].bitcast(I32),
                                    scalar1=consts["expmask"][:, 0:1], scalar2=None,
                                    op0=ALU.bitwise_and)
                    v.tensor_scalar(out=sp[:, :], in0=ib[:, :].bitcast(F32),
                                    scalar1=0.125, scalar2=2.0 ** -9,
                                    op0=ALU.mult, op1=ALU.max)
                    v.reciprocal(out=uu[:, :], in_=sp[:, :])
                    v.tensor_tensor(out=uu[:, :], in0=aa[:, :], in1=uu[:, :], op=ALU.mult)
                    v.tensor_scalar_min(uu[:, :], uu[:, :], 16.0)
                    v.tensor_scalar(out=vv[:, :], in0=uu[:, :], scalar1=0.5,
                                    scalar2=None, op0=ALU.add)
                    for cbit in (8.0, 8.0, 4.0, 2.0, 1.0):
                        v.tensor_scalar(out=bb[:, :], in0=vv[:, :], scalar1=cbit,
                                        scalar2=None, op0=ALU.is_ge)
                        v.scalar_tensor_tensor(out=vv[:, :], in0=bb[:, :], scalar=-cbit,
                                               in1=vv[:, :], op0=ALU.mult, op1=ALU.add)
                    v.tensor_tensor(out=uu[:, :], in0=uu[:, :], in1=vv[:, :], op=ALU.subtract)
                    v.tensor_scalar(out=uu[:, :], in0=uu[:, :], scalar1=0.5,
                                    scalar2=None, op0=ALU.add)
                    v.tensor_tensor(out=uu[:, :], in0=uu[:, :], in1=sp[:, :], op=ALU.mult)
                    ctx.inc(v.drain(), "v")

                @block.gpsimd
                def _(g, c=c):
                    ctx.wait(g, "v")
                    g.tensor_tensor(out=q_bf[:, c, :], in0=uu[:, :], in1=sg[:, :], op=ALU.mult)
                    ctx.inc(g.drain(), "g")
        _fence(ctx)

        with ExitStack() as st2:
            saved, ctx.stack = ctx.stack, st2
            fw = load_weight_bf16(ctx, wd["l0fw"], 1024, 0, 1024, "l0fw")
            with nc.sbuf_tensor(_u("f0_ev"), [128, 512], F32) as ev:
                def evf(v, psap, oc, tg):
                    v.tensor_scalar(out=ev[:, :], in0=psap, scalar1=sc2[:, 1:2],
                                    scalar2=bias[:, oc:oc + 1], op0=ALU.mult, op1=ALU.add)
                    v.tensor_tensor(out=xT[:, oc, tg * 512:(tg + 1) * 512],
                                    in0=ev[:, :], in1=xT[:, oc, tg * 512:(tg + 1) * 512],
                                    op=ALU.add)
                emit_proj_featmajor(ctx, fw, 1024, 1024, q_bf, evf, "f0ps")
            ctx.stack = saved
    emit_layernorm_inplace(ctx, consts, xT)


def emit_l1_attn(ctx, consts, wd, xT, xbf, scr_dram):
    nc, block = ctx.nc, ctx.block
    with ExitStack() as st:
        qt_bf = st.enter_context(nc.sbuf_tensor(_u("l1_qt"), [128, 8, 2048], BF16))
        kt_bf = st.enter_context(nc.sbuf_tensor(_u("l1_kt"), [128, 8, 2048], BF16))
        v_bf = st.enter_context(nc.sbuf_tensor(_u("l1_v"), [128, 16, 1024], BF16))
        bq = st.enter_context(nc.sbuf_tensor(_u("l1_bq"), [128, 8], F32))
        bk = st.enter_context(nc.sbuf_tensor(_u("l1_bk"), [128, 8], F32))
        bo = st.enter_context(nc.sbuf_tensor(_u("l1_bo"), [128, 8], F32))
        bv_scope = ExitStack()
        bv_bc = bv_scope.enter_context(nc.sbuf_tensor(_u("l1_bvbc"), [128, 1024], BF16))
        bvr_scope = ExitStack()
        bvr = bvr_scope.enter_context(nc.sbuf_tensor(_u("l1_bvr"), [1, 1024], F32))

        @block.gpsimd
        def _(g):
            ctx.wait_all(g)
            ctx.dma(g, bq[:, :], wd["l1bq"][:, :])
            ctx.dma(g, bk[:, :], wd["l1bk"][:, :])
            ctx.dma(g, bo[:, :], wd["l1bo"][:, :])
            ctx.dma(g, bvr[0:1, :], wd["l1bv_row"].rearrange("(o a) b -> o (a b)", o=1)[0:1, :])

        with nc.psum_tensor(_u("l1psbv"), [128, 2, 512], F32) as psbv:
            @block.tensor
            def _(te):
                ctx.wait(te, "d")
                for half in range(2):
                    i = te.matmul(psbv[:, half, :], consts["ones_row_f"][0:1, :],
                                  bvr[0:1, half * 512:(half + 1) * 512],
                                  start=True, stop=True)
                ctx.inc(i, "p")

            @block.vector
            def _(v):
                ctx.wait(v, "p")
                v.tensor_copy(out=bv_bc[:, 0:512], in_=psbv[:, 0, :])
                v.tensor_copy(out=bv_bc[:, 512:1024], in_=psbv[:, 1, :])
                ctx.inc(v.drain(), "v")
        _fence(ctx)
        bvr_scope.close()

        def evq(v, psap, oc, tg):
            v.tensor_scalar(out=qt_bf[:, oc, tg * 512:(tg + 1) * 512], in0=psap,
                            scalar1=bq[:, oc:oc + 1], scalar2=None, op0=ALU.add)
        for half in range(2):
            with ExitStack() as st2:
                saved, ctx.stack = ctx.stack, st2
                wq = load_weight_bf16(ctx, wd["l1wq"], 1024, half * 512,
                                      (half + 1) * 512, "l1wq")
                emit_proj_featmajor(ctx, wq, 1024, 512, xbf, evq, "l1psq", oc0=half * 4)
                ctx.stack = saved

        def evk(v, psap, oc, tg):
            v.tensor_scalar(out=kt_bf[:, oc, tg * 512:(tg + 1) * 512], in0=psap,
                            scalar1=bk[:, oc:oc + 1], scalar2=0.125,
                            op0=ALU.add, op1=ALU.mult)
        for half in range(2):
            with ExitStack() as st2:
                saved, ctx.stack = ctx.stack, st2
                wk = load_weight_bf16(ctx, wd["l1wk"], 1024, half * 512,
                                      (half + 1) * 512, "l1wk")
                emit_proj_featmajor(ctx, wk, 1024, 512, xbf, evk, "l1psk", oc0=half * 4)
                ctx.stack = saved

        for vhalf in range(2):
          with ExitStack() as st2:
            saved, ctx.stack = ctx.stack, st2
            wv = load_weight_bf16(ctx, wd["l1wv"], 1024, vhalf * 512,
                                  (vhalf + 1) * 512, "l1wv")
            with nc.psum_tensor(_u("l1psv"), [128, 2, 512], F32) as psv:
                for tc in range(NKC):
                    for vh in [vhalf]:
                        par = tc % 2

                        @block.tensor
                        def _(te, tc=tc, vh=vh, par=par):
                            if tc == 0:
                                ctx.wait_all(te)
                            else:
                                ctx.wait(te, "v")
                            for ic in range(NCH):
                                i = te.matmul(psv[:, par, :],
                                              xbf[:, ic, tc * 128:(tc + 1) * 128],
                                              wv[:, ic, :],
                                              start=(ic == 0), stop=(ic == NCH - 1))
                            ctx.inc(i, "p")

                        @block.vector
                        def _(v, tc=tc, vh=vh, par=par):
                            ctx.wait(v, "p")
                            v.tensor_tensor(out=v_bf[:, tc, vh * 512:(vh + 1) * 512],
                                            in0=psv[:, par, :],
                                            in1=bv_bc[:, vh * 512:(vh + 1) * 512], op=ALU.add)
                            ctx.inc(v.drain(), "v")
            _fence(ctx)
            ctx.stack = saved

        bv_scope.close()
        emit_attention(ctx, consts, 1, qt_bf, kt_bf, v_bf, None, xbf)

        for half in range(2):
            with ExitStack() as st2:
                saved, ctx.stack = ctx.stack, st2
                wo = load_weight_bf16(ctx, wd["l1wo"], 1024, half * 512,
                                      (half + 1) * 512, "l1wo")
                with nc.sbuf_tensor(_u("l1_ev"), [128, 512], F32) as ev:
                    def evo(v, psap, oc, tg):
                        v.tensor_scalar(out=ev[:, :], in0=psap, scalar1=bo[:, oc:oc + 1],
                                        scalar2=None, op0=ALU.add)
                        v.tensor_tensor(out=xT[:, oc, tg * 512:(tg + 1) * 512],
                                        in0=ev[:, :],
                                        in1=xT[:, oc, tg * 512:(tg + 1) * 512],
                                        op=ALU.add)
                    emit_proj_featmajor(ctx, wo, 1024, 512, xbf, evo, "l1pso",
                                        oc0=half * 4)
                ctx.stack = saved
    emit_layernorm_inplace(ctx, consts, xT)


def emit_l1_ffn(ctx, consts, wd, xT, xbf, scr2):
    nc, block = ctx.nc, ctx.block
    with ExitStack() as st:
        b1 = st.enter_context(nc.sbuf_tensor(_u("f1_b1"), [128, 32], F32))
        b2 = st.enter_context(nc.sbuf_tensor(_u("f1_b2"), [128, 8], F32))
        scl = st.enter_context(nc.sbuf_tensor(_u("f1_scl"), [1, 1], F32))
        se = st.enter_context(nc.sbuf_tensor(_u("f1_se"), [1, 8, 1024], BF16))
        se2q = st.enter_context(nc.sbuf_tensor(_u("f1_se2q"), [1, 16, 256], BF16))
        p_sb = st.enter_context(nc.sbuf_tensor(_u("f1_p"), [1, 4, 2048], BF16))
        r_sb = st.enter_context(nc.sbuf_tensor(_u("f1_r"), [1, 16, 512], BF16))
        hT = st.enter_context(nc.sbuf_tensor(_u("f1_h"), [128, 32, 512], BF16))

        @block.gpsimd
        def _(g):
            ctx.wait_all(g)
            ctx.dma(g, b1[:, :], wd["l1b1"][:, :])
            ctx.dma(g, b2[:, :], wd["l1b2"][:, :])
            ctx.dma(g, scl[0:1, 0:1], wd["l1sc"][0:1, 0:1])

        do_prep = "noprep" not in DBG
        lt_scope = ExitStack()
        lt = lt_scope.enter_context(nc.sbuf_tensor(_u("f1_lt"), [1, 8, 1024], BF16))
        with nc.sbuf_tensor(_u("f1_ptt"), [128, 128], BF16) as ptt:
          if do_prep:
              for c in range(NCH):
                  @block.scalar
                  def _(a, c=c):
                      if c == 0:
                          ctx.wait_all(a)
                      else:
                          ctx.wait(a, "d")
                      i = a.dma_start(out=ptt[:, :], in_=xbf[:, c, 0:128], transpose=True)
                      ctx.inc(i, "t", 16)

                  @block.gpsimd
                  def _(g, c=c):
                      ctx.wait(g, "t")
                      ctx.dma(g, scr2[:, c * 128:(c + 1) * 128], ptt[1:17, :])

              @block.gpsimd
              def _(g):
                  ctx.wait(g, "d")
                  i = g.dma_start(out=lt[0:1, :, :],
                                  in_=scr2.rearrange("(o r) p -> o r p", o=1)[0:1, 0:16:2, :])
                  ctx.inc(i, "d", 16)
          _fence(ctx)

        @block.vector
        def _(v):
            if not do_prep:
                ctx.inc(v.drain(), "v")
                return
            ctx.wait(v, "v")
            for i8 in range(8):
                v.tensor_scalar_mul(se[0:1, i8, :], lt[0:1, i8, :], scl[0:1, 0:1])
            ctx.inc(v.drain(), "v")
            ctx.wait(v, "v")
            for i4 in range(4):
                for q4 in range(4):
                    for half in range(2):
                        v.tensor_copy(out=se2q[0:1, i4 * 4 + q4, half * 128:(half + 1) * 128],
                                      in_=se[0:1, 4 + i4, half * 512 + q4:(half + 1) * 512:4])
            ctx.inc(v.drain(), "v")
        _fence(ctx)
        lt_scope.close()

        with nc.psum_tensor(_u("f1psp"), [1, 4, 512], F32) as psp:
            for tg in (range(NTG) if do_prep else []):
                @block.tensor
                def _(te, tg=tg):
                    if tg == 0:
                        ctx.wait_all(te)
                    else:
                        ctx.wait(te, "v")
                    for i4 in range(4):
                        for ic in range(NCH):
                            i = te.matmul(psp[:, i4, :],
                                          xbf[:, ic, 2 + 2 * i4:3 + 2 * i4],
                                          xbf[:, ic, tg * 512:(tg + 1) * 512],
                                          start=(ic == 0), stop=(ic == NCH - 1))
                    ctx.inc(i, "p")

                @block.vector
                def _(v, tg=tg):
                    ctx.wait(v, "p")
                    v.tensor_copy(out=p_sb[0:1, :, tg * 512:(tg + 1) * 512], in_=psp[0:1, :, :])
                    ctx.inc(v.drain(), "v")
        _fence(ctx)

        for tg in range(NTG):
            for hs in range(4):
                with ExitStack() as st2:
                    saved, ctx.stack = ctx.stack, st2
                    w1 = load_weight_bf16(ctx, wd["l1w1"], 1024, hs * 1024, (hs + 1) * 1024,
                                          "f1w1")
                    with nc.psum_tensor(_u("f1ps1"), [128, 2, 512], F32) as ps1:
                        for hb8 in range(8):
                            hb = hs * 8 + hb8
                            par = hb8 % 2

                            @block.tensor
                            def _(te, tg=tg, hs=hs, hb8=hb8, hb=hb, par=par):
                                if hb8 == 0:
                                    ctx.wait_all(te)
                                if hb8 >= 2:
                                    ctx.wait(te, "a")
                                for ic in range(NCH):
                                    i = te.matmul(ps1[:, par, :],
                                              w1[:, ic, hb8 * 128:(hb8 + 1) * 128],
                                              xbf[:, ic, tg * 512:(tg + 1) * 512],
                                              start=(ic == 0),
                                              stop=(ic == NCH - 1 and "nolora" in DBG))
                                if "nolora" not in DBG:
                                    i4 = hb // 8
                                    jsl = (hb % 8) * 128
                                    i = te.matmul(ps1[:, par, :], se[0:1, i4, jsl:jsl + 128],
                                                  p_sb[0:1, i4, tg * 512:(tg + 1) * 512],
                                                  start=False, stop=True)
                                ctx.inc(i, "p")

                            @block.scalar
                            def _(a, hb=hb, par=par):
                                ctx.wait(a, "p")
                                i = a.activation(out=hT[:, hb, :], in_=ps1[:, par, :],
                                                 func=AF.Gelu, bias=b1[:, hb:hb + 1],
                                                 scale=1.0)
                                ctx.inc(i, "a")
                    _fence(ctx)
                    ctx.stack = saved

            if "dumph" in DBG and tg == 0:
                with nc.sbuf_tensor(_u("dmp"), [128, 512], F32) as dmp:
                    for hc in range(32):
                        @block.gpsimd
                        def _(g, hc=hc):
                            ctx.wait_all(g)
                            i = g.tensor_copy(out=dmp[:, :], in_=hT[:, hc, :])
                            ctx.inc(i, "g")
                            ctx.wait(g, "g")
                            ctx.dma(g, DBG_DUMP[0][(hc % 8) * 128:(hc % 8 + 1) * 128,
                                                   (hc // 8) * 512:(hc // 8 + 1) * 512],
                                    dmp[:, :], name="o")
                            ctx.wait(g, "o")
                _fence(ctx)

            for i4h in (range(2) if do_prep else []):
                with nc.psum_tensor(_u("f1psr"), [1, 8, 512], F32) as psr:
                    @block.tensor
                    def _(te, tg=tg, i4h=i4h):
                        ctx.wait_all(te)
                        for i4x in range(2):
                            i4 = 2 * i4h + i4x
                            for q4 in range(4):
                                for kc in range(8):
                                    i = te.matmul(psr[:, i4x * 4 + q4, :],
                                                  xbf[:, kc, 10 + 2 * i4:11 + 2 * i4],
                                                  hT[:, q4 * 8 + kc, :],
                                                  start=(kc == 0), stop=(kc == 7))
                        ctx.inc(i, "p")

                    @block.vector
                    def _(v, tg=tg, i4h=i4h):
                        ctx.wait(v, "p")
                        v.tensor_copy(out=r_sb[0:1, i4h * 8:(i4h + 1) * 8, :],
                                      in_=psr[0:1, :, :])
                        ctx.inc(v.drain(), "v")
                _fence(ctx)

            with nc.sbuf_tensor(_u("f1_ev2"), [128, 512], F32) as ev2, \
                 nc.psum_tensor(_u("f1ps2"), [128, 2, 512], F32) as ps2:
                for oc in range(NCH):
                    par = oc % 2
                    i4 = oc // 2
                    with ExitStack() as st2:
                        saved, ctx.stack = ctx.stack, st2
                        w2 = load_weight_bf16(ctx, wd["l1w2"], 4096, oc * 128,
                                              (oc + 1) * 128, "f1w2")

                        @block.tensor
                        def _(te, tg=tg, oc=oc, par=par, i4=i4, w2=w2):
                            ctx.wait_all(te)
                            for hc in range(32):
                                i = te.matmul(ps2[:, par, :], w2[:, hc, :],
                                          hT[:, hc, :], start=(hc == 0),
                                          stop=(hc == 31 and "nolora" in DBG))
                            if "nolora" not in DBG:
                                for q4 in range(4):
                                    i = te.matmul(ps2[:, par, :],
                                                  se2q[0:1, i4 * 4 + q4,
                                                       (oc % 2) * 128:(oc % 2) * 128 + 128],
                                                  r_sb[0:1, i4 * 4 + q4, :],
                                                  start=False, stop=(q4 == 3))
                            ctx.inc(i, "p")

                        @block.vector
                        def _(v, tg=tg, oc=oc, par=par):
                            ctx.wait(v, "p")
                            v.tensor_scalar(out=ev2[:, :], in0=ps2[:, par, :],
                                            scalar1=b2[:, oc:oc + 1], scalar2=None,
                                            op0=ALU.add)
                            v.tensor_tensor(out=xT[:, oc, tg * 512:(tg + 1) * 512],
                                            in0=ev2[:, :],
                                            in1=xT[:, oc, tg * 512:(tg + 1) * 512],
                                            op=ALU.add)
                            ctx.inc(v.drain(), "v")
                        _fence(ctx)
                        ctx.stack = saved
    if "noln" not in DBG:
        emit_layernorm_inplace(ctx, consts, xT)


def build_program(stages):
    nc = bass.Bass("TRN2", target_bir_lowering=False, detect_race_conditions=False)
    cts = _make_consts(nc)
    wd = {}

    def din(nm, shp):
        wd[nm] = nc.dram_tensor(nm, shp, F32, kind="ExternalInput").ap()

    if "l0a" in stages:
        din("l0wq", [1024, 1024]); din("l0wk", [1024, 256])
        din("l0wv", [1024, 256]); din("l0wo", [1024, 1024])
    if "l0f" in stages:
        din("l0fw", [1024, 1024]); din("l0fb", [128, 8]); din("l0sc", [1, 2])
    if "l1a" in stages:
        for nm in ("l1wq", "l1wk", "l1wv", "l1wo"):
            din(nm, [1024, 1024])
        din("l1bq", [128, 8]); din("l1bk", [128, 8])
        din("l1bv_row", [8, 128]); din("l1bo", [128, 8])
    if "l1f" in stages:
        din("l1w1", [1024, 4096]); din("l1b1", [128, 32])
        din("l1w2", [4096, 1024]); din("l1b2", [128, 8]); din("l1sc", [1, 1])
    tabs = {}
    if "l0a" in stages:
        for nm in ("coshi", "coslo", "sinhi", "sinlo"):
            tabs[nm] = nc.dram_tensor(nm, [512, 2048], F32, kind="ExternalInput").ap()
    xin, xout, xro = [], [], []
    for j in range(BPC):
        xin.append(nc.dram_tensor(f"xin{j}", [1024, 2048], F32, kind="ExternalInput").ap())
        xout.append(nc.dram_tensor(f"xout{j}", [1024, 2048], F32, kind="ExternalOutput").ap())
        if "l0a" in stages:
            xro.append(nc.dram_tensor(f"xro{j}", [1024, 2048], F32, kind="ExternalInput").ap())
    scr = nc.dram_tensor("scr", [256, 8], F32, kind="Internal").ap()
    if "dump_att" in DBG:
        DBG_DRAM[0] = nc.dram_tensor("dbgatt", [128, 4200], F32, kind="ExternalOutput").ap()
    if "dump_ctx" in DBG:
        DBG_DRAM[0] = nc.dram_tensor("dbgatt", [1024, 2048], F32, kind="ExternalOutput").ap()
    if "dump_negm" in DBG:
        DBG_DRAM2[0] = nc.dram_tensor("dbgnegm", [128, 2128], F32, kind="ExternalOutput").ap()
    if "dumph" in DBG:
        DBG_DUMP[0] = nc.dram_tensor("dbgout", [1024, 2048], F32, kind="ExternalOutput").ap()
    scr2 = nc.dram_tensor("scr2", [16, 1024], BF16, kind="Internal").ap()

    need_bf = ("l1a" in stages) or ("l1f" in stages)
    with ExitStack() as stack:
        blk = stack.enter_context(nc.Block())
        ctx = Ctx(nc, blk, stack)
        consts = _load_consts(ctx, cts)
        _fence(ctx)
        xT = stack.enter_context(nc.sbuf_tensor(_u("xT"), [128, 8, 2048], F32))
        xbf = None
        if need_bf:
            xbf = stack.enter_context(nc.sbuf_tensor(_u("xbf"), [128, 8, 2048], BF16))
        for j in range(BPC):
            emit_load_stream(ctx, xin[j], xT, None)
            if "l0a" in stages:
                wd2 = dict(wd)
                wd2["xro"] = xro[j]
                emit_l0_attn(ctx, consts, wd2, tabs, xT, scr)
            if "l0f" in stages:
                emit_l0_ffn(ctx, consts, wd, xT)
            if "l1a" in stages:
                emit_refresh_xbf(ctx, xT, xbf)
                emit_l1_attn(ctx, consts, wd, xT, xbf, scr)
            if "l1f" in stages:
                emit_refresh_xbf(ctx, xT, xbf)
                emit_l1_ffn(ctx, consts, wd, xT, xbf, scr2)
            emit_store_stream(ctx, xT, xout[j])
    return nc


def _host_prep(inputs):
    d = {}
    d["l0wq"] = inputs["L0_attn_W_q"].T
    d["l0wk"] = inputs["L0_attn_W_k"].T
    d["l0wv"] = inputs["L0_attn_W_v"].T
    d["l0wo"] = inputs["L0_attn_W_o"].T
    d["l0fw"] = inputs["L0_ffn_weight_fp8"].T
    d["l0fb"] = inputs["L0_ffn_bias"].reshape(8, 128).T
    d["l0sc"] = np.stack([np.asarray(inputs["L0_ffn_input_scale"]).reshape(()),
                          np.asarray(inputs["L0_ffn_weight_scale"]).reshape(())]).reshape(1, 2)
    d["l1wq"] = inputs["L1_attn_Wq"].T
    d["l1wk"] = inputs["L1_attn_Wk"].T
    d["l1wv"] = inputs["L1_attn_Wv"].T
    d["l1wo"] = inputs["L1_attn_Wo"].T
    d["l1bq"] = inputs["L1_attn_bq"].reshape(8, 128).T
    d["l1bk"] = inputs["L1_attn_bk"].reshape(8, 128).T
    d["l1bv_row"] = inputs["L1_attn_bv"].reshape(8, 128)
    d["l1bo"] = inputs["L1_attn_bo"].reshape(8, 128).T
    d["l1w1"] = inputs["L1_ffn_W1"].T
    d["l1b1"] = inputs["L1_ffn_b1"].reshape(32, 128).T
    d["l1w2"] = inputs["L1_ffn_W2"].T
    d["l1b2"] = inputs["L1_ffn_b2"].reshape(8, 128).T
    d["l1sc"] = np.asarray(inputs["L1_ffn_scale"], np.float32).reshape(1, 1)
    f = np.arange(512, dtype=np.float64) / 512.0
    theta = 1.0 / (10000.0 ** f)
    ang = theta[:, None] * np.arange(S, dtype=np.float64)[None, :]
    for nm, fn in (("cos", np.cos), ("sin", np.sin)):
        v = fn(ang)
        hi = v.astype(np.float32)
        lo = (v - hi.astype(np.float64)).astype(np.float32)
        d[nm + "hi"] = hi
        d[nm + "lo"] = lo
    return {k: np.ascontiguousarray(v, dtype=np.float32) for k, v in d.items()}


def _get_program(stages):
    key = tuple(stages)
    if key not in _PROGRAM_CACHE:
        _PROGRAM_CACHE[key] = build_program(stages)
    return _PROGRAM_CACHE[key]


def _prog_input_names(nc):
    names = set()
    for a in nc.m.functions[0].allocations:
        if isinstance(a, mybir.MemoryLocationSet) and a.kind == "ExternalInput":
            for ml in a.memorylocations:
                names.add(ml.name)
    return names


def run_stage_group(stages, shared, streams, xros):
    nc = _get_program(stages)
    names = _prog_input_names(nc)
    in_maps = []
    for c in range(NC_N):
        m = {k: v for k, v in shared.items() if k in names}
        for j in range(BPC):
            m[f"xin{j}"] = streams[c][j]
            if f"xro{j}" in names:
                m[f"xro{j}"] = xros[c][j]
        in_maps.append(m)
    res = run_bass_kernel_spmd(nc, in_maps, core_ids=list(range(NC_N)))
    return [[np.asarray(res.results[c][f"xout{j}"]) for j in range(BPC)] for c in range(NC_N)]


STAGE_GROUPS = [("l0a",), ("l0f",), ("l1a",), ("l1f",)]


def kernel(**inputs):
    shared = _host_prep(inputs)
    x = np.asarray(inputs["L0_attn_X"], np.float32)
    streams, xros = [], []
    for c in range(NC_N):
        ss, rr = [], []
        for j in range(BPC):
            xb = x[c * BPC + j]
            ss.append(np.ascontiguousarray(xb.T))
            rr.append(np.ascontiguousarray(
                np.concatenate([xb[:, 0::2], xb[:, 1::2]], axis=1).T))
        streams.append(ss)
        xros.append(rr)
    for stages in STAGE_GROUPS:
        streams = run_stage_group(stages, shared, streams, xros)
    out = np.empty((B, S, D), np.float32)
    for c in range(NC_N):
        for j in range(BPC):
            out[c * BPC + j] = streams[c][j].T
    return out


# version 04
# speedup vs baseline: 1.0070x; 1.0070x over previous
import numpy as np
import ml_dtypes
import concourse.bass as bass
import concourse.mybir as mybir
from concourse.bass_utils import run_bass_kernel_spmd
from contextlib import ExitStack

F32 = mybir.dt.float32
BF16 = mybir.dt.bfloat16
I32 = mybir.dt.int32
AF = mybir.ActivationFunctionType
ALU = mybir.AluOpType
AXL = mybir.AxisListType

B, S, D = 16, 2048, 1024
NC_N = 8
BPC = B // NC_N
NCH = D // 128
NKC = S // 128
NTG = S // 512
NBIG = -3.0e38

_PROGRAM_CACHE = {}
DBG = set()
DBG_DUMP = [None]
DBG_ATT = [None]
DBG_DRAM = [None]
DBG_ATT2 = [None]
DBG_DRAM2 = [None]

_NAME_SEQ = [0]


def _u(name):
    _NAME_SEQ[0] += 1
    return f"{name}_{_NAME_SEQ[0]}"




class Ctx:
    def __init__(self, nc, block, stack):
        self.nc = nc
        self.block = block
        self.stack = stack
        self.sems = {}
        self.cnt = {}
        for name in ("d", "p", "v", "a", "g", "t", "o", "s"):
            self.sems[name] = stack.enter_context(nc.semaphore("sem_" + name))
            self.cnt[name] = 0

    def inc(self, inst, name, n=1):
        inst.then_inc(self.sems[name], n)
        self.cnt[name] += n
        return inst

    def dma(self, eng, out, in_, name="d"):
        return self.inc(eng.dma_start(out=out, in_=in_), name, 16)

    def wait(self, eng, name):
        eng.wait_ge(self.sems[name], self.cnt[name])

    def wait_all(self, eng):
        for name in ("d", "p", "v", "a", "g", "t", "o", "s"):
            if self.cnt[name]:
                eng.wait_ge(self.sems[name], self.cnt[name])


def _fence(ctx):
    block = ctx.block

    @block.tensor
    def _(e):
        ctx.inc(e.drain(), "p")

    @block.vector
    def _(e):
        ctx.inc(e.drain(), "v")

    @block.scalar
    def _(e):
        ctx.inc(e.drain(), "a")

    @block.gpsimd
    def _(e):
        ctx.inc(e.drain(), "g")

    @block.sync
    def _(e):
        ctx.inc(e.drain(), "s")

    for emit in (block.tensor, block.vector, block.scalar, block.gpsimd, block.sync):
        @emit
        def _(e):
            ctx.wait_all(e)


def _make_consts(nc):
    c = {}
    c["ident"] = nc.inline_tensor(np.eye(128, dtype=ml_dtypes.bfloat16), "c_ident")
    c["ones_row_f"] = nc.inline_tensor(np.ones((1, 128), np.float32), "c_onesrf")
    c["ones_col_f"] = nc.inline_tensor(np.ones((128, 1), np.float32), "c_onescf")
    mask = np.triu(np.ones((128, 128), np.float32), k=1) * (-1e9)
    c["causal"] = nc.inline_tensor(mask, "c_causal")
    c["expmask"] = nc.inline_tensor(np.full((128, 1), 0x7F800000, np.int32), "c_expmask")
    return c


def _load_consts(ctx, cts):
    nc, block = ctx.nc, ctx.block
    sb = {}
    sb["ident"] = ctx.stack.enter_context(nc.sbuf_tensor(_u("cs_ident"), [128, 128], BF16))
    sb["ones_row_f"] = ctx.stack.enter_context(nc.sbuf_tensor(_u("cs_onesrf"), [1, 128], F32))
    sb["ones_col_f"] = ctx.stack.enter_context(nc.sbuf_tensor(_u("cs_onescf"), [128, 1], F32))
    sb["causal"] = ctx.stack.enter_context(nc.sbuf_tensor(_u("cs_causal"), [128, 128], F32))
    sb["expmask"] = ctx.stack.enter_context(nc.sbuf_tensor(_u("cs_expmask"), [128, 1], I32))
    sb["eps"] = ctx.stack.enter_context(nc.sbuf_tensor(_u("cs_eps"), [128, 1], F32))

    @block.gpsimd
    def _(g):
        for k in ("ident", "ones_row_f", "ones_col_f", "causal", "expmask"):
            ctx.dma(g, sb[k][:, :], cts[k].ap()[:, :])

    @block.vector
    def _(v):
        v.memset(sb["eps"][:, :], 1e-5)
        ctx.inc(v.drain(), "v")
    return sb


def load_weight_bf16(ctx, w_dram, nin, c0, c1, name):
    nc, block = ctx.nc, ctx.block
    nchk = nin // 128
    ncol = c1 - c0
    wbf = ctx.stack.enter_context(nc.sbuf_tensor(_u(name + "_bf"), [128, nchk, ncol], BF16))
    sub = min(ncol, 512)
    with nc.sbuf_tensor(_u(name + "_f32"), [128, sub], F32) as wf:
        @block.gpsimd
        def _(g):
            ctx.wait_all(g)
            for ic in range(nchk):
                for s0 in range(0, ncol, sub):
                    ctx.dma(g, wf[:, :], w_dram[ic * 128:(ic + 1) * 128,
                                                c0 + s0:c0 + s0 + sub])
                    ctx.wait(g, "d")
                    i = g.tensor_copy(out=wbf[:, ic, s0:s0 + sub], in_=wf[:, :])
                    ctx.inc(i, "g")
                    ctx.wait(g, "g")
            ctx.inc(g.drain(), "g")
        _fence(ctx)
    return wbf


def emit_proj_featmajor(ctx, wbf, nin, nout, src_bf, evac, psname, oc0=0):
    nc, block = ctx.nc, ctx.block
    nic = nin // 128
    noc = nout // 128
    with nc.psum_tensor(_u(psname), [128, 2, 512], F32) as ps:
        for oc_ in range(noc):
            for tg in range(NTG):
                par = (oc_ * NTG + tg) % 2

                @block.tensor
                def _(te, oc=oc_, tg=tg, par=par):
                    if oc == 0 and tg == 0:
                        ctx.wait_all(te)
                    if oc * NTG + tg >= 2:
                        ctx.wait(te, "v")
                    for ic in range(nic):
                        i = te.matmul(ps[:, par, :], wbf[:, ic, oc * 128:(oc + 1) * 128],
                                      src_bf[:, ic, tg * 512:(tg + 1) * 512],
                                      start=(ic == 0), stop=(ic == nic - 1))
                    ctx.inc(i, "p")

                @block.vector
                def _(v, oc=oc_, tg=tg, par=par):
                    ctx.wait(v, "p")
                    evac(v, ps[:, par, :], oc0 + oc, tg)
                    ctx.inc(v.drain(), "v")
    _fence(ctx)


def emit_load_stream(ctx, x_dram, xT, xbf=None):
    nc, block = ctx.nc, ctx.block

    @block.gpsimd
    def _(g):
        ctx.wait_all(g)
        for c in range(NCH):
            ctx.dma(g, xT[:, c, :], x_dram[c * 128:(c + 1) * 128, :])
        if xbf is not None:
            ctx.wait(g, "d")
            for c in range(NCH):
                g.tensor_copy(out=xbf[:, c, :], in_=xT[:, c, :])
            ctx.inc(g.drain(), "g")
    _fence(ctx)


def emit_refresh_xbf(ctx, xT, xbf):
    @ctx.block.gpsimd
    def _(g):
        ctx.wait_all(g)
        for c in range(NCH):
            g.tensor_copy(out=xbf[:, c, :], in_=xT[:, c, :])
        ctx.inc(g.drain(), "g")
    _fence(ctx)


def emit_store_stream(ctx, xT, y_dram):
    nc, block = ctx.nc, ctx.block

    @block.gpsimd
    def _(g):
        ctx.wait_all(g)
        for c in range(NCH):
            ctx.dma(g, y_dram[c * 128:(c + 1) * 128, :], xT[:, c, :], name="o")
        ctx.wait(g, "o")
    _fence(ctx)


def emit_rope(ctx, xro_dram, tabs, xrbf):
    nc, block = ctx.nc, ctx.block
    with ExitStack() as st:
        xe = st.enter_context(nc.sbuf_tensor(_u("rp_xe"), [128, 2048], F32))
        xo = st.enter_context(nc.sbuf_tensor(_u("rp_xo"), [128, 2048], F32))
        tt = {k: st.enter_context(nc.sbuf_tensor(_u("rp_" + k), [128, 2048], F32))
              for k in ("ch", "cl", "sh", "sl")}
        m1 = st.enter_context(nc.sbuf_tensor(_u("rp_m1"), [128, 2048], F32))
        m2 = st.enter_context(nc.sbuf_tensor(_u("rp_m2"), [128, 2048], F32))
        p1 = st.enter_context(nc.sbuf_tensor(_u("rp_p1"), [128, 2048], F32))
        p2 = st.enter_context(nc.sbuf_tensor(_u("rp_p2"), [128, 2048], F32))
        for c in range(4):
            @block.gpsimd
            def _(g, c=c):
                if c == 0:
                    ctx.wait_all(g)
                else:
                    ctx.wait(g, "v")
                ctx.dma(g, xe[:, :], xro_dram[c * 128:(c + 1) * 128, :])
                ctx.dma(g, xo[:, :], xro_dram[512 + c * 128:512 + (c + 1) * 128, :])
                for k, nm in (("ch", "coshi"), ("cl", "coslo"), ("sh", "sinhi"), ("sl", "sinlo")):
                    ctx.dma(g, tt[k][:, :], tabs[nm][c * 128:(c + 1) * 128, :])

            @block.vector
            def _(v, c=c):
                ctx.wait(v, "d")
                v.tensor_tensor(out=m1[:, :], in0=xe[:, :], in1=tt["ch"][:, :], op=ALU.mult)
                v.tensor_tensor(out=m2[:, :], in0=xo[:, :], in1=tt["sh"][:, :], op=ALU.mult)
                v.tensor_tensor(out=p1[:, :], in0=m1[:, :], in1=m2[:, :], op=ALU.subtract)
                v.tensor_tensor(out=m1[:, :], in0=xe[:, :], in1=tt["cl"][:, :], op=ALU.mult)
                v.tensor_tensor(out=m2[:, :], in0=xo[:, :], in1=tt["sl"][:, :], op=ALU.mult)
                v.tensor_tensor(out=p2[:, :], in0=m1[:, :], in1=m2[:, :], op=ALU.subtract)
                v.tensor_tensor(out=p1[:, :], in0=p1[:, :], in1=p2[:, :], op=ALU.add)
                v.tensor_copy(out=xrbf[:, c, :], in_=p1[:, :])
                v.tensor_tensor(out=m1[:, :], in0=xe[:, :], in1=tt["sh"][:, :], op=ALU.mult)
                v.tensor_tensor(out=m2[:, :], in0=xo[:, :], in1=tt["ch"][:, :], op=ALU.mult)
                v.tensor_tensor(out=p1[:, :], in0=m1[:, :], in1=m2[:, :], op=ALU.add)
                v.tensor_tensor(out=m1[:, :], in0=xe[:, :], in1=tt["sl"][:, :], op=ALU.mult)
                v.tensor_tensor(out=m2[:, :], in0=xo[:, :], in1=tt["cl"][:, :], op=ALU.mult)
                v.tensor_tensor(out=p2[:, :], in0=m1[:, :], in1=m2[:, :], op=ALU.add)
                v.tensor_tensor(out=p1[:, :], in0=p1[:, :], in1=p2[:, :], op=ALU.add)
                v.tensor_copy(out=xrbf[:, 4 + c, :], in_=p1[:, :])
                ctx.inc(v.drain(), "v")
    _fence(ctx)


def emit_layernorm_inplace(ctx, consts, xT):
    nc, block = ctx.nc, ctx.block
    with ExitStack() as st:
        sq = st.enter_context(nc.sbuf_tensor(_u("ln_sq"), [128, 1024], F32))
        stats = st.enter_context(nc.sbuf_tensor(_u("ln_stats"), [1, 4, 1024], F32))
        bc = st.enter_context(nc.sbuf_tensor(_u("ln_bc"), [128, 2, 1024], F32))
        ps_mu = st.enter_context(nc.psum_tensor(_u("ln_psmu"), [1, 1024], F32))
        ps_sq = st.enter_context(nc.psum_tensor(_u("ln_pssq"), [1, 1024], F32))
        ps_bc = st.enter_context(nc.psum_tensor(_u("ln_psbc"), [128, 2, 1024], F32))
        for th in range(2):
            t0 = th * 1024

            for c in range(NCH):
                @block.scalar
                def _(a, th=th, t0=t0, c=c):
                    if th == 0 and c == 0:
                        ctx.wait_all(a)
                    elif c == 0:
                        ctx.wait(a, "v")
                    else:
                        ctx.wait(a, "p")
                    i = a.activation(out=sq[:, :], in_=xT[:, c, t0:t0 + 1024],
                                     func=AF.Square, bias=0.0, scale=1.0)
                    ctx.inc(i, "a")

                @block.tensor
                def _(te, th=th, t0=t0, c=c):
                    ctx.wait(te, "a")
                    for g2 in range(2):
                        te.matmul(ps_mu[:, g2 * 512:(g2 + 1) * 512], consts["ones_col_f"][:, :],
                                  xT[:, c, t0 + g2 * 512:t0 + (g2 + 1) * 512],
                                  start=(c == 0), stop=(c == NCH - 1))
                        i = te.matmul(ps_sq[:, g2 * 512:(g2 + 1) * 512], consts["ones_col_f"][:, :],
                                      sq[:, g2 * 512:(g2 + 1) * 512],
                                      start=(c == 0), stop=(c == NCH - 1))
                    ctx.inc(i, "p")

            @block.vector
            def _(v, th=th):
                ctx.wait(v, "p")
                v.tensor_scalar_mul(stats[0:1, 0, :], ps_mu[:, :], 1.0 / D)
                v.tensor_scalar_mul(stats[0:1, 1, :], ps_sq[:, :], 1.0 / D)
                v.tensor_tensor(out=stats[0:1, 3, :], in0=stats[0:1, 0, :],
                                in1=stats[0:1, 0, :], op=ALU.mult)
                v.tensor_tensor(out=stats[0:1, 1, :], in0=stats[0:1, 1, :],
                                in1=stats[0:1, 3, :], op=ALU.subtract)
                ctx.inc(v.drain(), "v")

            @block.scalar
            def _(a, th=th):
                ctx.wait(a, "v")
                i = a.activation(out=stats[0:1, 3, :], in_=stats[0:1, 1, :],
                                 func=AF.Sqrt, bias=consts["eps"][0:1, 0:1], scale=1.0)
                ctx.inc(i, "a")

            @block.vector
            def _(v, th=th):
                ctx.wait(v, "a")
                v.reciprocal(out=stats[0:1, 1, :], in_=stats[0:1, 3, :])
                v.tensor_tensor(out=stats[0:1, 2, :], in0=stats[0:1, 0, :],
                                in1=stats[0:1, 1, :], op=ALU.mult)
                v.tensor_scalar_mul(stats[0:1, 2, :], stats[0:1, 2, :], -1.0)
                ctx.inc(v.drain(), "v")

            @block.tensor
            def _(te, th=th):
                ctx.wait(te, "v")
                for g2 in range(2):
                    te.matmul(ps_bc[:, 0, g2 * 512:(g2 + 1) * 512], consts["ones_row_f"][0:1, :],
                              stats[0:1, 1, g2 * 512:(g2 + 1) * 512], start=True, stop=True)
                    i = te.matmul(ps_bc[:, 1, g2 * 512:(g2 + 1) * 512], consts["ones_row_f"][0:1, :],
                                  stats[0:1, 2, g2 * 512:(g2 + 1) * 512], start=True, stop=True)
                ctx.inc(i, "p")

            @block.vector
            def _(v, th=th, t0=t0):
                ctx.wait(v, "p")
                v.tensor_copy(out=bc[:, 0, :], in_=ps_bc[:, 0, :])
                v.tensor_copy(out=bc[:, 1, :], in_=ps_bc[:, 1, :])
                for c in range(NCH):
                    v.tensor_tensor(out=xT[:, c, t0:t0 + 1024], in0=xT[:, c, t0:t0 + 1024],
                                    in1=bc[:, 0, :], op=ALU.mult)
                    v.tensor_tensor(out=xT[:, c, t0:t0 + 1024], in0=xT[:, c, t0:t0 + 1024],
                                    in1=bc[:, 1, :], op=ALU.add)
                ctx.inc(v.drain(), "v")
    _fence(ctx)


def emit_negm(ctx, consts, qt_bf, kt_bf, layer, negm_bc, scr_dram, tag):
    nc, block = ctx.nc, ctx.block
    kch = 4 if layer == 0 else 8
    with ExitStack() as st:
        mq = st.enter_context(nc.sbuf_tensor(_u(tag + "_mq"), [128, 8], F32))
        mk = st.enter_context(nc.sbuf_tensor(_u(tag + "_mk"), [128, 8], F32))
        flat = st.enter_context(nc.sbuf_tensor(_u(tag + "_flat"), [1, 2048], F32))
        mrow = st.enter_context(nc.sbuf_tensor(_u(tag + "_mrow"), [1, 3, 16], F32))
        ps = st.enter_context(nc.psum_tensor(_u(tag + "_ps"), [128, 16], F32))
        if "dump_negm" in DBG and tag == "l0nm":
            DBG_ATT2[0] = st.enter_context(nc.sbuf_tensor(_u("nm_dbg"), [128, 2128], F32))

        @block.vector
        def _(v):
            ctx.wait_all(v)
            v.memset(mk[:, :], 0.0)
            for c in range(NCH):
                v.reduce_max(out=mq[:, c:c + 1], in_=qt_bf[:, c, :], axis=AXL.X,
                             apply_absolute_value=True)
            for c in range(kch):
                v.reduce_max(out=mk[:, c:c + 1], in_=kt_bf[:, c, :], axis=AXL.X,
                             apply_absolute_value=True)
            ctx.inc(v.drain(), "v")

        @block.gpsimd
        def _(g):
            ctx.wait(g, "v")
            ctx.dma(g, scr_dram[0:128, :], mq[:, :])
            ctx.dma(g, scr_dram[128:256, :], mk[:, :])

        @block.gpsimd
        def _(g):
            ctx.wait(g, "d")
            ctx.dma(g, flat[0:1, :], scr_dram.rearrange("(o p) c -> o (p c)", o=1)[0:1, :])

        @block.vector
        def _(v):
            ctx.wait(v, "d")
            for h in range(16):
                st_q = 64 * (h % 2) * 8 + (h // 2)
                v.reduce_max(out=mrow[0:1, 0, h:h + 1],
                             in_=flat[0:1, st_q:st_q + 505:8], axis=AXL.X)
            for h in range(16):
                if layer == 0:
                    gk = h // 4
                    st_k = 1024 + gk
                    v.reduce_max(out=mrow[0:1, 1, h:h + 1],
                                 in_=flat[0:1, st_k:st_k + 505:8], axis=AXL.X)
                else:
                    st_k = 1024 + 64 * (h % 2) * 8 + (h // 2)
                    v.reduce_max(out=mrow[0:1, 1, h:h + 1],
                                 in_=flat[0:1, st_k:st_k + 505:8], axis=AXL.X)
            ctx.inc(v.drain(), "v")
            ctx.wait(v, "v")
            v.scalar_tensor_tensor(out=mrow[0:1, 2, :], in0=mrow[0:1, 0, :],
                                   scalar=-64.0, in1=mrow[0:1, 1, :],
                                   op0=ALU.mult, op1=ALU.mult)
            ctx.inc(v.drain(), "v")

        @block.tensor
        def _(te):
            ctx.wait(te, "v")
            i = te.matmul(ps[:, :], consts["ones_row_f"][0:1, :], mrow[0:1, 2, :],
                          start=True, stop=True)
            ctx.inc(i, "p")

        @block.vector
        def _(v):
            ctx.wait(v, "p")
            v.tensor_copy(out=negm_bc[:, :], in_=ps[:, :])
            if "dump_negm" in DBG and tag == "l0nm":
                v.tensor_copy(out=DBG_ATT2[0][:, 0:8], in_=mq[:, :])
                v.tensor_copy(out=DBG_ATT2[0][:, 8:16], in_=mk[:, :])
                v.tensor_copy(out=DBG_ATT2[0][0:1, 16:2064], in_=flat[0:1, :])
                v.tensor_copy(out=DBG_ATT2[0][0:1, 2064:2112], in_=mrow[0:1, :, :])
                v.tensor_copy(out=DBG_ATT2[0][:, 2112:2128], in_=negm_bc[:, :])
            ctx.inc(v.drain(), "v")

        if "dump_negm" in DBG and tag == "l0nm":
            @block.gpsimd
            def _(g):
                ctx.wait(g, "v")
                ctx.dma(g, DBG_DRAM2[0][:, :], DBG_ATT2[0][:, :], name="o")
                ctx.wait(g, "o")
    _fence(ctx)


def emit_attention(ctx, consts, layer, qt_bf, kt_bf, v_bf, negm_bc, ctx_bf):
    nc, block = ctx.nc, ctx.block
    with ExitStack() as st:
        E = st.enter_context(nc.sbuf_tensor(_u("at_E"), [128, 2048], BF16))
        W = st.enter_context(nc.sbuf_tensor(_u("at_W"), [128, 2048], BF16))
        PT = st.enter_context(nc.sbuf_tensor(_u("at_PT"), [128, 16, 128], BF16))
        m8 = st.enter_context(nc.sbuf_tensor(_u("at_m8"), [128, 8], F32))
        thr = st.enter_context(nc.sbuf_tensor(_u("at_thr"), [128, 1], F32))
        rs = st.enter_context(nc.sbuf_tensor(_u("at_rs"), [128, 2], F32))
        rc = st.enter_context(nc.sbuf_tensor(_u("at_rc"), [128, 1], F32))
        rm = st.enter_context(nc.sbuf_tensor(_u("at_rm"), [128, 3], F32))
        if "dump_att" in DBG:
            DBG_ATT[0] = st.enter_context(nc.sbuf_tensor(_u("at_dbg"), [128, 4200], F32))
        ps_s = st.enter_context(nc.psum_tensor(_u("at_pss"), [128, 2, 1024], F32))
        ps_c = st.enter_context(nc.psum_tensor(_u("at_psc"), [128, 128], F32))

        for h in range(16):
            grp = h // 4
            qch, qba = h // 2, 64 * (h % 2)
            for qi in range(NKC):
                live = (qi + 1) if layer == 0 else NKC
                for half in range(2):
                    k0 = half * 8
                    nliv = min(8, max(0, live - k0))
                    if nliv == 0:
                        continue

                    @block.tensor
                    def _(te, h=h, qi=qi, half=half, k0=k0, nliv=nliv, qch=qch,
                          qba=qba, grp=grp):
                        if h == 0 and qi == 0 and half == 0:
                            ctx.wait_all(te)
                        ctx.wait(te, "a")
                        lhsT = qt_bf[qba:qba + 64, qch, 128 * qi:128 * (qi + 1)]
                        if layer == 0:
                            rhs_t = kt_bf[qba:qba + 64, grp, :]
                        else:
                            rhs_t = kt_bf[qba:qba + 64, qch, :]
                        for nb in range((nliv * 128 + 511) // 512):
                            n0 = k0 * 128 + nb * 512
                            nn = min(512, (k0 + nliv) * 128 - n0)
                            i = te.matmul(ps_s[:, half, nb * 512:nb * 512 + nn],
                                          lhsT, rhs_t[:, n0:n0 + nn],
                                          start=True, stop=True,
                                          tile_position=(qba, 0))
                        ctx.inc(i, "p")

                    if layer == 0 and k0 <= qi < k0 + 8:
                        @block.vector
                        def _(v, qi=qi, half=half, k0=k0):
                            ctx.wait(v, "p")
                            cp = (qi - k0) * 128
                            v.tensor_tensor(out=ps_s[:, half, cp:cp + 128],
                                            in0=ps_s[:, half, cp:cp + 128],
                                            in1=consts["causal"][:, :], op=ALU.add)
                            ctx.inc(v.drain(), "v")

                    @block.vector
                    def _(v, half=half, nliv=nliv):
                        ctx.wait(v, "p")
                        i = v.reduce_max(out=rm[:, half:half + 1],
                                         in_=ps_s[:, half, 0:nliv * 128],
                                         axis=AXL.X, negate=True)
                        ctx.inc(i, "v")

                nhalf = 1 if live <= 8 else 2

                @block.vector
                def _(v, nhalf=nhalf):
                    ctx.wait(v, "v")
                    if nhalf == 2:
                        i = v.tensor_tensor(out=rm[:, 2:3], in0=rm[:, 0:1],
                                            in1=rm[:, 1:2], op=ALU.min)
                    else:
                        i = v.tensor_copy(out=rm[:, 2:3], in_=rm[:, 0:1])
                    ctx.inc(i, "v")

                for half in range(2):
                    k0 = half * 8
                    nliv = min(8, max(0, live - k0))
                    if nliv == 0:
                        continue

                    @block.scalar
                    def _(a, h=h, half=half, k0=k0, nliv=nliv):
                        ctx.wait(a, "p")
                        ctx.wait(a, "v")
                        ctx.wait(a, "t")
                        i = a.activation(out=E[:, k0 * 128:(k0 + nliv) * 128],
                                         in_=ps_s[:, half, 0:nliv * 128], func=AF.Exp,
                                         bias=rm[:, 2:3], scale=1.0,
                                         accum_out=rs[:, half:half + 1])
                        ctx.inc(i, "a")

                nlive = live * 128
                if layer == 1:
                    @block.gpsimd
                    def _(gp):
                        ctx.wait(gp, "a")
                        i = gp.tensor_copy(out=W[:, :], in_=E[:, :])
                        ctx.inc(i, "g")

                    @block.vector
                    def _(v, h=h, qi=qi):
                        ctx.wait(v, "g")
                        for rnd in range(5):
                            i = v.max(out=m8[:, :], in_=W[:, :])
                            ctx.inc(i, "v")
                            ctx.wait(v, "v")
                            if rnd < 4:
                                v.tensor_scalar(out=PT[:, :, :], in0=W[:, :],
                                                scalar1=m8[:, 7:8], scalar2=NBIG,
                                                op0=ALU.is_ge, op1=ALU.mult)
                                v.tensor_tensor(out=W[:, :], in0=W[:, :],
                                                in1=PT[:, :, :], op=ALU.add)
                        i = v.tensor_copy(out=thr[:, 0:1], in_=m8[:, 5:6])
                        ctx.inc(i, "v")
                        ctx.wait(v, "v")
                        i = v.scalar_tensor_tensor(out=E[:, :], in0=E[:, :],
                                                   scalar=thr[:, 0:1], in1=E[:, :],
                                                   op0=ALU.is_ge, op1=ALU.mult,
                                                   accum_out=rs[:, 0:1])
                        ctx.inc(i, "v")
                        ctx.wait(v, "v")
                        i = v.reciprocal(out=rc[:, 0:1], in_=rs[:, 0:1])
                        ctx.inc(i, "v")
                        ctx.wait(v, "v")
                        v.tensor_scalar_mul(E[:, :], E[:, :], rc[:, 0:1])
                        ctx.inc(v.drain(), "v")
                else:
                    @block.vector
                    def _(v, h=h, qi=qi, nlive=nlive, live=live):
                        ctx.wait(v, "a")
                        if live > 8:
                            i = v.tensor_tensor(out=rs[:, 0:1], in0=rs[:, 0:1],
                                                in1=rs[:, 1:2], op=ALU.add)
                            ctx.inc(i, "v")
                            ctx.wait(v, "v")
                        i = v.reciprocal(out=rc[:, 0:1], in_=rs[:, 0:1])
                        ctx.inc(i, "v")
                        ctx.wait(v, "v")
                        if "dump_att" in DBG and h == 0 and qi == 2:
                            v.tensor_copy(out=DBG_ATT[0][:, 0:nlive], in_=E[:, 0:nlive])
                            v.tensor_copy(out=DBG_ATT[0][:, 2048:2049], in_=rs[:, 0:1])
                            v.tensor_copy(out=DBG_ATT[0][:, 2049:2050], in_=rc[:, 0:1])
                        v.tensor_scalar_mul(E[:, 0:nlive], E[:, 0:nlive], rc[:, 0:1])
                        ctx.inc(v.drain(), "v")

                @block.scalar
                def _(a, live=live):
                    ctx.wait(a, "v")
                    ctx.wait(a, "p")
                    ctx.wait(a, "t")
                    i = a.dma_start(out=PT[:, 0:live, :],
                                    in_=E[:, 0:live * 128], transpose=True)
                    ctx.inc(i, "t", 16)

                @block.tensor
                def _(te, h=h, qi=qi, grp=grp, live=live, qba=qba):
                    ctx.wait(te, "t")
                    ctx.wait(te, "v")
                    for c in range(live):
                        if layer == 0:
                            va = v_bf[:, c, 64 * grp:64 * grp + 64]
                        else:
                            va = v_bf[:, c, 64 * h:64 * h + 64]
                        i = te.matmul(ps_c[qba:qba + 64, :], va, PT[:, c, :],
                                      start=(c == 0), stop=(c == live - 1),
                                      tile_position=(0, qba), skip_group_check=True)
                    ctx.inc(i, "p")

                @block.vector
                def _(v, h=h, qi=qi, qba=qba):
                    ctx.wait(v, "p")
                    v.tensor_copy(out=ctx_bf[qba:qba + 64, h // 2, 128 * qi:128 * (qi + 1)],
                                  in_=ps_c[qba:qba + 64, :])
                    ctx.inc(v.drain(), "v")
        if "dump_att" in DBG:
            @block.gpsimd
            def _(g):
                ctx.wait_all(g)
                ctx.dma(g, DBG_DRAM[0][:, :], DBG_ATT[0][:, :], name="o")
                ctx.wait(g, "o")
        if "dump_ctx" in DBG:
            with nc.sbuf_tensor(_u("dmpc"), [128, 2048], F32) as dmpc:
                for c in range(NCH):
                    @block.gpsimd
                    def _(g, c=c):
                        ctx.wait_all(g)
                        i = g.tensor_copy(out=dmpc[:, :], in_=ctx_bf[:, c, :])
                        ctx.inc(i, "g")
                        ctx.wait(g, "g")
                        ctx.dma(g, DBG_DRAM[0][c * 128:(c + 1) * 128, 0:2048],
                                dmpc[:, :], name="o")
                        ctx.wait(g, "o")
    _fence(ctx)


def emit_l0_attn(ctx, consts, wd, tabs, xT, scr_dram):
    nc, block = ctx.nc, ctx.block
    with ExitStack() as st:
        xr_bf = st.enter_context(nc.sbuf_tensor(_u("l0_xr"), [128, 8, 2048], BF16))
        emit_rope(ctx, wd["xro"], tabs, xr_bf)

        qt_bf = st.enter_context(nc.sbuf_tensor(_u("l0_qt"), [128, 8, 2048], BF16))
        ktd_bf = st.enter_context(nc.sbuf_tensor(_u("l0_kt"), [128, 4, 2048], BF16))
        v_bf = st.enter_context(nc.sbuf_tensor(_u("l0_v"), [128, 16, 256], BF16))

        with ExitStack() as st2:
            saved, ctx.stack = ctx.stack, st2
            wq = load_weight_bf16(ctx, wd["l0wq"], 1024, 0, 1024, "l0wq")

            def evq(v, psap, oc, tg):
                v.tensor_copy(out=qt_bf[:, oc, tg * 512:(tg + 1) * 512], in_=psap)
            emit_proj_featmajor(ctx, wq, 1024, 1024, xr_bf, evq, "l0psq")
            ctx.stack = saved

        with ExitStack() as st2:
            saved, ctx.stack = ctx.stack, st2
            wk = load_weight_bf16(ctx, wd["l0wk"], 1024, 0, 256, "l0wk")

            def evk(v, psap, oc, tg):
                for gg in range(2):
                    gidx = 2 * oc + gg
                    for hf in range(2):
                        v.tensor_scalar(out=ktd_bf[64 * hf:64 * hf + 64, gidx,
                                                   tg * 512:(tg + 1) * 512],
                                        in0=psap[64 * gg:64 * gg + 64, :], scalar1=0.125,
                                        scalar2=None, op0=ALU.mult)
            emit_proj_featmajor(ctx, wk, 1024, 256, xr_bf, evk, "l0psk")
            ctx.stack = saved

        with ExitStack() as st2:
            saved, ctx.stack = ctx.stack, st2
            wv = load_weight_bf16(ctx, wd["l0wv"], 1024, 0, 256, "l0wv")
            with nc.psum_tensor(_u("l0psv"), [128, 2, 512], F32) as psv:
                for tc in range(NKC):
                    par = tc % 2

                    @block.tensor
                    def _(te, tc=tc, par=par):
                        if tc == 0:
                            ctx.wait_all(te)
                        if tc >= 2:
                            ctx.wait(te, "v")
                        for ic in range(NCH):
                            i = te.matmul(psv[:, par, 0:256],
                                          xr_bf[:, ic, tc * 128:(tc + 1) * 128],
                                          wv[:, ic, :], start=(ic == 0), stop=(ic == NCH - 1))
                        ctx.inc(i, "p")

                    @block.vector
                    def _(v, tc=tc, par=par):
                        ctx.wait(v, "p")
                        v.tensor_copy(out=v_bf[:, tc, :], in_=psv[:, par, 0:256])
                        ctx.inc(v.drain(), "v")
            _fence(ctx)
            ctx.stack = saved

        emit_attention(ctx, consts, 0, qt_bf, ktd_bf, v_bf, None, xr_bf)

        with ExitStack() as st2:
            saved, ctx.stack = ctx.stack, st2
            wo = load_weight_bf16(ctx, wd["l0wo"], 1024, 0, 1024, "l0wo")

            def evo(v, psap, oc, tg):
                v.tensor_tensor(out=xT[:, oc, tg * 512:(tg + 1) * 512],
                                in0=psap, in1=xT[:, oc, tg * 512:(tg + 1) * 512], op=ALU.add)
            emit_proj_featmajor(ctx, wo, 1024, 1024, xr_bf, evo, "l0pso")
            ctx.stack = saved
    emit_layernorm_inplace(ctx, consts, xT)


def emit_l0_ffn(ctx, consts, wd, xT):
    nc, block = ctx.nc, ctx.block
    with ExitStack() as st:
        q_bf = st.enter_context(nc.sbuf_tensor(_u("f0_q"), [128, 8, 2048], BF16))
        sc2 = st.enter_context(nc.sbuf_tensor(_u("f0_sc2"), [128, 2], F32))
        scs = st.enter_context(nc.sbuf_tensor(_u("f0_scs"), [1, 4], F32))
        bias = st.enter_context(nc.sbuf_tensor(_u("f0_bias"), [128, 8], F32))

        @block.gpsimd
        def _(g):
            ctx.wait_all(g)
            ctx.dma(g, scs[0:1, 0:2], wd["l0sc"][0:1, 0:2])
            ctx.dma(g, bias[:, :], wd["l0fb"][:, :])

        @block.vector
        def _(v):
            ctx.wait(v, "d")
            i = v.reciprocal(out=scs[0:1, 2:3], in_=scs[0:1, 0:1])
            ctx.inc(i, "v")
            ctx.wait(v, "v")
            v.tensor_tensor(out=scs[0:1, 3:4], in0=scs[0:1, 0:1],
                            in1=scs[0:1, 1:2], op=ALU.mult)
            ctx.inc(v.drain(), "v")

        with nc.psum_tensor(_u("f0psb"), [128, 2], F32) as psb:
            @block.tensor
            def _(te):
                ctx.wait(te, "v")
                i = te.matmul(psb[:, :], consts["ones_row_f"][0:1, :], scs[0:1, 2:4],
                              start=True, stop=True)
                ctx.inc(i, "p")

            @block.vector
            def _(v):
                ctx.wait(v, "p")
                v.tensor_copy(out=sc2[:, :], in_=psb[:, :])
                ctx.inc(v.drain(), "v")
        _fence(ctx)

        with ExitStack() as stq:
            t = stq.enter_context(nc.sbuf_tensor(_u("f0_t"), [128, 2048], F32))
            sg = stq.enter_context(nc.sbuf_tensor(_u("f0_sg"), [128, 2048], F32))
            aa = stq.enter_context(nc.sbuf_tensor(_u("f0_a"), [128, 2048], F32))
            ib = stq.enter_context(nc.sbuf_tensor(_u("f0_ib"), [128, 2048], I32))
            sp = stq.enter_context(nc.sbuf_tensor(_u("f0_sp"), [128, 2048], F32))
            uu = stq.enter_context(nc.sbuf_tensor(_u("f0_u"), [128, 2048], F32))
            vv = stq.enter_context(nc.sbuf_tensor(_u("f0_v"), [128, 2048], F32))
            bb = stq.enter_context(nc.sbuf_tensor(_u("f0_b"), [128, 2048], F32))
            for c in range(NCH):
                @block.vector
                def _(v, c=c):
                    if c == 0:
                        ctx.wait_all(v)
                    else:
                        ctx.wait(v, "g")
                    v.tensor_scalar(out=t[:, :], in0=xT[:, c, :], scalar1=sc2[:, 0:1],
                                    scalar2=448.0, op0=ALU.mult, op1=ALU.min)
                    v.tensor_scalar_max(t[:, :], t[:, :], -448.0)
                    ctx.inc(v.drain(), "v")

                @block.scalar
                def _(a, c=c):
                    ctx.wait(a, "v")
                    ctx.wait(a, "g")
                    a.activation(out=sg[:, :], in_=t[:, :], func=AF.Sign, bias=0.0, scale=1.0)
                    i = a.activation(out=aa[:, :], in_=t[:, :], func=AF.Abs, bias=0.0, scale=1.0)
                    ctx.inc(i, "a")

                @block.vector
                def _(v, c=c):
                    ctx.wait(v, "a")
                    v.tensor_scalar(out=ib[:, :], in0=aa[:, :].bitcast(I32),
                                    scalar1=consts["expmask"][:, 0:1], scalar2=None,
                                    op0=ALU.bitwise_and)
                    v.tensor_scalar(out=sp[:, :], in0=ib[:, :].bitcast(F32),
                                    scalar1=0.125, scalar2=2.0 ** -9,
                                    op0=ALU.mult, op1=ALU.max)
                    v.reciprocal(out=uu[:, :], in_=sp[:, :])
                    v.tensor_tensor(out=uu[:, :], in0=aa[:, :], in1=uu[:, :], op=ALU.mult)
                    v.tensor_scalar_min(uu[:, :], uu[:, :], 16.0)
                    v.tensor_scalar(out=vv[:, :], in0=uu[:, :], scalar1=0.5,
                                    scalar2=None, op0=ALU.add)
                    for cbit in (8.0, 8.0, 4.0, 2.0, 1.0):
                        v.tensor_scalar(out=bb[:, :], in0=vv[:, :], scalar1=cbit,
                                        scalar2=None, op0=ALU.is_ge)
                        v.scalar_tensor_tensor(out=vv[:, :], in0=bb[:, :], scalar=-cbit,
                                               in1=vv[:, :], op0=ALU.mult, op1=ALU.add)
                    v.tensor_tensor(out=uu[:, :], in0=uu[:, :], in1=vv[:, :], op=ALU.subtract)
                    v.tensor_scalar(out=uu[:, :], in0=uu[:, :], scalar1=0.5,
                                    scalar2=None, op0=ALU.add)
                    v.tensor_tensor(out=uu[:, :], in0=uu[:, :], in1=sp[:, :], op=ALU.mult)
                    ctx.inc(v.drain(), "v")

                @block.gpsimd
                def _(g, c=c):
                    ctx.wait(g, "v")
                    g.tensor_tensor(out=q_bf[:, c, :], in0=uu[:, :], in1=sg[:, :], op=ALU.mult)
                    ctx.inc(g.drain(), "g")
        _fence(ctx)

        with ExitStack() as st2:
            saved, ctx.stack = ctx.stack, st2
            fw = load_weight_bf16(ctx, wd["l0fw"], 1024, 0, 1024, "l0fw")
            with nc.sbuf_tensor(_u("f0_ev"), [128, 512], F32) as ev:
                def evf(v, psap, oc, tg):
                    v.tensor_scalar(out=ev[:, :], in0=psap, scalar1=sc2[:, 1:2],
                                    scalar2=bias[:, oc:oc + 1], op0=ALU.mult, op1=ALU.add)
                    v.tensor_tensor(out=xT[:, oc, tg * 512:(tg + 1) * 512],
                                    in0=ev[:, :], in1=xT[:, oc, tg * 512:(tg + 1) * 512],
                                    op=ALU.add)
                emit_proj_featmajor(ctx, fw, 1024, 1024, q_bf, evf, "f0ps")
            ctx.stack = saved
    emit_layernorm_inplace(ctx, consts, xT)


def emit_l1_attn(ctx, consts, wd, xT, xbf, scr_dram):
    nc, block = ctx.nc, ctx.block
    with ExitStack() as st:
        qt_bf = st.enter_context(nc.sbuf_tensor(_u("l1_qt"), [128, 8, 2048], BF16))
        kt_bf = st.enter_context(nc.sbuf_tensor(_u("l1_kt"), [128, 8, 2048], BF16))
        v_bf = st.enter_context(nc.sbuf_tensor(_u("l1_v"), [128, 16, 1024], BF16))
        bq = st.enter_context(nc.sbuf_tensor(_u("l1_bq"), [128, 8], F32))
        bk = st.enter_context(nc.sbuf_tensor(_u("l1_bk"), [128, 8], F32))
        bo = st.enter_context(nc.sbuf_tensor(_u("l1_bo"), [128, 8], F32))
        bv_scope = ExitStack()
        bv_bc = bv_scope.enter_context(nc.sbuf_tensor(_u("l1_bvbc"), [128, 1024], BF16))
        bvr_scope = ExitStack()
        bvr = bvr_scope.enter_context(nc.sbuf_tensor(_u("l1_bvr"), [1, 1024], F32))

        @block.gpsimd
        def _(g):
            ctx.wait_all(g)
            ctx.dma(g, bq[:, :], wd["l1bq"][:, :])
            ctx.dma(g, bk[:, :], wd["l1bk"][:, :])
            ctx.dma(g, bo[:, :], wd["l1bo"][:, :])
            ctx.dma(g, bvr[0:1, :], wd["l1bv_row"].rearrange("(o a) b -> o (a b)", o=1)[0:1, :])

        with nc.psum_tensor(_u("l1psbv"), [128, 2, 512], F32) as psbv:
            @block.tensor
            def _(te):
                ctx.wait(te, "d")
                for half in range(2):
                    i = te.matmul(psbv[:, half, :], consts["ones_row_f"][0:1, :],
                                  bvr[0:1, half * 512:(half + 1) * 512],
                                  start=True, stop=True)
                ctx.inc(i, "p")

            @block.vector
            def _(v):
                ctx.wait(v, "p")
                v.tensor_copy(out=bv_bc[:, 0:512], in_=psbv[:, 0, :])
                v.tensor_copy(out=bv_bc[:, 512:1024], in_=psbv[:, 1, :])
                ctx.inc(v.drain(), "v")
        _fence(ctx)
        bvr_scope.close()

        def evq(v, psap, oc, tg):
            v.tensor_scalar(out=qt_bf[:, oc, tg * 512:(tg + 1) * 512], in0=psap,
                            scalar1=bq[:, oc:oc + 1], scalar2=None, op0=ALU.add)
        for half in range(2):
            with ExitStack() as st2:
                saved, ctx.stack = ctx.stack, st2
                wq = load_weight_bf16(ctx, wd["l1wq"], 1024, half * 512,
                                      (half + 1) * 512, "l1wq")
                emit_proj_featmajor(ctx, wq, 1024, 512, xbf, evq, "l1psq", oc0=half * 4)
                ctx.stack = saved

        def evk(v, psap, oc, tg):
            v.tensor_scalar(out=kt_bf[:, oc, tg * 512:(tg + 1) * 512], in0=psap,
                            scalar1=bk[:, oc:oc + 1], scalar2=0.125,
                            op0=ALU.add, op1=ALU.mult)
        for half in range(2):
            with ExitStack() as st2:
                saved, ctx.stack = ctx.stack, st2
                wk = load_weight_bf16(ctx, wd["l1wk"], 1024, half * 512,
                                      (half + 1) * 512, "l1wk")
                emit_proj_featmajor(ctx, wk, 1024, 512, xbf, evk, "l1psk", oc0=half * 4)
                ctx.stack = saved

        for vhalf in range(2):
          with ExitStack() as st2:
            saved, ctx.stack = ctx.stack, st2
            wv = load_weight_bf16(ctx, wd["l1wv"], 1024, vhalf * 512,
                                  (vhalf + 1) * 512, "l1wv")
            with nc.psum_tensor(_u("l1psv"), [128, 2, 512], F32) as psv:
                for tc in range(NKC):
                    for vh in [vhalf]:
                        par = tc % 2

                        @block.tensor
                        def _(te, tc=tc, vh=vh, par=par):
                            if tc == 0:
                                ctx.wait_all(te)
                            else:
                                ctx.wait(te, "v")
                            for ic in range(NCH):
                                i = te.matmul(psv[:, par, :],
                                              xbf[:, ic, tc * 128:(tc + 1) * 128],
                                              wv[:, ic, :],
                                              start=(ic == 0), stop=(ic == NCH - 1))
                            ctx.inc(i, "p")

                        @block.vector
                        def _(v, tc=tc, vh=vh, par=par):
                            ctx.wait(v, "p")
                            v.tensor_tensor(out=v_bf[:, tc, vh * 512:(vh + 1) * 512],
                                            in0=psv[:, par, :],
                                            in1=bv_bc[:, vh * 512:(vh + 1) * 512], op=ALU.add)
                            ctx.inc(v.drain(), "v")
            _fence(ctx)
            ctx.stack = saved

        bv_scope.close()
        emit_attention(ctx, consts, 1, qt_bf, kt_bf, v_bf, None, xbf)

        for half in range(2):
            with ExitStack() as st2:
                saved, ctx.stack = ctx.stack, st2
                wo = load_weight_bf16(ctx, wd["l1wo"], 1024, half * 512,
                                      (half + 1) * 512, "l1wo")
                with nc.sbuf_tensor(_u("l1_ev"), [128, 512], F32) as ev:
                    def evo(v, psap, oc, tg):
                        v.tensor_scalar(out=ev[:, :], in0=psap, scalar1=bo[:, oc:oc + 1],
                                        scalar2=None, op0=ALU.add)
                        v.tensor_tensor(out=xT[:, oc, tg * 512:(tg + 1) * 512],
                                        in0=ev[:, :],
                                        in1=xT[:, oc, tg * 512:(tg + 1) * 512],
                                        op=ALU.add)
                    emit_proj_featmajor(ctx, wo, 1024, 512, xbf, evo, "l1pso",
                                        oc0=half * 4)
                ctx.stack = saved
    emit_layernorm_inplace(ctx, consts, xT)


def emit_l1_ffn(ctx, consts, wd, xT, xbf, scr2):
    nc, block = ctx.nc, ctx.block
    with ExitStack() as st:
        b1 = st.enter_context(nc.sbuf_tensor(_u("f1_b1"), [128, 32], F32))
        b2 = st.enter_context(nc.sbuf_tensor(_u("f1_b2"), [128, 8], F32))
        scl = st.enter_context(nc.sbuf_tensor(_u("f1_scl"), [1, 1], F32))
        se = st.enter_context(nc.sbuf_tensor(_u("f1_se"), [1, 8, 1024], BF16))
        se2q = st.enter_context(nc.sbuf_tensor(_u("f1_se2q"), [1, 16, 256], BF16))
        p_sb = st.enter_context(nc.sbuf_tensor(_u("f1_p"), [1, 4, 2048], BF16))
        r_sb = st.enter_context(nc.sbuf_tensor(_u("f1_r"), [1, 16, 512], BF16))
        hT = st.enter_context(nc.sbuf_tensor(_u("f1_h"), [128, 32, 512], BF16))

        @block.gpsimd
        def _(g):
            ctx.wait_all(g)
            ctx.dma(g, b1[:, :], wd["l1b1"][:, :])
            ctx.dma(g, b2[:, :], wd["l1b2"][:, :])
            ctx.dma(g, scl[0:1, 0:1], wd["l1sc"][0:1, 0:1])

        do_prep = "noprep" not in DBG
        lt_scope = ExitStack()
        lt = lt_scope.enter_context(nc.sbuf_tensor(_u("f1_lt"), [1, 8, 1024], BF16))
        with nc.sbuf_tensor(_u("f1_ptt"), [128, 128], BF16) as ptt:
          if do_prep:
              for c in range(NCH):
                  @block.scalar
                  def _(a, c=c):
                      if c == 0:
                          ctx.wait_all(a)
                      else:
                          ctx.wait(a, "d")
                      i = a.dma_start(out=ptt[:, :], in_=xbf[:, c, 0:128], transpose=True)
                      ctx.inc(i, "t", 16)

                  @block.gpsimd
                  def _(g, c=c):
                      ctx.wait(g, "t")
                      ctx.dma(g, scr2[:, c * 128:(c + 1) * 128], ptt[1:17, :])

              @block.gpsimd
              def _(g):
                  ctx.wait(g, "d")
                  i = g.dma_start(out=lt[0:1, :, :],
                                  in_=scr2.rearrange("(o r) p -> o r p", o=1)[0:1, 0:16:2, :])
                  ctx.inc(i, "d", 16)
          _fence(ctx)

        @block.vector
        def _(v):
            if not do_prep:
                ctx.inc(v.drain(), "v")
                return
            ctx.wait(v, "v")
            for i8 in range(8):
                v.tensor_scalar_mul(se[0:1, i8, :], lt[0:1, i8, :], scl[0:1, 0:1])
            ctx.inc(v.drain(), "v")
            ctx.wait(v, "v")
            for i4 in range(4):
                for q4 in range(4):
                    for half in range(2):
                        v.tensor_copy(out=se2q[0:1, i4 * 4 + q4, half * 128:(half + 1) * 128],
                                      in_=se[0:1, 4 + i4, half * 512 + q4:(half + 1) * 512:4])
            ctx.inc(v.drain(), "v")
        _fence(ctx)
        lt_scope.close()

        with nc.psum_tensor(_u("f1psp"), [1, 4, 512], F32) as psp:
            for tg in (range(NTG) if do_prep else []):
                @block.tensor
                def _(te, tg=tg):
                    if tg == 0:
                        ctx.wait_all(te)
                    else:
                        ctx.wait(te, "v")
                    for i4 in range(4):
                        for ic in range(NCH):
                            i = te.matmul(psp[:, i4, :],
                                          xbf[:, ic, 2 + 2 * i4:3 + 2 * i4],
                                          xbf[:, ic, tg * 512:(tg + 1) * 512],
                                          start=(ic == 0), stop=(ic == NCH - 1))
                    ctx.inc(i, "p")

                @block.vector
                def _(v, tg=tg):
                    ctx.wait(v, "p")
                    v.tensor_copy(out=p_sb[0:1, :, tg * 512:(tg + 1) * 512], in_=psp[0:1, :, :])
                    ctx.inc(v.drain(), "v")
        _fence(ctx)

        for tg in range(NTG):
            for hs in range(4):
                with ExitStack() as st2:
                    saved, ctx.stack = ctx.stack, st2
                    w1 = load_weight_bf16(ctx, wd["l1w1"], 1024, hs * 1024, (hs + 1) * 1024,
                                          "f1w1")
                    with nc.psum_tensor(_u("f1ps1"), [128, 2, 512], F32) as ps1:
                        for hb8 in range(8):
                            hb = hs * 8 + hb8
                            par = hb8 % 2

                            @block.tensor
                            def _(te, tg=tg, hs=hs, hb8=hb8, hb=hb, par=par):
                                if hb8 == 0:
                                    ctx.wait_all(te)
                                if hb8 >= 2:
                                    ctx.wait(te, "a")
                                for ic in range(NCH):
                                    i = te.matmul(ps1[:, par, :],
                                              w1[:, ic, hb8 * 128:(hb8 + 1) * 128],
                                              xbf[:, ic, tg * 512:(tg + 1) * 512],
                                              start=(ic == 0),
                                              stop=(ic == NCH - 1 and "nolora" in DBG))
                                if "nolora" not in DBG:
                                    i4 = hb // 8
                                    jsl = (hb % 8) * 128
                                    i = te.matmul(ps1[:, par, :], se[0:1, i4, jsl:jsl + 128],
                                                  p_sb[0:1, i4, tg * 512:(tg + 1) * 512],
                                                  start=False, stop=True)
                                ctx.inc(i, "p")

                            @block.scalar
                            def _(a, hb=hb, par=par):
                                ctx.wait(a, "p")
                                i = a.activation(out=hT[:, hb, :], in_=ps1[:, par, :],
                                                 func=AF.Gelu, bias=b1[:, hb:hb + 1],
                                                 scale=1.0)
                                ctx.inc(i, "a")
                    _fence(ctx)
                    ctx.stack = saved

            if "dumph" in DBG and tg == 0:
                with nc.sbuf_tensor(_u("dmp"), [128, 512], F32) as dmp:
                    for hc in range(32):
                        @block.gpsimd
                        def _(g, hc=hc):
                            ctx.wait_all(g)
                            i = g.tensor_copy(out=dmp[:, :], in_=hT[:, hc, :])
                            ctx.inc(i, "g")
                            ctx.wait(g, "g")
                            ctx.dma(g, DBG_DUMP[0][(hc % 8) * 128:(hc % 8 + 1) * 128,
                                                   (hc // 8) * 512:(hc // 8 + 1) * 512],
                                    dmp[:, :], name="o")
                            ctx.wait(g, "o")
                _fence(ctx)

            for i4h in (range(2) if do_prep else []):
                with nc.psum_tensor(_u("f1psr"), [1, 8, 512], F32) as psr:
                    @block.tensor
                    def _(te, tg=tg, i4h=i4h):
                        ctx.wait_all(te)
                        for i4x in range(2):
                            i4 = 2 * i4h + i4x
                            for q4 in range(4):
                                for kc in range(8):
                                    i = te.matmul(psr[:, i4x * 4 + q4, :],
                                                  xbf[:, kc, 10 + 2 * i4:11 + 2 * i4],
                                                  hT[:, q4 * 8 + kc, :],
                                                  start=(kc == 0), stop=(kc == 7))
                        ctx.inc(i, "p")

                    @block.vector
                    def _(v, tg=tg, i4h=i4h):
                        ctx.wait(v, "p")
                        v.tensor_copy(out=r_sb[0:1, i4h * 8:(i4h + 1) * 8, :],
                                      in_=psr[0:1, :, :])
                        ctx.inc(v.drain(), "v")
                _fence(ctx)

            with nc.sbuf_tensor(_u("f1_ev2"), [128, 512], F32) as ev2, \
                 nc.psum_tensor(_u("f1ps2"), [128, 2, 512], F32) as ps2:
                for oc in range(NCH):
                    par = oc % 2
                    i4 = oc // 2
                    with ExitStack() as st2:
                        saved, ctx.stack = ctx.stack, st2
                        w2 = load_weight_bf16(ctx, wd["l1w2"], 4096, oc * 128,
                                              (oc + 1) * 128, "f1w2")

                        @block.tensor
                        def _(te, tg=tg, oc=oc, par=par, i4=i4, w2=w2):
                            ctx.wait_all(te)
                            for hc in range(32):
                                i = te.matmul(ps2[:, par, :], w2[:, hc, :],
                                          hT[:, hc, :], start=(hc == 0),
                                          stop=(hc == 31 and "nolora" in DBG))
                            if "nolora" not in DBG:
                                for q4 in range(4):
                                    i = te.matmul(ps2[:, par, :],
                                                  se2q[0:1, i4 * 4 + q4,
                                                       (oc % 2) * 128:(oc % 2) * 128 + 128],
                                                  r_sb[0:1, i4 * 4 + q4, :],
                                                  start=False, stop=(q4 == 3))
                            ctx.inc(i, "p")

                        @block.vector
                        def _(v, tg=tg, oc=oc, par=par):
                            ctx.wait(v, "p")
                            v.tensor_scalar(out=ev2[:, :], in0=ps2[:, par, :],
                                            scalar1=b2[:, oc:oc + 1], scalar2=None,
                                            op0=ALU.add)
                            v.tensor_tensor(out=xT[:, oc, tg * 512:(tg + 1) * 512],
                                            in0=ev2[:, :],
                                            in1=xT[:, oc, tg * 512:(tg + 1) * 512],
                                            op=ALU.add)
                            ctx.inc(v.drain(), "v")
                        _fence(ctx)
                        ctx.stack = saved
    if "noln" not in DBG:
        emit_layernorm_inplace(ctx, consts, xT)


def build_program(stages):
    nc = bass.Bass("TRN2", target_bir_lowering=False, detect_race_conditions=False)
    cts = _make_consts(nc)
    wd = {}

    def din(nm, shp):
        wd[nm] = nc.dram_tensor(nm, shp, F32, kind="ExternalInput").ap()

    if "l0a" in stages:
        din("l0wq", [1024, 1024]); din("l0wk", [1024, 256])
        din("l0wv", [1024, 256]); din("l0wo", [1024, 1024])
    if "l0f" in stages:
        din("l0fw", [1024, 1024]); din("l0fb", [128, 8]); din("l0sc", [1, 2])
    if "l1a" in stages:
        for nm in ("l1wq", "l1wk", "l1wv", "l1wo"):
            din(nm, [1024, 1024])
        din("l1bq", [128, 8]); din("l1bk", [128, 8])
        din("l1bv_row", [8, 128]); din("l1bo", [128, 8])
    if "l1f" in stages:
        din("l1w1", [1024, 4096]); din("l1b1", [128, 32])
        din("l1w2", [4096, 1024]); din("l1b2", [128, 8]); din("l1sc", [1, 1])
    tabs = {}
    if "l0a" in stages:
        for nm in ("coshi", "coslo", "sinhi", "sinlo"):
            tabs[nm] = nc.dram_tensor(nm, [512, 2048], F32, kind="ExternalInput").ap()
    xin, xout, xro = [], [], []
    for j in range(BPC):
        xin.append(nc.dram_tensor(f"xin{j}", [1024, 2048], F32, kind="ExternalInput").ap())
        xout.append(nc.dram_tensor(f"xout{j}", [1024, 2048], F32, kind="ExternalOutput").ap())
        if "l0a" in stages:
            xro.append(nc.dram_tensor(f"xro{j}", [1024, 2048], F32, kind="ExternalInput").ap())
    scr = nc.dram_tensor("scr", [256, 8], F32, kind="Internal").ap()
    if "dump_att" in DBG:
        DBG_DRAM[0] = nc.dram_tensor("dbgatt", [128, 4200], F32, kind="ExternalOutput").ap()
    if "dump_ctx" in DBG:
        DBG_DRAM[0] = nc.dram_tensor("dbgatt", [1024, 2048], F32, kind="ExternalOutput").ap()
    if "dump_negm" in DBG:
        DBG_DRAM2[0] = nc.dram_tensor("dbgnegm", [128, 2128], F32, kind="ExternalOutput").ap()
    if "dumph" in DBG:
        DBG_DUMP[0] = nc.dram_tensor("dbgout", [1024, 2048], F32, kind="ExternalOutput").ap()
    scr2 = nc.dram_tensor("scr2", [16, 1024], BF16, kind="Internal").ap()

    need_bf = ("l1a" in stages) or ("l1f" in stages)
    with ExitStack() as stack:
        blk = stack.enter_context(nc.Block())
        ctx = Ctx(nc, blk, stack)
        consts = _load_consts(ctx, cts)
        _fence(ctx)
        xT = stack.enter_context(nc.sbuf_tensor(_u("xT"), [128, 8, 2048], F32))
        for j in range(BPC):
            with ExitStack() as bst:
                emit_load_stream(ctx, xin[j], xT, None)
                if "l0a" in stages:
                    wd2 = dict(wd)
                    wd2["xro"] = xro[j]
                    emit_l0_attn(ctx, consts, wd2, tabs, xT, scr)
                if "l0f" in stages:
                    emit_l0_ffn(ctx, consts, wd, xT)
                xbf = None
                if need_bf:
                    xbf = bst.enter_context(nc.sbuf_tensor(_u("xbf"), [128, 8, 2048], BF16))
                if "l1a" in stages:
                    emit_refresh_xbf(ctx, xT, xbf)
                    emit_l1_attn(ctx, consts, wd, xT, xbf, scr)
                if "l1f" in stages:
                    emit_refresh_xbf(ctx, xT, xbf)
                    emit_l1_ffn(ctx, consts, wd, xT, xbf, scr2)
                emit_store_stream(ctx, xT, xout[j])
    return nc


def _host_prep(inputs):
    d = {}
    d["l0wq"] = inputs["L0_attn_W_q"].T
    d["l0wk"] = inputs["L0_attn_W_k"].T
    d["l0wv"] = inputs["L0_attn_W_v"].T
    d["l0wo"] = inputs["L0_attn_W_o"].T
    d["l0fw"] = inputs["L0_ffn_weight_fp8"].T
    d["l0fb"] = inputs["L0_ffn_bias"].reshape(8, 128).T
    d["l0sc"] = np.stack([np.asarray(inputs["L0_ffn_input_scale"]).reshape(()),
                          np.asarray(inputs["L0_ffn_weight_scale"]).reshape(())]).reshape(1, 2)
    d["l1wq"] = inputs["L1_attn_Wq"].T
    d["l1wk"] = inputs["L1_attn_Wk"].T
    d["l1wv"] = inputs["L1_attn_Wv"].T
    d["l1wo"] = inputs["L1_attn_Wo"].T
    d["l1bq"] = inputs["L1_attn_bq"].reshape(8, 128).T
    d["l1bk"] = inputs["L1_attn_bk"].reshape(8, 128).T
    d["l1bv_row"] = inputs["L1_attn_bv"].reshape(8, 128)
    d["l1bo"] = inputs["L1_attn_bo"].reshape(8, 128).T
    d["l1w1"] = inputs["L1_ffn_W1"].T
    d["l1b1"] = inputs["L1_ffn_b1"].reshape(32, 128).T
    d["l1w2"] = inputs["L1_ffn_W2"].T
    d["l1b2"] = inputs["L1_ffn_b2"].reshape(8, 128).T
    d["l1sc"] = np.asarray(inputs["L1_ffn_scale"], np.float32).reshape(1, 1)
    f = np.arange(512, dtype=np.float64) / 512.0
    theta = 1.0 / (10000.0 ** f)
    ang = theta[:, None] * np.arange(S, dtype=np.float64)[None, :]
    for nm, fn in (("cos", np.cos), ("sin", np.sin)):
        v = fn(ang)
        hi = v.astype(np.float32)
        lo = (v - hi.astype(np.float64)).astype(np.float32)
        d[nm + "hi"] = hi
        d[nm + "lo"] = lo
    return {k: np.ascontiguousarray(v, dtype=np.float32) for k, v in d.items()}


def _get_program(stages):
    key = tuple(stages)
    if key not in _PROGRAM_CACHE:
        _PROGRAM_CACHE[key] = build_program(stages)
    return _PROGRAM_CACHE[key]


def _prog_input_names(nc):
    names = set()
    for a in nc.m.functions[0].allocations:
        if isinstance(a, mybir.MemoryLocationSet) and a.kind == "ExternalInput":
            for ml in a.memorylocations:
                names.add(ml.name)
    return names


def run_stage_group(stages, shared, streams, xros):
    nc = _get_program(stages)
    names = _prog_input_names(nc)
    in_maps = []
    for c in range(NC_N):
        m = {k: v for k, v in shared.items() if k in names}
        for j in range(BPC):
            m[f"xin{j}"] = streams[c][j]
            if f"xro{j}" in names:
                m[f"xro{j}"] = xros[c][j]
        in_maps.append(m)
    res = run_bass_kernel_spmd(nc, in_maps, core_ids=list(range(NC_N)))
    return [[np.asarray(res.results[c][f"xout{j}"]) for j in range(BPC)] for c in range(NC_N)]


STAGE_GROUPS = [("l0a", "l0f", "l1a", "l1f")]


def kernel(**inputs):
    shared = _host_prep(inputs)
    x = np.asarray(inputs["L0_attn_X"], np.float32)
    streams, xros = [], []
    for c in range(NC_N):
        ss, rr = [], []
        for j in range(BPC):
            xb = x[c * BPC + j]
            ss.append(np.ascontiguousarray(xb.T))
            rr.append(np.ascontiguousarray(
                np.concatenate([xb[:, 0::2], xb[:, 1::2]], axis=1).T))
        streams.append(ss)
        xros.append(rr)
    for stages in STAGE_GROUPS:
        streams = run_stage_group(stages, shared, streams, xros)
    out = np.empty((B, S, D), np.float32)
    for c in range(NC_N):
        for j in range(BPC):
            out[c * BPC + j] = streams[c][j].T
    return out


# version 07
# speedup vs baseline: 1.1257x; 1.1178x over previous
import numpy as np
import ml_dtypes
import concourse.bass as bass
import concourse.mybir as mybir
from concourse.bass_utils import run_bass_kernel_spmd
from contextlib import ExitStack

F32 = mybir.dt.float32
BF16 = mybir.dt.bfloat16
I32 = mybir.dt.int32
AF = mybir.ActivationFunctionType
ALU = mybir.AluOpType
AXL = mybir.AxisListType

B, S, D = 16, 2048, 1024
NC_N = 8
BPC = B // NC_N
NCH = D // 128
NKC = S // 128
NTG = S // 512
NBIG = -3.0e38

_PROGRAM_CACHE = {}
DBG = set()
DBG_DUMP = [None]
DBG_ATT = [None]
DBG_DRAM = [None]
DBG_ATT2 = [None]
DBG_DRAM2 = [None]

_NAME_SEQ = [0]


def _u(name):
    _NAME_SEQ[0] += 1
    return f"{name}_{_NAME_SEQ[0]}"




class Ctx:
    def __init__(self, nc, block, stack):
        self.nc = nc
        self.block = block
        self.stack = stack
        self.sems = {}
        self.cnt = {}
        for name in ("d", "p", "v", "a", "g", "t", "o", "s",
                     "w0", "w1", "w2", "w3"):
            self.sems[name] = stack.enter_context(nc.semaphore("sem_" + name))
            self.cnt[name] = 0

    def inc(self, inst, name, n=1):
        inst.then_inc(self.sems[name], n)
        self.cnt[name] += n
        return inst

    def dma(self, eng, out, in_, name="d"):
        return self.inc(eng.dma_start(out=out, in_=in_), name, 16)

    def wait(self, eng, name):
        eng.wait_ge(self.sems[name], self.cnt[name])

    def wait_all(self, eng):
        for name in ("d", "p", "v", "a", "g", "t", "o", "s",
                     "w0", "w1", "w2", "w3"):
            if self.cnt[name]:
                eng.wait_ge(self.sems[name], self.cnt[name])


def _fence(ctx):
    block = ctx.block

    @block.tensor
    def _(e):
        ctx.inc(e.drain(), "p")

    @block.vector
    def _(e):
        ctx.inc(e.drain(), "v")

    @block.scalar
    def _(e):
        ctx.inc(e.drain(), "a")

    @block.gpsimd
    def _(e):
        ctx.inc(e.drain(), "g")

    @block.sync
    def _(e):
        ctx.inc(e.drain(), "s")

    for emit in (block.tensor, block.vector, block.scalar, block.gpsimd, block.sync):
        @emit
        def _(e):
            ctx.wait_all(e)


def _make_consts(nc):
    c = {}
    c["ident"] = nc.inline_tensor(np.eye(128, dtype=ml_dtypes.bfloat16), "c_ident")
    c["ones_row_f"] = nc.inline_tensor(np.ones((1, 128), np.float32), "c_onesrf")
    c["ones_col_f"] = nc.inline_tensor(np.ones((128, 1), np.float32), "c_onescf")
    mask = np.triu(np.ones((128, 128), np.float32), k=1) * (-1e9)
    c["causal"] = nc.inline_tensor(mask, "c_causal")
    c["expmask"] = nc.inline_tensor(np.full((128, 1), 0x7F800000, np.int32), "c_expmask")
    return c


def _load_consts(ctx, cts):
    nc, block = ctx.nc, ctx.block
    sb = {}
    sb["ident"] = ctx.stack.enter_context(nc.sbuf_tensor(_u("cs_ident"), [128, 128], BF16))
    sb["ones_row_f"] = ctx.stack.enter_context(nc.sbuf_tensor(_u("cs_onesrf"), [1, 128], F32))
    sb["ones_col_f"] = ctx.stack.enter_context(nc.sbuf_tensor(_u("cs_onescf"), [128, 1], F32))
    sb["causal"] = ctx.stack.enter_context(nc.sbuf_tensor(_u("cs_causal"), [128, 128], F32))
    sb["expmask"] = ctx.stack.enter_context(nc.sbuf_tensor(_u("cs_expmask"), [128, 1], I32))
    sb["eps"] = ctx.stack.enter_context(nc.sbuf_tensor(_u("cs_eps"), [128, 1], F32))

    @block.gpsimd
    def _(g):
        for k in ("ident", "ones_row_f", "ones_col_f", "causal", "expmask"):
            ctx.dma(g, sb[k][:, :], cts[k].ap()[:, :])

    @block.vector
    def _(v):
        v.memset(sb["eps"][:, :], 1e-5)
        ctx.inc(v.drain(), "v")
    return sb


def load_weight_bf16(ctx, w_dram, nin, c0, c1, name):
    nc, block = ctx.nc, ctx.block
    nchk = nin // 128
    ncol = c1 - c0
    wbf = ctx.stack.enter_context(nc.sbuf_tensor(_u(name + "_bf"), [128, nchk, ncol], BF16))
    sub = min(ncol, 512)
    DEPTH = 2 if sub >= 512 else 4
    slices = [(ic, s0) for ic in range(nchk) for s0 in range(0, ncol, sub)]
    with nc.sbuf_tensor(_u(name + "_f32"), [128, DEPTH, sub], F32) as wf:
        @block.gpsimd
        def _(g):
            ctx.wait_all(g)
            casts = []

            def emit_cast(ci):
                ic2, s02 = slices[ci]
                slot = ci % DEPTH
                g.wait_ge(ctx.sems["w%d" % slot], ctx.cnt["w%d" % slot])
                i = g.tensor_copy(out=wbf[:, ic2, s02:s02 + sub],
                                  in_=wf[:, slot, :])
                ctx.inc(i, "g")
                casts.append(ctx.cnt["g"])

            for si, (ic, s0) in enumerate(slices):
                slot = si % DEPTH
                if si >= DEPTH:
                    g.wait_ge(ctx.sems["g"], casts[si - DEPTH])
                i = g.dma_start(out=wf[:, slot, :],
                                in_=w_dram[ic * 128:(ic + 1) * 128,
                                           c0 + s0:c0 + s0 + sub])
                ctx.inc(i, "w%d" % slot, 16)
                if si >= DEPTH - 1:
                    emit_cast(si - (DEPTH - 1))
            for ci in range(max(0, len(slices) - (DEPTH - 1)), len(slices)):
                emit_cast(ci)
            ctx.inc(g.drain(), "g")
        _fence(ctx)
    return wbf


def emit_proj_featmajor(ctx, wbf, nin, nout, src_bf, evac, psname, oc0=0):
    nc, block = ctx.nc, ctx.block
    nic = nin // 128
    noc = nout // 128
    with nc.psum_tensor(_u(psname), [128, 2, 512], F32) as ps:
        for oc_ in range(noc):
            for tg in range(NTG):
                par = (oc_ * NTG + tg) % 2

                @block.tensor
                def _(te, oc=oc_, tg=tg, par=par):
                    if oc == 0 and tg == 0:
                        ctx.wait_all(te)
                    if oc * NTG + tg >= 2:
                        ctx.wait(te, "v")
                    for ic in range(nic):
                        i = te.matmul(ps[:, par, :], wbf[:, ic, oc * 128:(oc + 1) * 128],
                                      src_bf[:, ic, tg * 512:(tg + 1) * 512],
                                      start=(ic == 0), stop=(ic == nic - 1))
                    ctx.inc(i, "p")

                @block.vector
                def _(v, oc=oc_, tg=tg, par=par):
                    ctx.wait(v, "p")
                    evac(v, ps[:, par, :], oc0 + oc, tg)
                    ctx.inc(v.drain(), "v")
    _fence(ctx)


def emit_load_stream(ctx, x_dram, xT, xbf=None):
    nc, block = ctx.nc, ctx.block

    @block.gpsimd
    def _(g):
        ctx.wait_all(g)
        for c in range(NCH):
            ctx.dma(g, xT[:, c, :], x_dram[c * 128:(c + 1) * 128, :])
        if xbf is not None:
            ctx.wait(g, "d")
            for c in range(NCH):
                g.tensor_copy(out=xbf[:, c, :], in_=xT[:, c, :])
            ctx.inc(g.drain(), "g")
    _fence(ctx)


def emit_refresh_xbf(ctx, xT, xbf):
    @ctx.block.gpsimd
    def _(g):
        ctx.wait_all(g)
        for c in range(NCH):
            g.tensor_copy(out=xbf[:, c, :], in_=xT[:, c, :])
        ctx.inc(g.drain(), "g")
    _fence(ctx)


def emit_store_stream(ctx, xT, y_dram):
    nc, block = ctx.nc, ctx.block

    @block.gpsimd
    def _(g):
        ctx.wait_all(g)
        for c in range(NCH):
            ctx.dma(g, y_dram[c * 128:(c + 1) * 128, :], xT[:, c, :], name="o")
        ctx.wait(g, "o")
    _fence(ctx)


def emit_rope(ctx, xro_dram, tabs, xrbf):
    nc, block = ctx.nc, ctx.block
    with ExitStack() as st:
        xe = st.enter_context(nc.sbuf_tensor(_u("rp_xe"), [128, 2048], F32))
        xo = st.enter_context(nc.sbuf_tensor(_u("rp_xo"), [128, 2048], F32))
        tt = {k: st.enter_context(nc.sbuf_tensor(_u("rp_" + k), [128, 2048], F32))
              for k in ("ch", "cl", "sh", "sl")}
        m1 = st.enter_context(nc.sbuf_tensor(_u("rp_m1"), [128, 2048], F32))
        m2 = st.enter_context(nc.sbuf_tensor(_u("rp_m2"), [128, 2048], F32))
        p1 = st.enter_context(nc.sbuf_tensor(_u("rp_p1"), [128, 2048], F32))
        p2 = st.enter_context(nc.sbuf_tensor(_u("rp_p2"), [128, 2048], F32))
        for c in range(4):
            @block.gpsimd
            def _(g, c=c):
                if c == 0:
                    ctx.wait_all(g)
                else:
                    ctx.wait(g, "v")
                ctx.dma(g, xe[:, :], xro_dram[c * 128:(c + 1) * 128, :])
                ctx.dma(g, xo[:, :], xro_dram[512 + c * 128:512 + (c + 1) * 128, :])
                for k, nm in (("ch", "coshi"), ("cl", "coslo"), ("sh", "sinhi"), ("sl", "sinlo")):
                    ctx.dma(g, tt[k][:, :], tabs[nm][c * 128:(c + 1) * 128, :])

            @block.vector
            def _(v, c=c):
                ctx.wait(v, "d")
                v.tensor_tensor(out=m1[:, :], in0=xe[:, :], in1=tt["ch"][:, :], op=ALU.mult)
                v.tensor_tensor(out=m2[:, :], in0=xo[:, :], in1=tt["sh"][:, :], op=ALU.mult)
                v.tensor_tensor(out=p1[:, :], in0=m1[:, :], in1=m2[:, :], op=ALU.subtract)
                v.tensor_tensor(out=m1[:, :], in0=xe[:, :], in1=tt["cl"][:, :], op=ALU.mult)
                v.tensor_tensor(out=m2[:, :], in0=xo[:, :], in1=tt["sl"][:, :], op=ALU.mult)
                v.tensor_tensor(out=p2[:, :], in0=m1[:, :], in1=m2[:, :], op=ALU.subtract)
                v.tensor_tensor(out=p1[:, :], in0=p1[:, :], in1=p2[:, :], op=ALU.add)
                v.tensor_copy(out=xrbf[:, c, :], in_=p1[:, :])
                v.tensor_tensor(out=m1[:, :], in0=xe[:, :], in1=tt["sh"][:, :], op=ALU.mult)
                v.tensor_tensor(out=m2[:, :], in0=xo[:, :], in1=tt["ch"][:, :], op=ALU.mult)
                v.tensor_tensor(out=p1[:, :], in0=m1[:, :], in1=m2[:, :], op=ALU.add)
                v.tensor_tensor(out=m1[:, :], in0=xe[:, :], in1=tt["sl"][:, :], op=ALU.mult)
                v.tensor_tensor(out=m2[:, :], in0=xo[:, :], in1=tt["cl"][:, :], op=ALU.mult)
                v.tensor_tensor(out=p2[:, :], in0=m1[:, :], in1=m2[:, :], op=ALU.add)
                v.tensor_tensor(out=p1[:, :], in0=p1[:, :], in1=p2[:, :], op=ALU.add)
                v.tensor_copy(out=xrbf[:, 4 + c, :], in_=p1[:, :])
                ctx.inc(v.drain(), "v")
    _fence(ctx)


def emit_layernorm_inplace(ctx, consts, xT):
    nc, block = ctx.nc, ctx.block
    with ExitStack() as st:
        sq = st.enter_context(nc.sbuf_tensor(_u("ln_sq"), [128, 1024], F32))
        stats = st.enter_context(nc.sbuf_tensor(_u("ln_stats"), [1, 4, 1024], F32))
        bc = st.enter_context(nc.sbuf_tensor(_u("ln_bc"), [128, 2, 1024], F32))
        ps_mu = st.enter_context(nc.psum_tensor(_u("ln_psmu"), [1, 1024], F32))
        ps_sq = st.enter_context(nc.psum_tensor(_u("ln_pssq"), [1, 1024], F32))
        ps_bc = st.enter_context(nc.psum_tensor(_u("ln_psbc"), [128, 2, 1024], F32))
        for th in range(2):
            t0 = th * 1024

            for c in range(NCH):
                @block.scalar
                def _(a, th=th, t0=t0, c=c):
                    if th == 0 and c == 0:
                        ctx.wait_all(a)
                    elif c == 0:
                        ctx.wait(a, "v")
                    else:
                        ctx.wait(a, "p")
                    i = a.activation(out=sq[:, :], in_=xT[:, c, t0:t0 + 1024],
                                     func=AF.Square, bias=0.0, scale=1.0)
                    ctx.inc(i, "a")

                @block.tensor
                def _(te, th=th, t0=t0, c=c):
                    ctx.wait(te, "a")
                    for g2 in range(2):
                        te.matmul(ps_mu[:, g2 * 512:(g2 + 1) * 512], consts["ones_col_f"][:, :],
                                  xT[:, c, t0 + g2 * 512:t0 + (g2 + 1) * 512],
                                  start=(c == 0), stop=(c == NCH - 1))
                        i = te.matmul(ps_sq[:, g2 * 512:(g2 + 1) * 512], consts["ones_col_f"][:, :],
                                      sq[:, g2 * 512:(g2 + 1) * 512],
                                      start=(c == 0), stop=(c == NCH - 1))
                    ctx.inc(i, "p")

            @block.vector
            def _(v, th=th):
                ctx.wait(v, "p")
                v.tensor_scalar_mul(stats[0:1, 0, :], ps_mu[:, :], 1.0 / D)
                v.tensor_scalar_mul(stats[0:1, 1, :], ps_sq[:, :], 1.0 / D)
                v.tensor_tensor(out=stats[0:1, 3, :], in0=stats[0:1, 0, :],
                                in1=stats[0:1, 0, :], op=ALU.mult)
                v.tensor_tensor(out=stats[0:1, 1, :], in0=stats[0:1, 1, :],
                                in1=stats[0:1, 3, :], op=ALU.subtract)
                ctx.inc(v.drain(), "v")

            @block.scalar
            def _(a, th=th):
                ctx.wait(a, "v")
                i = a.activation(out=stats[0:1, 3, :], in_=stats[0:1, 1, :],
                                 func=AF.Sqrt, bias=consts["eps"][0:1, 0:1], scale=1.0)
                ctx.inc(i, "a")

            @block.vector
            def _(v, th=th):
                ctx.wait(v, "a")
                v.reciprocal(out=stats[0:1, 1, :], in_=stats[0:1, 3, :])
                v.tensor_tensor(out=stats[0:1, 2, :], in0=stats[0:1, 0, :],
                                in1=stats[0:1, 1, :], op=ALU.mult)
                v.tensor_scalar_mul(stats[0:1, 2, :], stats[0:1, 2, :], -1.0)
                ctx.inc(v.drain(), "v")

            @block.tensor
            def _(te, th=th):
                ctx.wait(te, "v")
                for g2 in range(2):
                    te.matmul(ps_bc[:, 0, g2 * 512:(g2 + 1) * 512], consts["ones_row_f"][0:1, :],
                              stats[0:1, 1, g2 * 512:(g2 + 1) * 512], start=True, stop=True)
                    i = te.matmul(ps_bc[:, 1, g2 * 512:(g2 + 1) * 512], consts["ones_row_f"][0:1, :],
                                  stats[0:1, 2, g2 * 512:(g2 + 1) * 512], start=True, stop=True)
                ctx.inc(i, "p")

            @block.vector
            def _(v, th=th, t0=t0):
                ctx.wait(v, "p")
                v.tensor_copy(out=bc[:, 0, :], in_=ps_bc[:, 0, :])
                v.tensor_copy(out=bc[:, 1, :], in_=ps_bc[:, 1, :])
                for c in range(NCH):
                    v.tensor_tensor(out=xT[:, c, t0:t0 + 1024], in0=xT[:, c, t0:t0 + 1024],
                                    in1=bc[:, 0, :], op=ALU.mult)
                    v.tensor_tensor(out=xT[:, c, t0:t0 + 1024], in0=xT[:, c, t0:t0 + 1024],
                                    in1=bc[:, 1, :], op=ALU.add)
                ctx.inc(v.drain(), "v")
    _fence(ctx)


def emit_negm(ctx, consts, qt_bf, kt_bf, layer, negm_bc, scr_dram, tag):
    nc, block = ctx.nc, ctx.block
    kch = 4 if layer == 0 else 8
    with ExitStack() as st:
        mq = st.enter_context(nc.sbuf_tensor(_u(tag + "_mq"), [128, 8], F32))
        mk = st.enter_context(nc.sbuf_tensor(_u(tag + "_mk"), [128, 8], F32))
        flat = st.enter_context(nc.sbuf_tensor(_u(tag + "_flat"), [1, 2048], F32))
        mrow = st.enter_context(nc.sbuf_tensor(_u(tag + "_mrow"), [1, 3, 16], F32))
        ps = st.enter_context(nc.psum_tensor(_u(tag + "_ps"), [128, 16], F32))
        if "dump_negm" in DBG and tag == "l0nm":
            DBG_ATT2[0] = st.enter_context(nc.sbuf_tensor(_u("nm_dbg"), [128, 2128], F32))

        @block.vector
        def _(v):
            ctx.wait_all(v)
            v.memset(mk[:, :], 0.0)
            for c in range(NCH):
                v.reduce_max(out=mq[:, c:c + 1], in_=qt_bf[:, c, :], axis=AXL.X,
                             apply_absolute_value=True)
            for c in range(kch):
                v.reduce_max(out=mk[:, c:c + 1], in_=kt_bf[:, c, :], axis=AXL.X,
                             apply_absolute_value=True)
            ctx.inc(v.drain(), "v")

        @block.gpsimd
        def _(g):
            ctx.wait(g, "v")
            ctx.dma(g, scr_dram[0:128, :], mq[:, :])
            ctx.dma(g, scr_dram[128:256, :], mk[:, :])

        @block.gpsimd
        def _(g):
            ctx.wait(g, "d")
            ctx.dma(g, flat[0:1, :], scr_dram.rearrange("(o p) c -> o (p c)", o=1)[0:1, :])

        @block.vector
        def _(v):
            ctx.wait(v, "d")
            for h in range(16):
                st_q = 64 * (h % 2) * 8 + (h // 2)
                v.reduce_max(out=mrow[0:1, 0, h:h + 1],
                             in_=flat[0:1, st_q:st_q + 505:8], axis=AXL.X)
            for h in range(16):
                if layer == 0:
                    gk = h // 4
                    st_k = 1024 + gk
                    v.reduce_max(out=mrow[0:1, 1, h:h + 1],
                                 in_=flat[0:1, st_k:st_k + 505:8], axis=AXL.X)
                else:
                    st_k = 1024 + 64 * (h % 2) * 8 + (h // 2)
                    v.reduce_max(out=mrow[0:1, 1, h:h + 1],
                                 in_=flat[0:1, st_k:st_k + 505:8], axis=AXL.X)
            ctx.inc(v.drain(), "v")
            ctx.wait(v, "v")
            v.scalar_tensor_tensor(out=mrow[0:1, 2, :], in0=mrow[0:1, 0, :],
                                   scalar=-64.0, in1=mrow[0:1, 1, :],
                                   op0=ALU.mult, op1=ALU.mult)
            ctx.inc(v.drain(), "v")

        @block.tensor
        def _(te):
            ctx.wait(te, "v")
            i = te.matmul(ps[:, :], consts["ones_row_f"][0:1, :], mrow[0:1, 2, :],
                          start=True, stop=True)
            ctx.inc(i, "p")

        @block.vector
        def _(v):
            ctx.wait(v, "p")
            v.tensor_copy(out=negm_bc[:, :], in_=ps[:, :])
            if "dump_negm" in DBG and tag == "l0nm":
                v.tensor_copy(out=DBG_ATT2[0][:, 0:8], in_=mq[:, :])
                v.tensor_copy(out=DBG_ATT2[0][:, 8:16], in_=mk[:, :])
                v.tensor_copy(out=DBG_ATT2[0][0:1, 16:2064], in_=flat[0:1, :])
                v.tensor_copy(out=DBG_ATT2[0][0:1, 2064:2112], in_=mrow[0:1, :, :])
                v.tensor_copy(out=DBG_ATT2[0][:, 2112:2128], in_=negm_bc[:, :])
            ctx.inc(v.drain(), "v")

        if "dump_negm" in DBG and tag == "l0nm":
            @block.gpsimd
            def _(g):
                ctx.wait(g, "v")
                ctx.dma(g, DBG_DRAM2[0][:, :], DBG_ATT2[0][:, :], name="o")
                ctx.wait(g, "o")
    _fence(ctx)


def emit_attention(ctx, consts, layer, qt_bf, kt_bf, v_bf, negm_bc, ctx_bf):
    nc, block = ctx.nc, ctx.block
    with ExitStack() as st:
        E = st.enter_context(nc.sbuf_tensor(_u("at_E"), [128, 2048], BF16))
        W = st.enter_context(nc.sbuf_tensor(_u("at_W"), [128, 2048], BF16))
        PT = st.enter_context(nc.sbuf_tensor(_u("at_PT"), [128, 16, 128], BF16))
        m8 = st.enter_context(nc.sbuf_tensor(_u("at_m8"), [128, 8], F32))
        thr = st.enter_context(nc.sbuf_tensor(_u("at_thr"), [128, 1], F32))
        rs = st.enter_context(nc.sbuf_tensor(_u("at_rs"), [128, 2], F32))
        rc = st.enter_context(nc.sbuf_tensor(_u("at_rc"), [128, 1], F32))
        rm = st.enter_context(nc.sbuf_tensor(_u("at_rm"), [128, 3], F32))
        if "dump_att" in DBG:
            DBG_ATT[0] = st.enter_context(nc.sbuf_tensor(_u("at_dbg"), [128, 4200], F32))
        ps_s = st.enter_context(nc.psum_tensor(_u("at_pss"), [128, 2, 1024], F32))
        ps_c = st.enter_context(nc.psum_tensor(_u("at_psc"), [128, 128], F32))

        for h in range(16):
            grp = h // 4
            qch, qba = h // 2, 64 * (h % 2)
            for qi in range(NKC):
                live = (qi + 1) if layer == 0 else NKC
                for half in range(2):
                    k0 = half * 8
                    nliv = min(8, max(0, live - k0))
                    if nliv == 0:
                        continue

                    @block.tensor
                    def _(te, h=h, qi=qi, half=half, k0=k0, nliv=nliv, qch=qch,
                          qba=qba, grp=grp):
                        if h == 0 and qi == 0 and half == 0:
                            ctx.wait_all(te)
                        ctx.wait(te, "a")
                        lhsT = qt_bf[qba:qba + 64, qch, 128 * qi:128 * (qi + 1)]
                        if layer == 0:
                            rhs_t = kt_bf[qba:qba + 64, grp, :]
                        else:
                            rhs_t = kt_bf[qba:qba + 64, qch, :]
                        for nb in range((nliv * 128 + 511) // 512):
                            n0 = k0 * 128 + nb * 512
                            nn = min(512, (k0 + nliv) * 128 - n0)
                            i = te.matmul(ps_s[:, half, nb * 512:nb * 512 + nn],
                                          lhsT, rhs_t[:, n0:n0 + nn],
                                          start=True, stop=True,
                                          tile_position=(qba, 0))
                        ctx.inc(i, "p")

                    if layer == 0 and k0 <= qi < k0 + 8:
                        @block.vector
                        def _(v, qi=qi, half=half, k0=k0):
                            ctx.wait(v, "p")
                            cp = (qi - k0) * 128
                            v.tensor_tensor(out=ps_s[:, half, cp:cp + 128],
                                            in0=ps_s[:, half, cp:cp + 128],
                                            in1=consts["causal"][:, :], op=ALU.add)
                            ctx.inc(v.drain(), "v")

                    @block.vector
                    def _(v, half=half, nliv=nliv):
                        ctx.wait(v, "p")
                        i = v.reduce_max(out=rm[:, half:half + 1],
                                         in_=ps_s[:, half, 0:nliv * 128],
                                         axis=AXL.X, negate=True)
                        ctx.inc(i, "v")

                nhalf = 1 if live <= 8 else 2

                @block.vector
                def _(v, nhalf=nhalf):
                    ctx.wait(v, "v")
                    if nhalf == 2:
                        i = v.tensor_tensor(out=rm[:, 2:3], in0=rm[:, 0:1],
                                            in1=rm[:, 1:2], op=ALU.min)
                    else:
                        i = v.tensor_copy(out=rm[:, 2:3], in_=rm[:, 0:1])
                    ctx.inc(i, "v")

                for half in range(2):
                    k0 = half * 8
                    nliv = min(8, max(0, live - k0))
                    if nliv == 0:
                        continue

                    @block.scalar
                    def _(a, h=h, half=half, k0=k0, nliv=nliv):
                        ctx.wait(a, "p")
                        ctx.wait(a, "v")
                        ctx.wait(a, "t")
                        i = a.activation(out=E[:, k0 * 128:(k0 + nliv) * 128],
                                         in_=ps_s[:, half, 0:nliv * 128], func=AF.Exp,
                                         bias=rm[:, 2:3], scale=1.0,
                                         accum_out=rs[:, half:half + 1])
                        ctx.inc(i, "a")

                nlive = live * 128
                if layer == 1:
                    @block.gpsimd
                    def _(gp):
                        ctx.wait(gp, "a")
                        i = gp.tensor_copy(out=W[:, :], in_=E[:, :])
                        ctx.inc(i, "g")

                    @block.vector
                    def _(v, h=h, qi=qi):
                        ctx.wait(v, "g")
                        for rnd in range(5):
                            i = v.max(out=m8[:, :], in_=W[:, :])
                            ctx.inc(i, "v")
                            ctx.wait(v, "v")
                            if rnd < 4:
                                v.tensor_scalar(out=PT[:, :, :], in0=W[:, :],
                                                scalar1=m8[:, 7:8], scalar2=NBIG,
                                                op0=ALU.is_ge, op1=ALU.mult)
                                v.tensor_tensor(out=W[:, :], in0=W[:, :],
                                                in1=PT[:, :, :], op=ALU.add)
                        i = v.tensor_copy(out=thr[:, 0:1], in_=m8[:, 5:6])
                        ctx.inc(i, "v")
                        ctx.wait(v, "v")
                        i = v.scalar_tensor_tensor(out=E[:, :], in0=E[:, :],
                                                   scalar=thr[:, 0:1], in1=E[:, :],
                                                   op0=ALU.is_ge, op1=ALU.mult,
                                                   accum_out=rs[:, 0:1])
                        ctx.inc(i, "v")
                        ctx.wait(v, "v")
                        i = v.reciprocal(out=rc[:, 0:1], in_=rs[:, 0:1])
                        ctx.inc(i, "v")
                        ctx.wait(v, "v")
                        v.tensor_scalar_mul(E[:, :], E[:, :], rc[:, 0:1])
                        ctx.inc(v.drain(), "v")
                else:
                    @block.vector
                    def _(v, h=h, qi=qi, nlive=nlive, live=live):
                        ctx.wait(v, "a")
                        if live > 8:
                            i = v.tensor_tensor(out=rs[:, 0:1], in0=rs[:, 0:1],
                                                in1=rs[:, 1:2], op=ALU.add)
                            ctx.inc(i, "v")
                            ctx.wait(v, "v")
                        i = v.reciprocal(out=rc[:, 0:1], in_=rs[:, 0:1])
                        ctx.inc(i, "v")
                        ctx.wait(v, "v")
                        if "dump_att" in DBG and h == 0 and qi == 2:
                            v.tensor_copy(out=DBG_ATT[0][:, 0:nlive], in_=E[:, 0:nlive])
                            v.tensor_copy(out=DBG_ATT[0][:, 2048:2049], in_=rs[:, 0:1])
                            v.tensor_copy(out=DBG_ATT[0][:, 2049:2050], in_=rc[:, 0:1])
                        v.tensor_scalar_mul(E[:, 0:nlive], E[:, 0:nlive], rc[:, 0:1])
                        ctx.inc(v.drain(), "v")

                @block.scalar
                def _(a, live=live):
                    ctx.wait(a, "v")
                    ctx.wait(a, "p")
                    ctx.wait(a, "t")
                    i = a.dma_start(out=PT[:, 0:live, :],
                                    in_=E[:, 0:live * 128], transpose=True)
                    ctx.inc(i, "t", 16)

                @block.tensor
                def _(te, h=h, qi=qi, grp=grp, live=live, qba=qba):
                    ctx.wait(te, "t")
                    ctx.wait(te, "v")
                    for c in range(live):
                        if layer == 0:
                            va = v_bf[:, c, 64 * grp:64 * grp + 64]
                        else:
                            va = v_bf[:, c, 64 * h:64 * h + 64]
                        i = te.matmul(ps_c[qba:qba + 64, :], va, PT[:, c, :],
                                      start=(c == 0), stop=(c == live - 1),
                                      tile_position=(0, qba), skip_group_check=True)
                    ctx.inc(i, "p")

                @block.vector
                def _(v, h=h, qi=qi, qba=qba):
                    ctx.wait(v, "p")
                    v.tensor_copy(out=ctx_bf[qba:qba + 64, h // 2, 128 * qi:128 * (qi + 1)],
                                  in_=ps_c[qba:qba + 64, :])
                    ctx.inc(v.drain(), "v")
        if "dump_att" in DBG:
            @block.gpsimd
            def _(g):
                ctx.wait_all(g)
                ctx.dma(g, DBG_DRAM[0][:, :], DBG_ATT[0][:, :], name="o")
                ctx.wait(g, "o")
        if "dump_ctx" in DBG:
            with nc.sbuf_tensor(_u("dmpc"), [128, 2048], F32) as dmpc:
                for c in range(NCH):
                    @block.gpsimd
                    def _(g, c=c):
                        ctx.wait_all(g)
                        i = g.tensor_copy(out=dmpc[:, :], in_=ctx_bf[:, c, :])
                        ctx.inc(i, "g")
                        ctx.wait(g, "g")
                        ctx.dma(g, DBG_DRAM[0][c * 128:(c + 1) * 128, 0:2048],
                                dmpc[:, :], name="o")
                        ctx.wait(g, "o")
    _fence(ctx)


def emit_l0_attn(ctx, consts, wd, tabs, xT, scr_dram):
    nc, block = ctx.nc, ctx.block
    with ExitStack() as st:
        xr_bf = st.enter_context(nc.sbuf_tensor(_u("l0_xr"), [128, 8, 2048], BF16))
        emit_rope(ctx, wd["xro"], tabs, xr_bf)

        qt_bf = st.enter_context(nc.sbuf_tensor(_u("l0_qt"), [128, 8, 2048], BF16))
        ktd_bf = st.enter_context(nc.sbuf_tensor(_u("l0_kt"), [128, 4, 2048], BF16))
        v_bf = st.enter_context(nc.sbuf_tensor(_u("l0_v"), [128, 16, 256], BF16))

        with ExitStack() as st2:
            saved, ctx.stack = ctx.stack, st2
            wq = load_weight_bf16(ctx, wd["l0wq"], 1024, 0, 1024, "l0wq")

            def evq(v, psap, oc, tg):
                v.tensor_copy(out=qt_bf[:, oc, tg * 512:(tg + 1) * 512], in_=psap)
            emit_proj_featmajor(ctx, wq, 1024, 1024, xr_bf, evq, "l0psq")
            ctx.stack = saved

        with ExitStack() as st2:
            saved, ctx.stack = ctx.stack, st2
            wk = load_weight_bf16(ctx, wd["l0wk"], 1024, 0, 256, "l0wk")

            def evk(v, psap, oc, tg):
                for gg in range(2):
                    gidx = 2 * oc + gg
                    for hf in range(2):
                        v.tensor_scalar(out=ktd_bf[64 * hf:64 * hf + 64, gidx,
                                                   tg * 512:(tg + 1) * 512],
                                        in0=psap[64 * gg:64 * gg + 64, :], scalar1=0.125,
                                        scalar2=None, op0=ALU.mult)
            emit_proj_featmajor(ctx, wk, 1024, 256, xr_bf, evk, "l0psk")
            ctx.stack = saved

        with ExitStack() as st2:
            saved, ctx.stack = ctx.stack, st2
            wv = load_weight_bf16(ctx, wd["l0wv"], 1024, 0, 256, "l0wv")
            with nc.psum_tensor(_u("l0psv"), [128, 2, 512], F32) as psv:
                for tc in range(NKC):
                    par = tc % 2

                    @block.tensor
                    def _(te, tc=tc, par=par):
                        if tc == 0:
                            ctx.wait_all(te)
                        if tc >= 2:
                            ctx.wait(te, "v")
                        for ic in range(NCH):
                            i = te.matmul(psv[:, par, 0:256],
                                          xr_bf[:, ic, tc * 128:(tc + 1) * 128],
                                          wv[:, ic, :], start=(ic == 0), stop=(ic == NCH - 1))
                        ctx.inc(i, "p")

                    @block.vector
                    def _(v, tc=tc, par=par):
                        ctx.wait(v, "p")
                        v.tensor_copy(out=v_bf[:, tc, :], in_=psv[:, par, 0:256])
                        ctx.inc(v.drain(), "v")
            _fence(ctx)
            ctx.stack = saved

        emit_attention(ctx, consts, 0, qt_bf, ktd_bf, v_bf, None, xr_bf)

        with ExitStack() as st2:
            saved, ctx.stack = ctx.stack, st2
            wo = load_weight_bf16(ctx, wd["l0wo"], 1024, 0, 1024, "l0wo")

            def evo(v, psap, oc, tg):
                v.tensor_tensor(out=xT[:, oc, tg * 512:(tg + 1) * 512],
                                in0=psap, in1=xT[:, oc, tg * 512:(tg + 1) * 512], op=ALU.add)
            emit_proj_featmajor(ctx, wo, 1024, 1024, xr_bf, evo, "l0pso")
            ctx.stack = saved
    emit_layernorm_inplace(ctx, consts, xT)


def emit_l0_ffn(ctx, consts, wd, xT):
    nc, block = ctx.nc, ctx.block
    with ExitStack() as st:
        q_bf = st.enter_context(nc.sbuf_tensor(_u("f0_q"), [128, 8, 2048], BF16))
        sc2 = st.enter_context(nc.sbuf_tensor(_u("f0_sc2"), [128, 2], F32))
        scs = st.enter_context(nc.sbuf_tensor(_u("f0_scs"), [1, 4], F32))
        bias = st.enter_context(nc.sbuf_tensor(_u("f0_bias"), [128, 8], F32))

        @block.gpsimd
        def _(g):
            ctx.wait_all(g)
            ctx.dma(g, scs[0:1, 0:2], wd["l0sc"][0:1, 0:2])
            ctx.dma(g, bias[:, :], wd["l0fb"][:, :])

        @block.vector
        def _(v):
            ctx.wait(v, "d")
            i = v.reciprocal(out=scs[0:1, 2:3], in_=scs[0:1, 0:1])
            ctx.inc(i, "v")
            ctx.wait(v, "v")
            v.tensor_tensor(out=scs[0:1, 3:4], in0=scs[0:1, 0:1],
                            in1=scs[0:1, 1:2], op=ALU.mult)
            ctx.inc(v.drain(), "v")

        with nc.psum_tensor(_u("f0psb"), [128, 2], F32) as psb:
            @block.tensor
            def _(te):
                ctx.wait(te, "v")
                i = te.matmul(psb[:, :], consts["ones_row_f"][0:1, :], scs[0:1, 2:4],
                              start=True, stop=True)
                ctx.inc(i, "p")

            @block.vector
            def _(v):
                ctx.wait(v, "p")
                v.tensor_copy(out=sc2[:, :], in_=psb[:, :])
                ctx.inc(v.drain(), "v")
        _fence(ctx)

        with ExitStack() as stq:
            t = stq.enter_context(nc.sbuf_tensor(_u("f0_t"), [128, 2048], F32))
            sg = stq.enter_context(nc.sbuf_tensor(_u("f0_sg"), [128, 2048], F32))
            aa = stq.enter_context(nc.sbuf_tensor(_u("f0_a"), [128, 2048], F32))
            ib = stq.enter_context(nc.sbuf_tensor(_u("f0_ib"), [128, 2048], I32))
            sp = stq.enter_context(nc.sbuf_tensor(_u("f0_sp"), [128, 2048], F32))
            uu = stq.enter_context(nc.sbuf_tensor(_u("f0_u"), [128, 2048], F32))
            vv = stq.enter_context(nc.sbuf_tensor(_u("f0_v"), [128, 2048], F32))
            bb = stq.enter_context(nc.sbuf_tensor(_u("f0_b"), [128, 2048], F32))
            for c in range(NCH):
                @block.vector
                def _(v, c=c):
                    if c == 0:
                        ctx.wait_all(v)
                    else:
                        ctx.wait(v, "g")
                    v.tensor_scalar(out=t[:, :], in0=xT[:, c, :], scalar1=sc2[:, 0:1],
                                    scalar2=448.0, op0=ALU.mult, op1=ALU.min)
                    v.tensor_scalar_max(t[:, :], t[:, :], -448.0)
                    ctx.inc(v.drain(), "v")

                @block.scalar
                def _(a, c=c):
                    ctx.wait(a, "v")
                    ctx.wait(a, "g")
                    a.activation(out=sg[:, :], in_=t[:, :], func=AF.Sign, bias=0.0, scale=1.0)
                    i = a.activation(out=aa[:, :], in_=t[:, :], func=AF.Abs, bias=0.0, scale=1.0)
                    ctx.inc(i, "a")

                @block.vector
                def _(v, c=c):
                    ctx.wait(v, "a")
                    v.tensor_scalar(out=ib[:, :], in0=aa[:, :].bitcast(I32),
                                    scalar1=consts["expmask"][:, 0:1], scalar2=None,
                                    op0=ALU.bitwise_and)
                    v.tensor_scalar(out=sp[:, :], in0=ib[:, :].bitcast(F32),
                                    scalar1=0.125, scalar2=2.0 ** -9,
                                    op0=ALU.mult, op1=ALU.max)
                    v.reciprocal(out=uu[:, :], in_=sp[:, :])
                    v.tensor_tensor(out=uu[:, :], in0=aa[:, :], in1=uu[:, :], op=ALU.mult)
                    v.tensor_scalar_min(uu[:, :], uu[:, :], 16.0)
                    v.tensor_scalar(out=vv[:, :], in0=uu[:, :], scalar1=0.5,
                                    scalar2=None, op0=ALU.add)
                    for cbit in (8.0, 8.0, 4.0, 2.0, 1.0):
                        v.tensor_scalar(out=bb[:, :], in0=vv[:, :], scalar1=cbit,
                                        scalar2=None, op0=ALU.is_ge)
                        v.scalar_tensor_tensor(out=vv[:, :], in0=bb[:, :], scalar=-cbit,
                                               in1=vv[:, :], op0=ALU.mult, op1=ALU.add)
                    v.tensor_tensor(out=uu[:, :], in0=uu[:, :], in1=vv[:, :], op=ALU.subtract)
                    v.tensor_scalar(out=uu[:, :], in0=uu[:, :], scalar1=0.5,
                                    scalar2=None, op0=ALU.add)
                    v.tensor_tensor(out=uu[:, :], in0=uu[:, :], in1=sp[:, :], op=ALU.mult)
                    ctx.inc(v.drain(), "v")

                @block.gpsimd
                def _(g, c=c):
                    ctx.wait(g, "v")
                    g.tensor_tensor(out=q_bf[:, c, :], in0=uu[:, :], in1=sg[:, :], op=ALU.mult)
                    ctx.inc(g.drain(), "g")
        _fence(ctx)

        with ExitStack() as st2:
            saved, ctx.stack = ctx.stack, st2
            fw = load_weight_bf16(ctx, wd["l0fw"], 1024, 0, 1024, "l0fw")
            with nc.sbuf_tensor(_u("f0_ev"), [128, 512], F32) as ev:
                def evf(v, psap, oc, tg):
                    v.tensor_scalar(out=ev[:, :], in0=psap, scalar1=sc2[:, 1:2],
                                    scalar2=bias[:, oc:oc + 1], op0=ALU.mult, op1=ALU.add)
                    v.tensor_tensor(out=xT[:, oc, tg * 512:(tg + 1) * 512],
                                    in0=ev[:, :], in1=xT[:, oc, tg * 512:(tg + 1) * 512],
                                    op=ALU.add)
                emit_proj_featmajor(ctx, fw, 1024, 1024, q_bf, evf, "f0ps")
            ctx.stack = saved
    emit_layernorm_inplace(ctx, consts, xT)


def emit_l1_attn(ctx, consts, wd, xT, xbf, scr_dram):
    nc, block = ctx.nc, ctx.block
    with ExitStack() as st:
        qt_bf = st.enter_context(nc.sbuf_tensor(_u("l1_qt"), [128, 8, 2048], BF16))
        kt_bf = st.enter_context(nc.sbuf_tensor(_u("l1_kt"), [128, 8, 2048], BF16))
        v_bf = st.enter_context(nc.sbuf_tensor(_u("l1_v"), [128, 16, 1024], BF16))
        bq = st.enter_context(nc.sbuf_tensor(_u("l1_bq"), [128, 8], F32))
        bk = st.enter_context(nc.sbuf_tensor(_u("l1_bk"), [128, 8], F32))
        bo = st.enter_context(nc.sbuf_tensor(_u("l1_bo"), [128, 8], F32))
        bv_scope = ExitStack()
        bv_bc = bv_scope.enter_context(nc.sbuf_tensor(_u("l1_bvbc"), [128, 1024], BF16))
        bvr_scope = ExitStack()
        bvr = bvr_scope.enter_context(nc.sbuf_tensor(_u("l1_bvr"), [1, 1024], F32))

        @block.gpsimd
        def _(g):
            ctx.wait_all(g)
            ctx.dma(g, bq[:, :], wd["l1bq"][:, :])
            ctx.dma(g, bk[:, :], wd["l1bk"][:, :])
            ctx.dma(g, bo[:, :], wd["l1bo"][:, :])
            ctx.dma(g, bvr[0:1, :], wd["l1bv_row"].rearrange("(o a) b -> o (a b)", o=1)[0:1, :])

        with nc.psum_tensor(_u("l1psbv"), [128, 2, 512], F32) as psbv:
            @block.tensor
            def _(te):
                ctx.wait(te, "d")
                for half in range(2):
                    i = te.matmul(psbv[:, half, :], consts["ones_row_f"][0:1, :],
                                  bvr[0:1, half * 512:(half + 1) * 512],
                                  start=True, stop=True)
                ctx.inc(i, "p")

            @block.vector
            def _(v):
                ctx.wait(v, "p")
                v.tensor_copy(out=bv_bc[:, 0:512], in_=psbv[:, 0, :])
                v.tensor_copy(out=bv_bc[:, 512:1024], in_=psbv[:, 1, :])
                ctx.inc(v.drain(), "v")
        _fence(ctx)
        bvr_scope.close()

        def evq(v, psap, oc, tg):
            v.tensor_scalar(out=qt_bf[:, oc, tg * 512:(tg + 1) * 512], in0=psap,
                            scalar1=bq[:, oc:oc + 1], scalar2=None, op0=ALU.add)
        for half in range(2):
            with ExitStack() as st2:
                saved, ctx.stack = ctx.stack, st2
                wq = load_weight_bf16(ctx, wd["l1wq"], 1024, half * 512,
                                      (half + 1) * 512, "l1wq")
                emit_proj_featmajor(ctx, wq, 1024, 512, xbf, evq, "l1psq", oc0=half * 4)
                ctx.stack = saved

        def evk(v, psap, oc, tg):
            v.tensor_scalar(out=kt_bf[:, oc, tg * 512:(tg + 1) * 512], in0=psap,
                            scalar1=bk[:, oc:oc + 1], scalar2=0.125,
                            op0=ALU.add, op1=ALU.mult)
        for half in range(2):
            with ExitStack() as st2:
                saved, ctx.stack = ctx.stack, st2
                wk = load_weight_bf16(ctx, wd["l1wk"], 1024, half * 512,
                                      (half + 1) * 512, "l1wk")
                emit_proj_featmajor(ctx, wk, 1024, 512, xbf, evk, "l1psk", oc0=half * 4)
                ctx.stack = saved

        for vhalf in range(2):
          with ExitStack() as st2:
            saved, ctx.stack = ctx.stack, st2
            wv = load_weight_bf16(ctx, wd["l1wv"], 1024, vhalf * 512,
                                  (vhalf + 1) * 512, "l1wv")
            with nc.psum_tensor(_u("l1psv"), [128, 2, 512], F32) as psv:
                for tc in range(NKC):
                    for vh in [vhalf]:
                        par = tc % 2

                        @block.tensor
                        def _(te, tc=tc, vh=vh, par=par):
                            if tc == 0:
                                ctx.wait_all(te)
                            else:
                                ctx.wait(te, "v")
                            for ic in range(NCH):
                                i = te.matmul(psv[:, par, :],
                                              xbf[:, ic, tc * 128:(tc + 1) * 128],
                                              wv[:, ic, :],
                                              start=(ic == 0), stop=(ic == NCH - 1))
                            ctx.inc(i, "p")

                        @block.vector
                        def _(v, tc=tc, vh=vh, par=par):
                            ctx.wait(v, "p")
                            v.tensor_tensor(out=v_bf[:, tc, vh * 512:(vh + 1) * 512],
                                            in0=psv[:, par, :],
                                            in1=bv_bc[:, vh * 512:(vh + 1) * 512], op=ALU.add)
                            ctx.inc(v.drain(), "v")
            _fence(ctx)
            ctx.stack = saved

        bv_scope.close()
        emit_attention(ctx, consts, 1, qt_bf, kt_bf, v_bf, None, xbf)

        for half in range(2):
            with ExitStack() as st2:
                saved, ctx.stack = ctx.stack, st2
                wo = load_weight_bf16(ctx, wd["l1wo"], 1024, half * 512,
                                      (half + 1) * 512, "l1wo")
                with nc.sbuf_tensor(_u("l1_ev"), [128, 512], F32) as ev:
                    def evo(v, psap, oc, tg):
                        v.tensor_scalar(out=ev[:, :], in0=psap, scalar1=bo[:, oc:oc + 1],
                                        scalar2=None, op0=ALU.add)
                        v.tensor_tensor(out=xT[:, oc, tg * 512:(tg + 1) * 512],
                                        in0=ev[:, :],
                                        in1=xT[:, oc, tg * 512:(tg + 1) * 512],
                                        op=ALU.add)
                    emit_proj_featmajor(ctx, wo, 1024, 512, xbf, evo, "l1pso",
                                        oc0=half * 4)
                ctx.stack = saved
    emit_layernorm_inplace(ctx, consts, xT)


def emit_l1_ffn(ctx, consts, wd, xT, xbf, scr2):
    nc, block = ctx.nc, ctx.block
    with ExitStack() as st:
        b1 = st.enter_context(nc.sbuf_tensor(_u("f1_b1"), [128, 32], F32))
        b2 = st.enter_context(nc.sbuf_tensor(_u("f1_b2"), [128, 8], F32))
        scl = st.enter_context(nc.sbuf_tensor(_u("f1_scl"), [1, 1], F32))
        se = st.enter_context(nc.sbuf_tensor(_u("f1_se"), [1, 8, 1024], BF16))
        se2q = st.enter_context(nc.sbuf_tensor(_u("f1_se2q"), [1, 16, 256], BF16))
        p_sb = st.enter_context(nc.sbuf_tensor(_u("f1_p"), [1, 4, 2048], BF16))
        r_sb = st.enter_context(nc.sbuf_tensor(_u("f1_r"), [1, 16, 512], BF16))
        hT = st.enter_context(nc.sbuf_tensor(_u("f1_h"), [128, 32, 512], BF16))

        @block.gpsimd
        def _(g):
            ctx.wait_all(g)
            ctx.dma(g, b1[:, :], wd["l1b1"][:, :])
            ctx.dma(g, b2[:, :], wd["l1b2"][:, :])
            ctx.dma(g, scl[0:1, 0:1], wd["l1sc"][0:1, 0:1])

        do_prep = "noprep" not in DBG
        lt_scope = ExitStack()
        lt = lt_scope.enter_context(nc.sbuf_tensor(_u("f1_lt"), [1, 8, 1024], BF16))
        with nc.sbuf_tensor(_u("f1_ptt"), [128, 128], BF16) as ptt:
          if do_prep:
              for c in range(NCH):
                  @block.scalar
                  def _(a, c=c):
                      if c == 0:
                          ctx.wait_all(a)
                      else:
                          ctx.wait(a, "d")
                      i = a.dma_start(out=ptt[:, :], in_=xbf[:, c, 0:128], transpose=True)
                      ctx.inc(i, "t", 16)

                  @block.gpsimd
                  def _(g, c=c):
                      ctx.wait(g, "t")
                      ctx.dma(g, scr2[:, c * 128:(c + 1) * 128], ptt[1:17, :])

              @block.gpsimd
              def _(g):
                  ctx.wait(g, "d")
                  i = g.dma_start(out=lt[0:1, :, :],
                                  in_=scr2.rearrange("(o r) p -> o r p", o=1)[0:1, 0:16:2, :])
                  ctx.inc(i, "d", 16)
          _fence(ctx)

        @block.vector
        def _(v):
            if not do_prep:
                ctx.inc(v.drain(), "v")
                return
            ctx.wait(v, "v")
            for i8 in range(8):
                v.tensor_scalar_mul(se[0:1, i8, :], lt[0:1, i8, :], scl[0:1, 0:1])
            ctx.inc(v.drain(), "v")
            ctx.wait(v, "v")
            for i4 in range(4):
                for q4 in range(4):
                    for half in range(2):
                        v.tensor_copy(out=se2q[0:1, i4 * 4 + q4, half * 128:(half + 1) * 128],
                                      in_=se[0:1, 4 + i4, half * 512 + q4:(half + 1) * 512:4])
            ctx.inc(v.drain(), "v")
        _fence(ctx)
        lt_scope.close()

        with nc.psum_tensor(_u("f1psp"), [1, 4, 512], F32) as psp:
            for tg in (range(NTG) if do_prep else []):
                @block.tensor
                def _(te, tg=tg):
                    if tg == 0:
                        ctx.wait_all(te)
                    else:
                        ctx.wait(te, "v")
                    for i4 in range(4):
                        for ic in range(NCH):
                            i = te.matmul(psp[:, i4, :],
                                          xbf[:, ic, 2 + 2 * i4:3 + 2 * i4],
                                          xbf[:, ic, tg * 512:(tg + 1) * 512],
                                          start=(ic == 0), stop=(ic == NCH - 1))
                    ctx.inc(i, "p")

                @block.vector
                def _(v, tg=tg):
                    ctx.wait(v, "p")
                    v.tensor_copy(out=p_sb[0:1, :, tg * 512:(tg + 1) * 512], in_=psp[0:1, :, :])
                    ctx.inc(v.drain(), "v")
        _fence(ctx)

        for tg in range(NTG):
            for hs in range(4):
                with ExitStack() as st2:
                    saved, ctx.stack = ctx.stack, st2
                    w1 = load_weight_bf16(ctx, wd["l1w1"], 1024, hs * 1024, (hs + 1) * 1024,
                                          "f1w1")
                    with nc.psum_tensor(_u("f1ps1"), [128, 2, 512], F32) as ps1:
                        for hb8 in range(8):
                            hb = hs * 8 + hb8
                            par = hb8 % 2

                            @block.tensor
                            def _(te, tg=tg, hs=hs, hb8=hb8, hb=hb, par=par):
                                if hb8 == 0:
                                    ctx.wait_all(te)
                                if hb8 >= 2:
                                    ctx.wait(te, "a")
                                for ic in range(NCH):
                                    i = te.matmul(ps1[:, par, :],
                                              w1[:, ic, hb8 * 128:(hb8 + 1) * 128],
                                              xbf[:, ic, tg * 512:(tg + 1) * 512],
                                              start=(ic == 0),
                                              stop=(ic == NCH - 1 and "nolora" in DBG))
                                if "nolora" not in DBG:
                                    i4 = hb // 8
                                    jsl = (hb % 8) * 128
                                    i = te.matmul(ps1[:, par, :], se[0:1, i4, jsl:jsl + 128],
                                                  p_sb[0:1, i4, tg * 512:(tg + 1) * 512],
                                                  start=False, stop=True)
                                ctx.inc(i, "p")

                            @block.scalar
                            def _(a, hb=hb, par=par):
                                ctx.wait(a, "p")
                                i = a.activation(out=hT[:, hb, :], in_=ps1[:, par, :],
                                                 func=AF.Gelu, bias=b1[:, hb:hb + 1],
                                                 scale=1.0)
                                ctx.inc(i, "a")
                    _fence(ctx)
                    ctx.stack = saved

            if "dumph" in DBG and tg == 0:
                with nc.sbuf_tensor(_u("dmp"), [128, 512], F32) as dmp:
                    for hc in range(32):
                        @block.gpsimd
                        def _(g, hc=hc):
                            ctx.wait_all(g)
                            i = g.tensor_copy(out=dmp[:, :], in_=hT[:, hc, :])
                            ctx.inc(i, "g")
                            ctx.wait(g, "g")
                            ctx.dma(g, DBG_DUMP[0][(hc % 8) * 128:(hc % 8 + 1) * 128,
                                                   (hc // 8) * 512:(hc // 8 + 1) * 512],
                                    dmp[:, :], name="o")
                            ctx.wait(g, "o")
                _fence(ctx)

            for i4h in (range(2) if do_prep else []):
                with nc.psum_tensor(_u("f1psr"), [1, 8, 512], F32) as psr:
                    @block.tensor
                    def _(te, tg=tg, i4h=i4h):
                        ctx.wait_all(te)
                        for i4x in range(2):
                            i4 = 2 * i4h + i4x
                            for q4 in range(4):
                                for kc in range(8):
                                    i = te.matmul(psr[:, i4x * 4 + q4, :],
                                                  xbf[:, kc, 10 + 2 * i4:11 + 2 * i4],
                                                  hT[:, q4 * 8 + kc, :],
                                                  start=(kc == 0), stop=(kc == 7))
                        ctx.inc(i, "p")

                    @block.vector
                    def _(v, tg=tg, i4h=i4h):
                        ctx.wait(v, "p")
                        v.tensor_copy(out=r_sb[0:1, i4h * 8:(i4h + 1) * 8, :],
                                      in_=psr[0:1, :, :])
                        ctx.inc(v.drain(), "v")
                _fence(ctx)

            with nc.sbuf_tensor(_u("f1_ev2"), [128, 512], F32) as ev2, \
                 nc.psum_tensor(_u("f1ps2"), [128, 2, 512], F32) as ps2:
                for oc in range(NCH):
                    par = oc % 2
                    i4 = oc // 2
                    with ExitStack() as st2:
                        saved, ctx.stack = ctx.stack, st2
                        w2 = load_weight_bf16(ctx, wd["l1w2"], 4096, oc * 128,
                                              (oc + 1) * 128, "f1w2")

                        @block.tensor
                        def _(te, tg=tg, oc=oc, par=par, i4=i4, w2=w2):
                            ctx.wait_all(te)
                            for hc in range(32):
                                i = te.matmul(ps2[:, par, :], w2[:, hc, :],
                                          hT[:, hc, :], start=(hc == 0),
                                          stop=(hc == 31 and "nolora" in DBG))
                            if "nolora" not in DBG:
                                for q4 in range(4):
                                    i = te.matmul(ps2[:, par, :],
                                                  se2q[0:1, i4 * 4 + q4,
                                                       (oc % 2) * 128:(oc % 2) * 128 + 128],
                                                  r_sb[0:1, i4 * 4 + q4, :],
                                                  start=False, stop=(q4 == 3))
                            ctx.inc(i, "p")

                        @block.vector
                        def _(v, tg=tg, oc=oc, par=par):
                            ctx.wait(v, "p")
                            v.tensor_scalar(out=ev2[:, :], in0=ps2[:, par, :],
                                            scalar1=b2[:, oc:oc + 1], scalar2=None,
                                            op0=ALU.add)
                            v.tensor_tensor(out=xT[:, oc, tg * 512:(tg + 1) * 512],
                                            in0=ev2[:, :],
                                            in1=xT[:, oc, tg * 512:(tg + 1) * 512],
                                            op=ALU.add)
                            ctx.inc(v.drain(), "v")
                        _fence(ctx)
                        ctx.stack = saved
    if "noln" not in DBG:
        emit_layernorm_inplace(ctx, consts, xT)


def build_program(stages):
    nc = bass.Bass("TRN2", target_bir_lowering=False, detect_race_conditions=False)
    cts = _make_consts(nc)
    wd = {}

    def din(nm, shp):
        wd[nm] = nc.dram_tensor(nm, shp, F32, kind="ExternalInput").ap()

    if "l0a" in stages:
        din("l0wq", [1024, 1024]); din("l0wk", [1024, 256])
        din("l0wv", [1024, 256]); din("l0wo", [1024, 1024])
    if "l0f" in stages:
        din("l0fw", [1024, 1024]); din("l0fb", [128, 8]); din("l0sc", [1, 2])
    if "l1a" in stages:
        for nm in ("l1wq", "l1wk", "l1wv", "l1wo"):
            din(nm, [1024, 1024])
        din("l1bq", [128, 8]); din("l1bk", [128, 8])
        din("l1bv_row", [8, 128]); din("l1bo", [128, 8])
    if "l1f" in stages:
        din("l1w1", [1024, 4096]); din("l1b1", [128, 32])
        din("l1w2", [4096, 1024]); din("l1b2", [128, 8]); din("l1sc", [1, 1])
    tabs = {}
    if "l0a" in stages:
        for nm in ("coshi", "coslo", "sinhi", "sinlo"):
            tabs[nm] = nc.dram_tensor(nm, [512, 2048], F32, kind="ExternalInput").ap()
    xin, xout, xro = [], [], []
    for j in range(BPC):
        xin.append(nc.dram_tensor(f"xin{j}", [1024, 2048], F32, kind="ExternalInput").ap())
        xout.append(nc.dram_tensor(f"xout{j}", [1024, 2048], F32, kind="ExternalOutput").ap())
        if "l0a" in stages:
            xro.append(nc.dram_tensor(f"xro{j}", [1024, 2048], F32, kind="ExternalInput").ap())
    scr = nc.dram_tensor("scr", [256, 8], F32, kind="Internal").ap()
    if "dump_att" in DBG:
        DBG_DRAM[0] = nc.dram_tensor("dbgatt", [128, 4200], F32, kind="ExternalOutput").ap()
    if "dump_ctx" in DBG:
        DBG_DRAM[0] = nc.dram_tensor("dbgatt", [1024, 2048], F32, kind="ExternalOutput").ap()
    if "dump_negm" in DBG:
        DBG_DRAM2[0] = nc.dram_tensor("dbgnegm", [128, 2128], F32, kind="ExternalOutput").ap()
    if "dumph" in DBG:
        DBG_DUMP[0] = nc.dram_tensor("dbgout", [1024, 2048], F32, kind="ExternalOutput").ap()
    scr2 = nc.dram_tensor("scr2", [16, 1024], BF16, kind="Internal").ap()

    need_bf = ("l1a" in stages) or ("l1f" in stages)
    with ExitStack() as stack:
        blk = stack.enter_context(nc.Block())
        ctx = Ctx(nc, blk, stack)
        consts = _load_consts(ctx, cts)
        _fence(ctx)
        xT = stack.enter_context(nc.sbuf_tensor(_u("xT"), [128, 8, 2048], F32))
        for j in range(BPC):
            with ExitStack() as bst:
                emit_load_stream(ctx, xin[j], xT, None)
                if "l0a" in stages:
                    wd2 = dict(wd)
                    wd2["xro"] = xro[j]
                    emit_l0_attn(ctx, consts, wd2, tabs, xT, scr)
                if "l0f" in stages:
                    emit_l0_ffn(ctx, consts, wd, xT)
                xbf = None
                if need_bf:
                    xbf = bst.enter_context(nc.sbuf_tensor(_u("xbf"), [128, 8, 2048], BF16))
                if "l1a" in stages:
                    emit_refresh_xbf(ctx, xT, xbf)
                    emit_l1_attn(ctx, consts, wd, xT, xbf, scr)
                if "l1f" in stages:
                    emit_refresh_xbf(ctx, xT, xbf)
                    emit_l1_ffn(ctx, consts, wd, xT, xbf, scr2)
                emit_store_stream(ctx, xT, xout[j])
    return nc


def _host_prep(inputs):
    d = {}
    d["l0wq"] = inputs["L0_attn_W_q"].T
    d["l0wk"] = inputs["L0_attn_W_k"].T
    d["l0wv"] = inputs["L0_attn_W_v"].T
    d["l0wo"] = inputs["L0_attn_W_o"].T
    d["l0fw"] = inputs["L0_ffn_weight_fp8"].T
    d["l0fb"] = inputs["L0_ffn_bias"].reshape(8, 128).T
    d["l0sc"] = np.stack([np.asarray(inputs["L0_ffn_input_scale"]).reshape(()),
                          np.asarray(inputs["L0_ffn_weight_scale"]).reshape(())]).reshape(1, 2)
    d["l1wq"] = inputs["L1_attn_Wq"].T
    d["l1wk"] = inputs["L1_attn_Wk"].T
    d["l1wv"] = inputs["L1_attn_Wv"].T
    d["l1wo"] = inputs["L1_attn_Wo"].T
    d["l1bq"] = inputs["L1_attn_bq"].reshape(8, 128).T
    d["l1bk"] = inputs["L1_attn_bk"].reshape(8, 128).T
    d["l1bv_row"] = inputs["L1_attn_bv"].reshape(8, 128)
    d["l1bo"] = inputs["L1_attn_bo"].reshape(8, 128).T
    d["l1w1"] = inputs["L1_ffn_W1"].T
    d["l1b1"] = inputs["L1_ffn_b1"].reshape(32, 128).T
    d["l1w2"] = inputs["L1_ffn_W2"].T
    d["l1b2"] = inputs["L1_ffn_b2"].reshape(8, 128).T
    d["l1sc"] = np.asarray(inputs["L1_ffn_scale"], np.float32).reshape(1, 1)
    f = np.arange(512, dtype=np.float64) / 512.0
    theta = 1.0 / (10000.0 ** f)
    ang = theta[:, None] * np.arange(S, dtype=np.float64)[None, :]
    for nm, fn in (("cos", np.cos), ("sin", np.sin)):
        v = fn(ang)
        hi = v.astype(np.float32)
        lo = (v - hi.astype(np.float64)).astype(np.float32)
        d[nm + "hi"] = hi
        d[nm + "lo"] = lo
    return {k: np.ascontiguousarray(v, dtype=np.float32) for k, v in d.items()}


def _get_program(stages):
    key = tuple(stages)
    if key not in _PROGRAM_CACHE:
        _PROGRAM_CACHE[key] = build_program(stages)
    return _PROGRAM_CACHE[key]


def _prog_input_names(nc):
    names = set()
    for a in nc.m.functions[0].allocations:
        if isinstance(a, mybir.MemoryLocationSet) and a.kind == "ExternalInput":
            for ml in a.memorylocations:
                names.add(ml.name)
    return names


def run_stage_group(stages, shared, streams, xros):
    nc = _get_program(stages)
    names = _prog_input_names(nc)
    in_maps = []
    for c in range(NC_N):
        m = {k: v for k, v in shared.items() if k in names}
        for j in range(BPC):
            m[f"xin{j}"] = streams[c][j]
            if f"xro{j}" in names:
                m[f"xro{j}"] = xros[c][j]
        in_maps.append(m)
    res = run_bass_kernel_spmd(nc, in_maps, core_ids=list(range(NC_N)))
    return [[np.asarray(res.results[c][f"xout{j}"]) for j in range(BPC)] for c in range(NC_N)]


STAGE_GROUPS = [("l0a", "l0f", "l1a", "l1f")]


def kernel(**inputs):
    shared = _host_prep(inputs)
    x = np.asarray(inputs["L0_attn_X"], np.float32)
    streams, xros = [], []
    for c in range(NC_N):
        ss, rr = [], []
        for j in range(BPC):
            xb = x[c * BPC + j]
            ss.append(np.ascontiguousarray(xb.T))
            rr.append(np.ascontiguousarray(
                np.concatenate([xb[:, 0::2], xb[:, 1::2]], axis=1).T))
        streams.append(ss)
        xros.append(rr)
    for stages in STAGE_GROUPS:
        streams = run_stage_group(stages, shared, streams, xros)
    out = np.empty((B, S, D), np.float32)
    for c in range(NC_N):
        for j in range(BPC):
            out[c * BPC + j] = streams[c][j].T
    return out


# version 08
# speedup vs baseline: 1.2193x; 1.0832x over previous
import numpy as np
import ml_dtypes
import concourse.bass as bass
import concourse.mybir as mybir
from concourse.bass_utils import run_bass_kernel_spmd
from contextlib import ExitStack

F32 = mybir.dt.float32
BF16 = mybir.dt.bfloat16
I32 = mybir.dt.int32
AF = mybir.ActivationFunctionType
ALU = mybir.AluOpType
AXL = mybir.AxisListType

B, S, D = 16, 2048, 1024
NC_N = 8
BPC = B // NC_N
NCH = D // 128
NKC = S // 128
NTG = S // 512
NBIG = -3.0e38

_PROGRAM_CACHE = {}
DBG = set()
DBG_DUMP = [None]
DBG_ATT = [None]
DBG_DRAM = [None]
DBG_ATT2 = [None]
DBG_DRAM2 = [None]

_NAME_SEQ = [0]


def _u(name):
    _NAME_SEQ[0] += 1
    return f"{name}_{_NAME_SEQ[0]}"




class Ctx:
    def __init__(self, nc, block, stack):
        self.nc = nc
        self.block = block
        self.stack = stack
        self.sems = {}
        self.cnt = {}
        for name in ("d", "p", "v", "a", "g", "t", "o", "s",
                     "w0", "w1", "w2", "w3"):
            self.sems[name] = stack.enter_context(nc.semaphore("sem_" + name))
            self.cnt[name] = 0

    def inc(self, inst, name, n=1):
        inst.then_inc(self.sems[name], n)
        self.cnt[name] += n
        return inst

    def dma(self, eng, out, in_, name="d"):
        return self.inc(eng.dma_start(out=out, in_=in_), name, 16)

    def wait(self, eng, name):
        eng.wait_ge(self.sems[name], self.cnt[name])

    def wait_all(self, eng):
        for name in ("d", "p", "v", "a", "g", "t", "o", "s",
                     "w0", "w1", "w2", "w3"):
            if self.cnt[name]:
                eng.wait_ge(self.sems[name], self.cnt[name])


def _fence(ctx):
    block = ctx.block

    @block.tensor
    def _(e):
        ctx.inc(e.drain(), "p")

    @block.vector
    def _(e):
        ctx.inc(e.drain(), "v")

    @block.scalar
    def _(e):
        ctx.inc(e.drain(), "a")

    @block.gpsimd
    def _(e):
        ctx.inc(e.drain(), "g")

    @block.sync
    def _(e):
        ctx.inc(e.drain(), "s")

    for emit in (block.tensor, block.vector, block.scalar, block.gpsimd, block.sync):
        @emit
        def _(e):
            ctx.wait_all(e)


def _make_consts(nc):
    c = {}
    c["ident"] = nc.inline_tensor(np.eye(128, dtype=ml_dtypes.bfloat16), "c_ident")
    c["ones_row_f"] = nc.inline_tensor(np.ones((1, 128), np.float32), "c_onesrf")
    c["ones_col_f"] = nc.inline_tensor(np.ones((128, 1), np.float32), "c_onescf")
    mask = np.triu(np.ones((128, 128), np.float32), k=1) * (-1e9)
    c["causal"] = nc.inline_tensor(mask, "c_causal")
    c["expmask"] = nc.inline_tensor(np.full((128, 1), 0x7F800000, np.int32), "c_expmask")
    return c


def _load_consts(ctx, cts):
    nc, block = ctx.nc, ctx.block
    sb = {}
    sb["ident"] = ctx.stack.enter_context(nc.sbuf_tensor(_u("cs_ident"), [128, 128], BF16))
    sb["ones_row_f"] = ctx.stack.enter_context(nc.sbuf_tensor(_u("cs_onesrf"), [1, 128], F32))
    sb["ones_col_f"] = ctx.stack.enter_context(nc.sbuf_tensor(_u("cs_onescf"), [128, 1], F32))
    sb["causal"] = ctx.stack.enter_context(nc.sbuf_tensor(_u("cs_causal"), [128, 128], F32))
    sb["expmask"] = ctx.stack.enter_context(nc.sbuf_tensor(_u("cs_expmask"), [128, 1], I32))
    sb["eps"] = ctx.stack.enter_context(nc.sbuf_tensor(_u("cs_eps"), [128, 1], F32))

    @block.gpsimd
    def _(g):
        for k in ("ident", "ones_row_f", "ones_col_f", "causal", "expmask"):
            ctx.dma(g, sb[k][:, :], cts[k].ap()[:, :])

    @block.vector
    def _(v):
        v.memset(sb["eps"][:, :], 1e-5)
        ctx.inc(v.drain(), "v")
    return sb


def emit_weight_prepass(ctx, w_dram, w16_dram, nin, nout, name):
    nc, block = ctx.nc, ctx.block
    sub = min(nout, 512)
    slices = [(ic, s0) for ic in range(nin // 128) for s0 in range(0, nout, sub)]
    with nc.sbuf_tensor(_u(name + "_pf"), [128, 2, sub], F32) as pf, \
         nc.sbuf_tensor(_u(name + "_pb"), [128, 2, sub], BF16) as pb:
        @block.gpsimd
        def _(g):
            ctx.wait_all(g)
            casts = [None] * len(slices)
            stores = [None] * len(slices)

            def cast(ci):
                slot = ci % 2
                g.wait_ge(ctx.sems["w%d" % slot], ctx.cnt["w%d" % slot])
                if ci >= 2:
                    g.wait_ge(ctx.sems["w%d" % (2 + slot)], stores[ci - 2])
                i = g.tensor_copy(out=pb[:, slot, :], in_=pf[:, slot, :])
                ctx.inc(i, "g")
                casts[ci] = ctx.cnt["g"]

            def store(ci):
                ic, s0 = slices[ci]
                slot = ci % 2
                g.wait_ge(ctx.sems["g"], casts[ci])
                i = g.dma_start(out=w16_dram[ic * 128:(ic + 1) * 128, s0:s0 + sub],
                                in_=pb[:, slot, :])
                ctx.inc(i, "w%d" % (2 + slot), 16)
                stores[ci] = ctx.cnt["w%d" % (2 + slot)]

            for si, (ic, s0) in enumerate(slices):
                slot = si % 2
                if si >= 2:
                    g.wait_ge(ctx.sems["g"], casts[si - 2])
                i = g.dma_start(out=pf[:, slot, :],
                                in_=w_dram[ic * 128:(ic + 1) * 128, s0:s0 + sub])
                ctx.inc(i, "w%d" % slot, 16)
                if si >= 1:
                    cast(si - 1)
                    store(si - 1)
            cast(len(slices) - 1)
            store(len(slices) - 1)
            for slot in (2, 3):
                g.wait_ge(ctx.sems["w%d" % slot], ctx.cnt["w%d" % slot])
            ctx.inc(g.drain(), "g")
        _fence(ctx)


def load_weight_bf16(ctx, w16_dram, nin, c0, c1, name):
    nc, block = ctx.nc, ctx.block
    nchk = nin // 128
    ncol = c1 - c0
    wbf = ctx.stack.enter_context(nc.sbuf_tensor(_u(name + "_bf"), [128, nchk, ncol], BF16))

    @block.gpsimd
    def _(g):
        ctx.wait_all(g)
        for ic in range(nchk):
            ctx.dma(g, wbf[:, ic, :], w16_dram[ic * 128:(ic + 1) * 128, c0:c1])
        ctx.wait(g, "d")
        ctx.inc(g.drain(), "g")
    _fence(ctx)
    return wbf


def emit_proj_featmajor(ctx, wbf, nin, nout, src_bf, evac, psname, oc0=0):
    nc, block = ctx.nc, ctx.block
    nic = nin // 128
    noc = nout // 128
    with nc.psum_tensor(_u(psname), [128, 2, 512], F32) as ps:
        for oc_ in range(noc):
            for tg in range(NTG):
                par = (oc_ * NTG + tg) % 2

                @block.tensor
                def _(te, oc=oc_, tg=tg, par=par):
                    if oc == 0 and tg == 0:
                        ctx.wait_all(te)
                    if oc * NTG + tg >= 2:
                        ctx.wait(te, "v")
                    for ic in range(nic):
                        i = te.matmul(ps[:, par, :], wbf[:, ic, oc * 128:(oc + 1) * 128],
                                      src_bf[:, ic, tg * 512:(tg + 1) * 512],
                                      start=(ic == 0), stop=(ic == nic - 1))
                    ctx.inc(i, "p")

                @block.vector
                def _(v, oc=oc_, tg=tg, par=par):
                    ctx.wait(v, "p")
                    evac(v, ps[:, par, :], oc0 + oc, tg)
                    ctx.inc(v.drain(), "v")
    _fence(ctx)


def emit_load_stream(ctx, x_dram, xT, xbf=None):
    nc, block = ctx.nc, ctx.block

    @block.gpsimd
    def _(g):
        ctx.wait_all(g)
        for c in range(NCH):
            ctx.dma(g, xT[:, c, :], x_dram[c * 128:(c + 1) * 128, :])
        if xbf is not None:
            ctx.wait(g, "d")
            for c in range(NCH):
                g.tensor_copy(out=xbf[:, c, :], in_=xT[:, c, :])
            ctx.inc(g.drain(), "g")
    _fence(ctx)


def emit_refresh_xbf(ctx, xT, xbf):
    @ctx.block.gpsimd
    def _(g):
        ctx.wait_all(g)
        for c in range(NCH):
            g.tensor_copy(out=xbf[:, c, :], in_=xT[:, c, :])
        ctx.inc(g.drain(), "g")
    _fence(ctx)


def emit_store_stream(ctx, xT, y_dram):
    nc, block = ctx.nc, ctx.block

    @block.gpsimd
    def _(g):
        ctx.wait_all(g)
        for c in range(NCH):
            ctx.dma(g, y_dram[c * 128:(c + 1) * 128, :], xT[:, c, :], name="o")
        ctx.wait(g, "o")
    _fence(ctx)


def emit_rope(ctx, xro_dram, tabs, xrbf):
    nc, block = ctx.nc, ctx.block
    with ExitStack() as st:
        xe = st.enter_context(nc.sbuf_tensor(_u("rp_xe"), [128, 2048], F32))
        xo = st.enter_context(nc.sbuf_tensor(_u("rp_xo"), [128, 2048], F32))
        tt = {k: st.enter_context(nc.sbuf_tensor(_u("rp_" + k), [128, 2048], F32))
              for k in ("ch", "cl", "sh", "sl")}
        m1 = st.enter_context(nc.sbuf_tensor(_u("rp_m1"), [128, 2048], F32))
        m2 = st.enter_context(nc.sbuf_tensor(_u("rp_m2"), [128, 2048], F32))
        p1 = st.enter_context(nc.sbuf_tensor(_u("rp_p1"), [128, 2048], F32))
        p2 = st.enter_context(nc.sbuf_tensor(_u("rp_p2"), [128, 2048], F32))
        for c in range(4):
            @block.gpsimd
            def _(g, c=c):
                if c == 0:
                    ctx.wait_all(g)
                else:
                    ctx.wait(g, "v")
                ctx.dma(g, xe[:, :], xro_dram[c * 128:(c + 1) * 128, :])
                ctx.dma(g, xo[:, :], xro_dram[512 + c * 128:512 + (c + 1) * 128, :])
                for k, nm in (("ch", "coshi"), ("cl", "coslo"), ("sh", "sinhi"), ("sl", "sinlo")):
                    ctx.dma(g, tt[k][:, :], tabs[nm][c * 128:(c + 1) * 128, :])

            @block.vector
            def _(v, c=c):
                ctx.wait(v, "d")
                v.tensor_tensor(out=m1[:, :], in0=xe[:, :], in1=tt["ch"][:, :], op=ALU.mult)
                v.tensor_tensor(out=m2[:, :], in0=xo[:, :], in1=tt["sh"][:, :], op=ALU.mult)
                v.tensor_tensor(out=p1[:, :], in0=m1[:, :], in1=m2[:, :], op=ALU.subtract)
                v.tensor_tensor(out=m1[:, :], in0=xe[:, :], in1=tt["cl"][:, :], op=ALU.mult)
                v.tensor_tensor(out=m2[:, :], in0=xo[:, :], in1=tt["sl"][:, :], op=ALU.mult)
                v.tensor_tensor(out=p2[:, :], in0=m1[:, :], in1=m2[:, :], op=ALU.subtract)
                v.tensor_tensor(out=p1[:, :], in0=p1[:, :], in1=p2[:, :], op=ALU.add)
                v.tensor_copy(out=xrbf[:, c, :], in_=p1[:, :])
                v.tensor_tensor(out=m1[:, :], in0=xe[:, :], in1=tt["sh"][:, :], op=ALU.mult)
                v.tensor_tensor(out=m2[:, :], in0=xo[:, :], in1=tt["ch"][:, :], op=ALU.mult)
                v.tensor_tensor(out=p1[:, :], in0=m1[:, :], in1=m2[:, :], op=ALU.add)
                v.tensor_tensor(out=m1[:, :], in0=xe[:, :], in1=tt["sl"][:, :], op=ALU.mult)
                v.tensor_tensor(out=m2[:, :], in0=xo[:, :], in1=tt["cl"][:, :], op=ALU.mult)
                v.tensor_tensor(out=p2[:, :], in0=m1[:, :], in1=m2[:, :], op=ALU.add)
                v.tensor_tensor(out=p1[:, :], in0=p1[:, :], in1=p2[:, :], op=ALU.add)
                v.tensor_copy(out=xrbf[:, 4 + c, :], in_=p1[:, :])
                ctx.inc(v.drain(), "v")
    _fence(ctx)


def emit_layernorm_inplace(ctx, consts, xT):
    nc, block = ctx.nc, ctx.block
    with ExitStack() as st:
        sq = st.enter_context(nc.sbuf_tensor(_u("ln_sq"), [128, 1024], F32))
        stats = st.enter_context(nc.sbuf_tensor(_u("ln_stats"), [1, 4, 1024], F32))
        bc = st.enter_context(nc.sbuf_tensor(_u("ln_bc"), [128, 2, 1024], F32))
        ps_mu = st.enter_context(nc.psum_tensor(_u("ln_psmu"), [1, 1024], F32))
        ps_sq = st.enter_context(nc.psum_tensor(_u("ln_pssq"), [1, 1024], F32))
        ps_bc = st.enter_context(nc.psum_tensor(_u("ln_psbc"), [128, 2, 1024], F32))
        for th in range(2):
            t0 = th * 1024

            for c in range(NCH):
                @block.scalar
                def _(a, th=th, t0=t0, c=c):
                    if th == 0 and c == 0:
                        ctx.wait_all(a)
                    elif c == 0:
                        ctx.wait(a, "v")
                    else:
                        ctx.wait(a, "p")
                    i = a.activation(out=sq[:, :], in_=xT[:, c, t0:t0 + 1024],
                                     func=AF.Square, bias=0.0, scale=1.0)
                    ctx.inc(i, "a")

                @block.tensor
                def _(te, th=th, t0=t0, c=c):
                    ctx.wait(te, "a")
                    for g2 in range(2):
                        te.matmul(ps_mu[:, g2 * 512:(g2 + 1) * 512], consts["ones_col_f"][:, :],
                                  xT[:, c, t0 + g2 * 512:t0 + (g2 + 1) * 512],
                                  start=(c == 0), stop=(c == NCH - 1))
                        i = te.matmul(ps_sq[:, g2 * 512:(g2 + 1) * 512], consts["ones_col_f"][:, :],
                                      sq[:, g2 * 512:(g2 + 1) * 512],
                                      start=(c == 0), stop=(c == NCH - 1))
                    ctx.inc(i, "p")

            @block.vector
            def _(v, th=th):
                ctx.wait(v, "p")
                v.tensor_scalar_mul(stats[0:1, 0, :], ps_mu[:, :], 1.0 / D)
                v.tensor_scalar_mul(stats[0:1, 1, :], ps_sq[:, :], 1.0 / D)
                v.tensor_tensor(out=stats[0:1, 3, :], in0=stats[0:1, 0, :],
                                in1=stats[0:1, 0, :], op=ALU.mult)
                v.tensor_tensor(out=stats[0:1, 1, :], in0=stats[0:1, 1, :],
                                in1=stats[0:1, 3, :], op=ALU.subtract)
                ctx.inc(v.drain(), "v")

            @block.scalar
            def _(a, th=th):
                ctx.wait(a, "v")
                i = a.activation(out=stats[0:1, 3, :], in_=stats[0:1, 1, :],
                                 func=AF.Sqrt, bias=consts["eps"][0:1, 0:1], scale=1.0)
                ctx.inc(i, "a")

            @block.vector
            def _(v, th=th):
                ctx.wait(v, "a")
                v.reciprocal(out=stats[0:1, 1, :], in_=stats[0:1, 3, :])
                v.tensor_tensor(out=stats[0:1, 2, :], in0=stats[0:1, 0, :],
                                in1=stats[0:1, 1, :], op=ALU.mult)
                v.tensor_scalar_mul(stats[0:1, 2, :], stats[0:1, 2, :], -1.0)
                ctx.inc(v.drain(), "v")

            @block.tensor
            def _(te, th=th):
                ctx.wait(te, "v")
                for g2 in range(2):
                    te.matmul(ps_bc[:, 0, g2 * 512:(g2 + 1) * 512], consts["ones_row_f"][0:1, :],
                              stats[0:1, 1, g2 * 512:(g2 + 1) * 512], start=True, stop=True)
                    i = te.matmul(ps_bc[:, 1, g2 * 512:(g2 + 1) * 512], consts["ones_row_f"][0:1, :],
                                  stats[0:1, 2, g2 * 512:(g2 + 1) * 512], start=True, stop=True)
                ctx.inc(i, "p")

            @block.vector
            def _(v, th=th, t0=t0):
                ctx.wait(v, "p")
                v.tensor_copy(out=bc[:, 0, :], in_=ps_bc[:, 0, :])
                v.tensor_copy(out=bc[:, 1, :], in_=ps_bc[:, 1, :])
                for c in range(NCH):
                    v.tensor_tensor(out=xT[:, c, t0:t0 + 1024], in0=xT[:, c, t0:t0 + 1024],
                                    in1=bc[:, 0, :], op=ALU.mult)
                    v.tensor_tensor(out=xT[:, c, t0:t0 + 1024], in0=xT[:, c, t0:t0 + 1024],
                                    in1=bc[:, 1, :], op=ALU.add)
                ctx.inc(v.drain(), "v")
    _fence(ctx)


def emit_negm(ctx, consts, qt_bf, kt_bf, layer, negm_bc, scr_dram, tag):
    nc, block = ctx.nc, ctx.block
    kch = 4 if layer == 0 else 8
    with ExitStack() as st:
        mq = st.enter_context(nc.sbuf_tensor(_u(tag + "_mq"), [128, 8], F32))
        mk = st.enter_context(nc.sbuf_tensor(_u(tag + "_mk"), [128, 8], F32))
        flat = st.enter_context(nc.sbuf_tensor(_u(tag + "_flat"), [1, 2048], F32))
        mrow = st.enter_context(nc.sbuf_tensor(_u(tag + "_mrow"), [1, 3, 16], F32))
        ps = st.enter_context(nc.psum_tensor(_u(tag + "_ps"), [128, 16], F32))
        if "dump_negm" in DBG and tag == "l0nm":
            DBG_ATT2[0] = st.enter_context(nc.sbuf_tensor(_u("nm_dbg"), [128, 2128], F32))

        @block.vector
        def _(v):
            ctx.wait_all(v)
            v.memset(mk[:, :], 0.0)
            for c in range(NCH):
                v.reduce_max(out=mq[:, c:c + 1], in_=qt_bf[:, c, :], axis=AXL.X,
                             apply_absolute_value=True)
            for c in range(kch):
                v.reduce_max(out=mk[:, c:c + 1], in_=kt_bf[:, c, :], axis=AXL.X,
                             apply_absolute_value=True)
            ctx.inc(v.drain(), "v")

        @block.gpsimd
        def _(g):
            ctx.wait(g, "v")
            ctx.dma(g, scr_dram[0:128, :], mq[:, :])
            ctx.dma(g, scr_dram[128:256, :], mk[:, :])

        @block.gpsimd
        def _(g):
            ctx.wait(g, "d")
            ctx.dma(g, flat[0:1, :], scr_dram.rearrange("(o p) c -> o (p c)", o=1)[0:1, :])

        @block.vector
        def _(v):
            ctx.wait(v, "d")
            for h in range(16):
                st_q = 64 * (h % 2) * 8 + (h // 2)
                v.reduce_max(out=mrow[0:1, 0, h:h + 1],
                             in_=flat[0:1, st_q:st_q + 505:8], axis=AXL.X)
            for h in range(16):
                if layer == 0:
                    gk = h // 4
                    st_k = 1024 + gk
                    v.reduce_max(out=mrow[0:1, 1, h:h + 1],
                                 in_=flat[0:1, st_k:st_k + 505:8], axis=AXL.X)
                else:
                    st_k = 1024 + 64 * (h % 2) * 8 + (h // 2)
                    v.reduce_max(out=mrow[0:1, 1, h:h + 1],
                                 in_=flat[0:1, st_k:st_k + 505:8], axis=AXL.X)
            ctx.inc(v.drain(), "v")
            ctx.wait(v, "v")
            v.scalar_tensor_tensor(out=mrow[0:1, 2, :], in0=mrow[0:1, 0, :],
                                   scalar=-64.0, in1=mrow[0:1, 1, :],
                                   op0=ALU.mult, op1=ALU.mult)
            ctx.inc(v.drain(), "v")

        @block.tensor
        def _(te):
            ctx.wait(te, "v")
            i = te.matmul(ps[:, :], consts["ones_row_f"][0:1, :], mrow[0:1, 2, :],
                          start=True, stop=True)
            ctx.inc(i, "p")

        @block.vector
        def _(v):
            ctx.wait(v, "p")
            v.tensor_copy(out=negm_bc[:, :], in_=ps[:, :])
            if "dump_negm" in DBG and tag == "l0nm":
                v.tensor_copy(out=DBG_ATT2[0][:, 0:8], in_=mq[:, :])
                v.tensor_copy(out=DBG_ATT2[0][:, 8:16], in_=mk[:, :])
                v.tensor_copy(out=DBG_ATT2[0][0:1, 16:2064], in_=flat[0:1, :])
                v.tensor_copy(out=DBG_ATT2[0][0:1, 2064:2112], in_=mrow[0:1, :, :])
                v.tensor_copy(out=DBG_ATT2[0][:, 2112:2128], in_=negm_bc[:, :])
            ctx.inc(v.drain(), "v")

        if "dump_negm" in DBG and tag == "l0nm":
            @block.gpsimd
            def _(g):
                ctx.wait(g, "v")
                ctx.dma(g, DBG_DRAM2[0][:, :], DBG_ATT2[0][:, :], name="o")
                ctx.wait(g, "o")
    _fence(ctx)


def emit_attention(ctx, consts, layer, qt_bf, kt_bf, v_bf, negm_bc, ctx_bf):
    nc, block = ctx.nc, ctx.block
    with ExitStack() as st:
        E = st.enter_context(nc.sbuf_tensor(_u("at_E"), [128, 2048], BF16))
        W = st.enter_context(nc.sbuf_tensor(_u("at_W"), [128, 2048], BF16))
        PT = st.enter_context(nc.sbuf_tensor(_u("at_PT"), [128, 16, 128], BF16))
        m8 = st.enter_context(nc.sbuf_tensor(_u("at_m8"), [128, 8], F32))
        thr = st.enter_context(nc.sbuf_tensor(_u("at_thr"), [128, 1], F32))
        rs = st.enter_context(nc.sbuf_tensor(_u("at_rs"), [128, 2], F32))
        rc = st.enter_context(nc.sbuf_tensor(_u("at_rc"), [128, 1], F32))
        rm = st.enter_context(nc.sbuf_tensor(_u("at_rm"), [128, 3], F32))
        if "dump_att" in DBG:
            DBG_ATT[0] = st.enter_context(nc.sbuf_tensor(_u("at_dbg"), [128, 4200], F32))
        ps_s = st.enter_context(nc.psum_tensor(_u("at_pss"), [128, 2, 1024], F32))
        ps_c = st.enter_context(nc.psum_tensor(_u("at_psc"), [128, 128], F32))

        for h in range(16):
            grp = h // 4
            qch, qba = h // 2, 64 * (h % 2)
            for qi in range(NKC):
                live = (qi + 1) if layer == 0 else NKC
                for half in range(2):
                    k0 = half * 8
                    nliv = min(8, max(0, live - k0))
                    if nliv == 0:
                        continue

                    @block.tensor
                    def _(te, h=h, qi=qi, half=half, k0=k0, nliv=nliv, qch=qch,
                          qba=qba, grp=grp):
                        if h == 0 and qi == 0 and half == 0:
                            ctx.wait_all(te)
                        ctx.wait(te, "a")
                        lhsT = qt_bf[qba:qba + 64, qch, 128 * qi:128 * (qi + 1)]
                        if layer == 0:
                            rhs_t = kt_bf[qba:qba + 64, grp, :]
                        else:
                            rhs_t = kt_bf[qba:qba + 64, qch, :]
                        for nb in range((nliv * 128 + 511) // 512):
                            n0 = k0 * 128 + nb * 512
                            nn = min(512, (k0 + nliv) * 128 - n0)
                            i = te.matmul(ps_s[:, half, nb * 512:nb * 512 + nn],
                                          lhsT, rhs_t[:, n0:n0 + nn],
                                          start=True, stop=True,
                                          tile_position=(qba, 0))
                        ctx.inc(i, "p")

                    if layer == 0 and k0 <= qi < k0 + 8:
                        @block.vector
                        def _(v, qi=qi, half=half, k0=k0):
                            ctx.wait(v, "p")
                            cp = (qi - k0) * 128
                            v.tensor_tensor(out=ps_s[:, half, cp:cp + 128],
                                            in0=ps_s[:, half, cp:cp + 128],
                                            in1=consts["causal"][:, :], op=ALU.add)
                            ctx.inc(v.drain(), "v")

                    @block.vector
                    def _(v, half=half, nliv=nliv):
                        ctx.wait(v, "p")
                        i = v.reduce_max(out=rm[:, half:half + 1],
                                         in_=ps_s[:, half, 0:nliv * 128],
                                         axis=AXL.X, negate=True)
                        ctx.inc(i, "v")

                nhalf = 1 if live <= 8 else 2

                @block.vector
                def _(v, nhalf=nhalf):
                    ctx.wait(v, "v")
                    if nhalf == 2:
                        i = v.tensor_tensor(out=rm[:, 2:3], in0=rm[:, 0:1],
                                            in1=rm[:, 1:2], op=ALU.min)
                    else:
                        i = v.tensor_copy(out=rm[:, 2:3], in_=rm[:, 0:1])
                    ctx.inc(i, "v")

                for half in range(2):
                    k0 = half * 8
                    nliv = min(8, max(0, live - k0))
                    if nliv == 0:
                        continue

                    @block.scalar
                    def _(a, h=h, half=half, k0=k0, nliv=nliv):
                        ctx.wait(a, "p")
                        ctx.wait(a, "v")
                        ctx.wait(a, "t")
                        i = a.activation(out=E[:, k0 * 128:(k0 + nliv) * 128],
                                         in_=ps_s[:, half, 0:nliv * 128], func=AF.Exp,
                                         bias=rm[:, 2:3], scale=1.0,
                                         accum_out=rs[:, half:half + 1])
                        ctx.inc(i, "a")

                nlive = live * 128
                if layer == 1:
                    @block.gpsimd
                    def _(gp):
                        ctx.wait(gp, "a")
                        i = gp.tensor_copy(out=W[:, :], in_=E[:, :])
                        ctx.inc(i, "g")

                    @block.vector
                    def _(v, h=h, qi=qi):
                        ctx.wait(v, "g")
                        for rnd in range(5):
                            i = v.max(out=m8[:, :], in_=W[:, :])
                            ctx.inc(i, "v")
                            ctx.wait(v, "v")
                            if rnd < 4:
                                v.tensor_scalar(out=PT[:, :, :], in0=W[:, :],
                                                scalar1=m8[:, 7:8], scalar2=NBIG,
                                                op0=ALU.is_ge, op1=ALU.mult)
                                v.tensor_tensor(out=W[:, :], in0=W[:, :],
                                                in1=PT[:, :, :], op=ALU.add)
                        i = v.tensor_copy(out=thr[:, 0:1], in_=m8[:, 5:6])
                        ctx.inc(i, "v")
                        ctx.wait(v, "v")
                        i = v.scalar_tensor_tensor(out=E[:, :], in0=E[:, :],
                                                   scalar=thr[:, 0:1], in1=E[:, :],
                                                   op0=ALU.is_ge, op1=ALU.mult,
                                                   accum_out=rs[:, 0:1])
                        ctx.inc(i, "v")
                        ctx.wait(v, "v")
                        i = v.reciprocal(out=rc[:, 0:1], in_=rs[:, 0:1])
                        ctx.inc(i, "v")
                        ctx.wait(v, "v")
                        v.tensor_scalar_mul(E[:, :], E[:, :], rc[:, 0:1])
                        ctx.inc(v.drain(), "v")
                else:
                    @block.vector
                    def _(v, h=h, qi=qi, nlive=nlive, live=live):
                        ctx.wait(v, "a")
                        if live > 8:
                            i = v.tensor_tensor(out=rs[:, 0:1], in0=rs[:, 0:1],
                                                in1=rs[:, 1:2], op=ALU.add)
                            ctx.inc(i, "v")
                            ctx.wait(v, "v")
                        i = v.reciprocal(out=rc[:, 0:1], in_=rs[:, 0:1])
                        ctx.inc(i, "v")
                        ctx.wait(v, "v")
                        if "dump_att" in DBG and h == 0 and qi == 2:
                            v.tensor_copy(out=DBG_ATT[0][:, 0:nlive], in_=E[:, 0:nlive])
                            v.tensor_copy(out=DBG_ATT[0][:, 2048:2049], in_=rs[:, 0:1])
                            v.tensor_copy(out=DBG_ATT[0][:, 2049:2050], in_=rc[:, 0:1])
                        v.tensor_scalar_mul(E[:, 0:nlive], E[:, 0:nlive], rc[:, 0:1])
                        ctx.inc(v.drain(), "v")

                @block.scalar
                def _(a, live=live):
                    ctx.wait(a, "v")
                    ctx.wait(a, "p")
                    ctx.wait(a, "t")
                    i = a.dma_start(out=PT[:, 0:live, :],
                                    in_=E[:, 0:live * 128], transpose=True)
                    ctx.inc(i, "t", 16)

                @block.tensor
                def _(te, h=h, qi=qi, grp=grp, live=live, qba=qba):
                    ctx.wait(te, "t")
                    ctx.wait(te, "v")
                    for c in range(live):
                        if layer == 0:
                            va = v_bf[:, c, 64 * grp:64 * grp + 64]
                        else:
                            va = v_bf[:, c, 64 * h:64 * h + 64]
                        i = te.matmul(ps_c[qba:qba + 64, :], va, PT[:, c, :],
                                      start=(c == 0), stop=(c == live - 1),
                                      tile_position=(0, qba), skip_group_check=True)
                    ctx.inc(i, "p")

                @block.vector
                def _(v, h=h, qi=qi, qba=qba):
                    ctx.wait(v, "p")
                    v.tensor_copy(out=ctx_bf[qba:qba + 64, h // 2, 128 * qi:128 * (qi + 1)],
                                  in_=ps_c[qba:qba + 64, :])
                    ctx.inc(v.drain(), "v")
        if "dump_att" in DBG:
            @block.gpsimd
            def _(g):
                ctx.wait_all(g)
                ctx.dma(g, DBG_DRAM[0][:, :], DBG_ATT[0][:, :], name="o")
                ctx.wait(g, "o")
        if "dump_ctx" in DBG:
            with nc.sbuf_tensor(_u("dmpc"), [128, 2048], F32) as dmpc:
                for c in range(NCH):
                    @block.gpsimd
                    def _(g, c=c):
                        ctx.wait_all(g)
                        i = g.tensor_copy(out=dmpc[:, :], in_=ctx_bf[:, c, :])
                        ctx.inc(i, "g")
                        ctx.wait(g, "g")
                        ctx.dma(g, DBG_DRAM[0][c * 128:(c + 1) * 128, 0:2048],
                                dmpc[:, :], name="o")
                        ctx.wait(g, "o")
    _fence(ctx)


def emit_l0_attn(ctx, consts, wd, tabs, xT, scr_dram):
    nc, block = ctx.nc, ctx.block
    with ExitStack() as st:
        xr_bf = st.enter_context(nc.sbuf_tensor(_u("l0_xr"), [128, 8, 2048], BF16))
        emit_rope(ctx, wd["xro"], tabs, xr_bf)

        qt_bf = st.enter_context(nc.sbuf_tensor(_u("l0_qt"), [128, 8, 2048], BF16))
        ktd_bf = st.enter_context(nc.sbuf_tensor(_u("l0_kt"), [128, 4, 2048], BF16))
        v_bf = st.enter_context(nc.sbuf_tensor(_u("l0_v"), [128, 16, 256], BF16))

        with ExitStack() as st2:
            saved, ctx.stack = ctx.stack, st2
            wq = load_weight_bf16(ctx, wd["l0wq"], 1024, 0, 1024, "l0wq")

            def evq(v, psap, oc, tg):
                v.tensor_copy(out=qt_bf[:, oc, tg * 512:(tg + 1) * 512], in_=psap)
            emit_proj_featmajor(ctx, wq, 1024, 1024, xr_bf, evq, "l0psq")
            ctx.stack = saved

        with ExitStack() as st2:
            saved, ctx.stack = ctx.stack, st2
            wk = load_weight_bf16(ctx, wd["l0wk"], 1024, 0, 256, "l0wk")

            def evk(v, psap, oc, tg):
                for gg in range(2):
                    gidx = 2 * oc + gg
                    for hf in range(2):
                        v.tensor_scalar(out=ktd_bf[64 * hf:64 * hf + 64, gidx,
                                                   tg * 512:(tg + 1) * 512],
                                        in0=psap[64 * gg:64 * gg + 64, :], scalar1=0.125,
                                        scalar2=None, op0=ALU.mult)
            emit_proj_featmajor(ctx, wk, 1024, 256, xr_bf, evk, "l0psk")
            ctx.stack = saved

        with ExitStack() as st2:
            saved, ctx.stack = ctx.stack, st2
            wv = load_weight_bf16(ctx, wd["l0wv"], 1024, 0, 256, "l0wv")
            with nc.psum_tensor(_u("l0psv"), [128, 2, 512], F32) as psv:
                for tc in range(NKC):
                    par = tc % 2

                    @block.tensor
                    def _(te, tc=tc, par=par):
                        if tc == 0:
                            ctx.wait_all(te)
                        if tc >= 2:
                            ctx.wait(te, "v")
                        for ic in range(NCH):
                            i = te.matmul(psv[:, par, 0:256],
                                          xr_bf[:, ic, tc * 128:(tc + 1) * 128],
                                          wv[:, ic, :], start=(ic == 0), stop=(ic == NCH - 1))
                        ctx.inc(i, "p")

                    @block.vector
                    def _(v, tc=tc, par=par):
                        ctx.wait(v, "p")
                        v.tensor_copy(out=v_bf[:, tc, :], in_=psv[:, par, 0:256])
                        ctx.inc(v.drain(), "v")
            _fence(ctx)
            ctx.stack = saved

        emit_attention(ctx, consts, 0, qt_bf, ktd_bf, v_bf, None, xr_bf)

        with ExitStack() as st2:
            saved, ctx.stack = ctx.stack, st2
            wo = load_weight_bf16(ctx, wd["l0wo"], 1024, 0, 1024, "l0wo")

            def evo(v, psap, oc, tg):
                v.tensor_tensor(out=xT[:, oc, tg * 512:(tg + 1) * 512],
                                in0=psap, in1=xT[:, oc, tg * 512:(tg + 1) * 512], op=ALU.add)
            emit_proj_featmajor(ctx, wo, 1024, 1024, xr_bf, evo, "l0pso")
            ctx.stack = saved
    emit_layernorm_inplace(ctx, consts, xT)


def emit_l0_ffn(ctx, consts, wd, xT):
    nc, block = ctx.nc, ctx.block
    with ExitStack() as st:
        q_bf = st.enter_context(nc.sbuf_tensor(_u("f0_q"), [128, 8, 2048], BF16))
        sc2 = st.enter_context(nc.sbuf_tensor(_u("f0_sc2"), [128, 2], F32))
        scs = st.enter_context(nc.sbuf_tensor(_u("f0_scs"), [1, 4], F32))
        bias = st.enter_context(nc.sbuf_tensor(_u("f0_bias"), [128, 8], F32))

        @block.gpsimd
        def _(g):
            ctx.wait_all(g)
            ctx.dma(g, scs[0:1, 0:2], wd["l0sc"][0:1, 0:2])
            ctx.dma(g, bias[:, :], wd["l0fb"][:, :])

        @block.vector
        def _(v):
            ctx.wait(v, "d")
            i = v.reciprocal(out=scs[0:1, 2:3], in_=scs[0:1, 0:1])
            ctx.inc(i, "v")
            ctx.wait(v, "v")
            v.tensor_tensor(out=scs[0:1, 3:4], in0=scs[0:1, 0:1],
                            in1=scs[0:1, 1:2], op=ALU.mult)
            ctx.inc(v.drain(), "v")

        with nc.psum_tensor(_u("f0psb"), [128, 2], F32) as psb:
            @block.tensor
            def _(te):
                ctx.wait(te, "v")
                i = te.matmul(psb[:, :], consts["ones_row_f"][0:1, :], scs[0:1, 2:4],
                              start=True, stop=True)
                ctx.inc(i, "p")

            @block.vector
            def _(v):
                ctx.wait(v, "p")
                v.tensor_copy(out=sc2[:, :], in_=psb[:, :])
                ctx.inc(v.drain(), "v")
        _fence(ctx)

        with ExitStack() as stq:
            t = stq.enter_context(nc.sbuf_tensor(_u("f0_t"), [128, 2048], F32))
            sg = stq.enter_context(nc.sbuf_tensor(_u("f0_sg"), [128, 2048], F32))
            aa = stq.enter_context(nc.sbuf_tensor(_u("f0_a"), [128, 2048], F32))
            ib = stq.enter_context(nc.sbuf_tensor(_u("f0_ib"), [128, 2048], I32))
            sp = stq.enter_context(nc.sbuf_tensor(_u("f0_sp"), [128, 2048], F32))
            uu = stq.enter_context(nc.sbuf_tensor(_u("f0_u"), [128, 2048], F32))
            vv = stq.enter_context(nc.sbuf_tensor(_u("f0_v"), [128, 2048], F32))
            bb = stq.enter_context(nc.sbuf_tensor(_u("f0_b"), [128, 2048], F32))
            for c in range(NCH):
                @block.vector
                def _(v, c=c):
                    if c == 0:
                        ctx.wait_all(v)
                    else:
                        ctx.wait(v, "g")
                    v.tensor_scalar(out=t[:, :], in0=xT[:, c, :], scalar1=sc2[:, 0:1],
                                    scalar2=448.0, op0=ALU.mult, op1=ALU.min)
                    v.tensor_scalar_max(t[:, :], t[:, :], -448.0)
                    ctx.inc(v.drain(), "v")

                @block.scalar
                def _(a, c=c):
                    ctx.wait(a, "v")
                    ctx.wait(a, "g")
                    a.activation(out=sg[:, :], in_=t[:, :], func=AF.Sign, bias=0.0, scale=1.0)
                    i = a.activation(out=aa[:, :], in_=t[:, :], func=AF.Abs, bias=0.0, scale=1.0)
                    ctx.inc(i, "a")

                @block.vector
                def _(v, c=c):
                    ctx.wait(v, "a")
                    v.tensor_scalar(out=ib[:, :], in0=aa[:, :].bitcast(I32),
                                    scalar1=consts["expmask"][:, 0:1], scalar2=None,
                                    op0=ALU.bitwise_and)
                    v.tensor_scalar(out=sp[:, :], in0=ib[:, :].bitcast(F32),
                                    scalar1=0.125, scalar2=2.0 ** -9,
                                    op0=ALU.mult, op1=ALU.max)
                    v.reciprocal(out=uu[:, :], in_=sp[:, :])
                    v.tensor_tensor(out=uu[:, :], in0=aa[:, :], in1=uu[:, :], op=ALU.mult)
                    v.tensor_scalar_min(uu[:, :], uu[:, :], 16.0)
                    v.tensor_scalar(out=vv[:, :], in0=uu[:, :], scalar1=0.5,
                                    scalar2=None, op0=ALU.add)
                    for cbit in (8.0, 8.0, 4.0, 2.0, 1.0):
                        v.tensor_scalar(out=bb[:, :], in0=vv[:, :], scalar1=cbit,
                                        scalar2=None, op0=ALU.is_ge)
                        v.scalar_tensor_tensor(out=vv[:, :], in0=bb[:, :], scalar=-cbit,
                                               in1=vv[:, :], op0=ALU.mult, op1=ALU.add)
                    v.tensor_tensor(out=uu[:, :], in0=uu[:, :], in1=vv[:, :], op=ALU.subtract)
                    v.tensor_scalar(out=uu[:, :], in0=uu[:, :], scalar1=0.5,
                                    scalar2=None, op0=ALU.add)
                    v.tensor_tensor(out=uu[:, :], in0=uu[:, :], in1=sp[:, :], op=ALU.mult)
                    ctx.inc(v.drain(), "v")

                @block.gpsimd
                def _(g, c=c):
                    ctx.wait(g, "v")
                    g.tensor_tensor(out=q_bf[:, c, :], in0=uu[:, :], in1=sg[:, :], op=ALU.mult)
                    ctx.inc(g.drain(), "g")
        _fence(ctx)

        with ExitStack() as st2:
            saved, ctx.stack = ctx.stack, st2
            fw = load_weight_bf16(ctx, wd["l0fw"], 1024, 0, 1024, "l0fw")
            with nc.sbuf_tensor(_u("f0_ev"), [128, 512], F32) as ev:
                def evf(v, psap, oc, tg):
                    v.tensor_scalar(out=ev[:, :], in0=psap, scalar1=sc2[:, 1:2],
                                    scalar2=bias[:, oc:oc + 1], op0=ALU.mult, op1=ALU.add)
                    v.tensor_tensor(out=xT[:, oc, tg * 512:(tg + 1) * 512],
                                    in0=ev[:, :], in1=xT[:, oc, tg * 512:(tg + 1) * 512],
                                    op=ALU.add)
                emit_proj_featmajor(ctx, fw, 1024, 1024, q_bf, evf, "f0ps")
            ctx.stack = saved
    emit_layernorm_inplace(ctx, consts, xT)


def emit_l1_attn(ctx, consts, wd, xT, xbf, scr_dram):
    nc, block = ctx.nc, ctx.block
    with ExitStack() as st:
        qt_bf = st.enter_context(nc.sbuf_tensor(_u("l1_qt"), [128, 8, 2048], BF16))
        kt_bf = st.enter_context(nc.sbuf_tensor(_u("l1_kt"), [128, 8, 2048], BF16))
        v_bf = st.enter_context(nc.sbuf_tensor(_u("l1_v"), [128, 16, 1024], BF16))
        bq = st.enter_context(nc.sbuf_tensor(_u("l1_bq"), [128, 8], F32))
        bk = st.enter_context(nc.sbuf_tensor(_u("l1_bk"), [128, 8], F32))
        bo = st.enter_context(nc.sbuf_tensor(_u("l1_bo"), [128, 8], F32))
        bv_scope = ExitStack()
        bv_bc = bv_scope.enter_context(nc.sbuf_tensor(_u("l1_bvbc"), [128, 1024], BF16))
        bvr_scope = ExitStack()
        bvr = bvr_scope.enter_context(nc.sbuf_tensor(_u("l1_bvr"), [1, 1024], F32))

        @block.gpsimd
        def _(g):
            ctx.wait_all(g)
            ctx.dma(g, bq[:, :], wd["l1bq"][:, :])
            ctx.dma(g, bk[:, :], wd["l1bk"][:, :])
            ctx.dma(g, bo[:, :], wd["l1bo"][:, :])
            ctx.dma(g, bvr[0:1, :], wd["l1bv_row"].rearrange("(o a) b -> o (a b)", o=1)[0:1, :])

        with nc.psum_tensor(_u("l1psbv"), [128, 2, 512], F32) as psbv:
            @block.tensor
            def _(te):
                ctx.wait(te, "d")
                for half in range(2):
                    i = te.matmul(psbv[:, half, :], consts["ones_row_f"][0:1, :],
                                  bvr[0:1, half * 512:(half + 1) * 512],
                                  start=True, stop=True)
                ctx.inc(i, "p")

            @block.vector
            def _(v):
                ctx.wait(v, "p")
                v.tensor_copy(out=bv_bc[:, 0:512], in_=psbv[:, 0, :])
                v.tensor_copy(out=bv_bc[:, 512:1024], in_=psbv[:, 1, :])
                ctx.inc(v.drain(), "v")
        _fence(ctx)
        bvr_scope.close()

        def evq(v, psap, oc, tg):
            v.tensor_scalar(out=qt_bf[:, oc, tg * 512:(tg + 1) * 512], in0=psap,
                            scalar1=bq[:, oc:oc + 1], scalar2=None, op0=ALU.add)
        for half in range(2):
            with ExitStack() as st2:
                saved, ctx.stack = ctx.stack, st2
                wq = load_weight_bf16(ctx, wd["l1wq"], 1024, half * 512,
                                      (half + 1) * 512, "l1wq")
                emit_proj_featmajor(ctx, wq, 1024, 512, xbf, evq, "l1psq", oc0=half * 4)
                ctx.stack = saved

        def evk(v, psap, oc, tg):
            v.tensor_scalar(out=kt_bf[:, oc, tg * 512:(tg + 1) * 512], in0=psap,
                            scalar1=bk[:, oc:oc + 1], scalar2=0.125,
                            op0=ALU.add, op1=ALU.mult)
        for half in range(2):
            with ExitStack() as st2:
                saved, ctx.stack = ctx.stack, st2
                wk = load_weight_bf16(ctx, wd["l1wk"], 1024, half * 512,
                                      (half + 1) * 512, "l1wk")
                emit_proj_featmajor(ctx, wk, 1024, 512, xbf, evk, "l1psk", oc0=half * 4)
                ctx.stack = saved

        for vhalf in range(2):
          with ExitStack() as st2:
            saved, ctx.stack = ctx.stack, st2
            wv = load_weight_bf16(ctx, wd["l1wv"], 1024, vhalf * 512,
                                  (vhalf + 1) * 512, "l1wv")
            with nc.psum_tensor(_u("l1psv"), [128, 2, 512], F32) as psv:
                for tc in range(NKC):
                    for vh in [vhalf]:
                        par = tc % 2

                        @block.tensor
                        def _(te, tc=tc, vh=vh, par=par):
                            if tc == 0:
                                ctx.wait_all(te)
                            else:
                                ctx.wait(te, "v")
                            for ic in range(NCH):
                                i = te.matmul(psv[:, par, :],
                                              xbf[:, ic, tc * 128:(tc + 1) * 128],
                                              wv[:, ic, :],
                                              start=(ic == 0), stop=(ic == NCH - 1))
                            ctx.inc(i, "p")

                        @block.vector
                        def _(v, tc=tc, vh=vh, par=par):
                            ctx.wait(v, "p")
                            v.tensor_tensor(out=v_bf[:, tc, vh * 512:(vh + 1) * 512],
                                            in0=psv[:, par, :],
                                            in1=bv_bc[:, vh * 512:(vh + 1) * 512], op=ALU.add)
                            ctx.inc(v.drain(), "v")
            _fence(ctx)
            ctx.stack = saved

        bv_scope.close()
        emit_attention(ctx, consts, 1, qt_bf, kt_bf, v_bf, None, xbf)

        for half in range(2):
            with ExitStack() as st2:
                saved, ctx.stack = ctx.stack, st2
                wo = load_weight_bf16(ctx, wd["l1wo"], 1024, half * 512,
                                      (half + 1) * 512, "l1wo")
                with nc.sbuf_tensor(_u("l1_ev"), [128, 512], F32) as ev:
                    def evo(v, psap, oc, tg):
                        v.tensor_scalar(out=ev[:, :], in0=psap, scalar1=bo[:, oc:oc + 1],
                                        scalar2=None, op0=ALU.add)
                        v.tensor_tensor(out=xT[:, oc, tg * 512:(tg + 1) * 512],
                                        in0=ev[:, :],
                                        in1=xT[:, oc, tg * 512:(tg + 1) * 512],
                                        op=ALU.add)
                    emit_proj_featmajor(ctx, wo, 1024, 512, xbf, evo, "l1pso",
                                        oc0=half * 4)
                ctx.stack = saved
    emit_layernorm_inplace(ctx, consts, xT)


def emit_l1_ffn(ctx, consts, wd, xT, xbf, scr2):
    nc, block = ctx.nc, ctx.block
    with ExitStack() as st:
        b1 = st.enter_context(nc.sbuf_tensor(_u("f1_b1"), [128, 32], F32))
        b2 = st.enter_context(nc.sbuf_tensor(_u("f1_b2"), [128, 8], F32))
        scl = st.enter_context(nc.sbuf_tensor(_u("f1_scl"), [1, 1], F32))
        se = st.enter_context(nc.sbuf_tensor(_u("f1_se"), [1, 8, 1024], BF16))
        se2q = st.enter_context(nc.sbuf_tensor(_u("f1_se2q"), [1, 16, 256], BF16))
        p_sb = st.enter_context(nc.sbuf_tensor(_u("f1_p"), [1, 4, 2048], BF16))
        r_sb = st.enter_context(nc.sbuf_tensor(_u("f1_r"), [1, 16, 512], BF16))
        hT = st.enter_context(nc.sbuf_tensor(_u("f1_h"), [128, 32, 512], BF16))

        @block.gpsimd
        def _(g):
            ctx.wait_all(g)
            ctx.dma(g, b1[:, :], wd["l1b1"][:, :])
            ctx.dma(g, b2[:, :], wd["l1b2"][:, :])
            ctx.dma(g, scl[0:1, 0:1], wd["l1sc"][0:1, 0:1])

        do_prep = "noprep" not in DBG
        lt_scope = ExitStack()
        lt = lt_scope.enter_context(nc.sbuf_tensor(_u("f1_lt"), [1, 8, 1024], BF16))
        with nc.sbuf_tensor(_u("f1_ptt"), [128, 128], BF16) as ptt:
          if do_prep:
              for c in range(NCH):
                  @block.scalar
                  def _(a, c=c):
                      if c == 0:
                          ctx.wait_all(a)
                      else:
                          ctx.wait(a, "d")
                      i = a.dma_start(out=ptt[:, :], in_=xbf[:, c, 0:128], transpose=True)
                      ctx.inc(i, "t", 16)

                  @block.gpsimd
                  def _(g, c=c):
                      ctx.wait(g, "t")
                      ctx.dma(g, scr2[:, c * 128:(c + 1) * 128], ptt[1:17, :])

              @block.gpsimd
              def _(g):
                  ctx.wait(g, "d")
                  i = g.dma_start(out=lt[0:1, :, :],
                                  in_=scr2.rearrange("(o r) p -> o r p", o=1)[0:1, 0:16:2, :])
                  ctx.inc(i, "d", 16)
          _fence(ctx)

        @block.vector
        def _(v):
            if not do_prep:
                ctx.inc(v.drain(), "v")
                return
            ctx.wait(v, "v")
            for i8 in range(8):
                v.tensor_scalar_mul(se[0:1, i8, :], lt[0:1, i8, :], scl[0:1, 0:1])
            ctx.inc(v.drain(), "v")
            ctx.wait(v, "v")
            for i4 in range(4):
                for q4 in range(4):
                    for half in range(2):
                        v.tensor_copy(out=se2q[0:1, i4 * 4 + q4, half * 128:(half + 1) * 128],
                                      in_=se[0:1, 4 + i4, half * 512 + q4:(half + 1) * 512:4])
            ctx.inc(v.drain(), "v")
        _fence(ctx)
        lt_scope.close()

        with nc.psum_tensor(_u("f1psp"), [1, 4, 512], F32) as psp:
            for tg in (range(NTG) if do_prep else []):
                @block.tensor
                def _(te, tg=tg):
                    if tg == 0:
                        ctx.wait_all(te)
                    else:
                        ctx.wait(te, "v")
                    for i4 in range(4):
                        for ic in range(NCH):
                            i = te.matmul(psp[:, i4, :],
                                          xbf[:, ic, 2 + 2 * i4:3 + 2 * i4],
                                          xbf[:, ic, tg * 512:(tg + 1) * 512],
                                          start=(ic == 0), stop=(ic == NCH - 1))
                    ctx.inc(i, "p")

                @block.vector
                def _(v, tg=tg):
                    ctx.wait(v, "p")
                    v.tensor_copy(out=p_sb[0:1, :, tg * 512:(tg + 1) * 512], in_=psp[0:1, :, :])
                    ctx.inc(v.drain(), "v")
        _fence(ctx)

        for tg in range(NTG):
            for hs in range(4):
                with ExitStack() as st2:
                    saved, ctx.stack = ctx.stack, st2
                    w1 = load_weight_bf16(ctx, wd["l1w1"], 1024, hs * 1024, (hs + 1) * 1024,
                                          "f1w1")
                    with nc.psum_tensor(_u("f1ps1"), [128, 2, 512], F32) as ps1:
                        for hb8 in range(8):
                            hb = hs * 8 + hb8
                            par = hb8 % 2

                            @block.tensor
                            def _(te, tg=tg, hs=hs, hb8=hb8, hb=hb, par=par):
                                if hb8 == 0:
                                    ctx.wait_all(te)
                                if hb8 >= 2:
                                    ctx.wait(te, "a")
                                for ic in range(NCH):
                                    i = te.matmul(ps1[:, par, :],
                                              w1[:, ic, hb8 * 128:(hb8 + 1) * 128],
                                              xbf[:, ic, tg * 512:(tg + 1) * 512],
                                              start=(ic == 0),
                                              stop=(ic == NCH - 1 and "nolora" in DBG))
                                if "nolora" not in DBG:
                                    i4 = hb // 8
                                    jsl = (hb % 8) * 128
                                    i = te.matmul(ps1[:, par, :], se[0:1, i4, jsl:jsl + 128],
                                                  p_sb[0:1, i4, tg * 512:(tg + 1) * 512],
                                                  start=False, stop=True)
                                ctx.inc(i, "p")

                            @block.scalar
                            def _(a, hb=hb, par=par):
                                ctx.wait(a, "p")
                                i = a.activation(out=hT[:, hb, :], in_=ps1[:, par, :],
                                                 func=AF.Gelu, bias=b1[:, hb:hb + 1],
                                                 scale=1.0)
                                ctx.inc(i, "a")
                    _fence(ctx)
                    ctx.stack = saved

            if "dumph" in DBG and tg == 0:
                with nc.sbuf_tensor(_u("dmp"), [128, 512], F32) as dmp:
                    for hc in range(32):
                        @block.gpsimd
                        def _(g, hc=hc):
                            ctx.wait_all(g)
                            i = g.tensor_copy(out=dmp[:, :], in_=hT[:, hc, :])
                            ctx.inc(i, "g")
                            ctx.wait(g, "g")
                            ctx.dma(g, DBG_DUMP[0][(hc % 8) * 128:(hc % 8 + 1) * 128,
                                                   (hc // 8) * 512:(hc // 8 + 1) * 512],
                                    dmp[:, :], name="o")
                            ctx.wait(g, "o")
                _fence(ctx)

            for i4h in (range(2) if do_prep else []):
                with nc.psum_tensor(_u("f1psr"), [1, 8, 512], F32) as psr:
                    @block.tensor
                    def _(te, tg=tg, i4h=i4h):
                        ctx.wait_all(te)
                        for i4x in range(2):
                            i4 = 2 * i4h + i4x
                            for q4 in range(4):
                                for kc in range(8):
                                    i = te.matmul(psr[:, i4x * 4 + q4, :],
                                                  xbf[:, kc, 10 + 2 * i4:11 + 2 * i4],
                                                  hT[:, q4 * 8 + kc, :],
                                                  start=(kc == 0), stop=(kc == 7))
                        ctx.inc(i, "p")

                    @block.vector
                    def _(v, tg=tg, i4h=i4h):
                        ctx.wait(v, "p")
                        v.tensor_copy(out=r_sb[0:1, i4h * 8:(i4h + 1) * 8, :],
                                      in_=psr[0:1, :, :])
                        ctx.inc(v.drain(), "v")
                _fence(ctx)

            with nc.sbuf_tensor(_u("f1_ev2"), [128, 512], F32) as ev2, \
                 nc.psum_tensor(_u("f1ps2"), [128, 2, 512], F32) as ps2:
                for oc in range(NCH):
                    par = oc % 2
                    i4 = oc // 2
                    with ExitStack() as st2:
                        saved, ctx.stack = ctx.stack, st2
                        w2 = load_weight_bf16(ctx, wd["l1w2"], 4096, oc * 128,
                                              (oc + 1) * 128, "f1w2")

                        @block.tensor
                        def _(te, tg=tg, oc=oc, par=par, i4=i4, w2=w2):
                            ctx.wait_all(te)
                            for hc in range(32):
                                i = te.matmul(ps2[:, par, :], w2[:, hc, :],
                                          hT[:, hc, :], start=(hc == 0),
                                          stop=(hc == 31 and "nolora" in DBG))
                            if "nolora" not in DBG:
                                for q4 in range(4):
                                    i = te.matmul(ps2[:, par, :],
                                                  se2q[0:1, i4 * 4 + q4,
                                                       (oc % 2) * 128:(oc % 2) * 128 + 128],
                                                  r_sb[0:1, i4 * 4 + q4, :],
                                                  start=False, stop=(q4 == 3))
                            ctx.inc(i, "p")

                        @block.vector
                        def _(v, tg=tg, oc=oc, par=par):
                            ctx.wait(v, "p")
                            v.tensor_scalar(out=ev2[:, :], in0=ps2[:, par, :],
                                            scalar1=b2[:, oc:oc + 1], scalar2=None,
                                            op0=ALU.add)
                            v.tensor_tensor(out=xT[:, oc, tg * 512:(tg + 1) * 512],
                                            in0=ev2[:, :],
                                            in1=xT[:, oc, tg * 512:(tg + 1) * 512],
                                            op=ALU.add)
                            ctx.inc(v.drain(), "v")
                        _fence(ctx)
                        ctx.stack = saved
    if "noln" not in DBG:
        emit_layernorm_inplace(ctx, consts, xT)


def build_program(stages):
    nc = bass.Bass("TRN2", target_bir_lowering=False, detect_race_conditions=False)
    cts = _make_consts(nc)
    wd = {}

    def din(nm, shp):
        wd[nm] = nc.dram_tensor(nm, shp, F32, kind="ExternalInput").ap()

    if "l0a" in stages:
        din("l0wq", [1024, 1024]); din("l0wk", [1024, 256])
        din("l0wv", [1024, 256]); din("l0wo", [1024, 1024])
    if "l0f" in stages:
        din("l0fw", [1024, 1024]); din("l0fb", [128, 8]); din("l0sc", [1, 2])
    if "l1a" in stages:
        for nm in ("l1wq", "l1wk", "l1wv", "l1wo"):
            din(nm, [1024, 1024])
        din("l1bq", [128, 8]); din("l1bk", [128, 8])
        din("l1bv_row", [8, 128]); din("l1bo", [128, 8])
    if "l1f" in stages:
        din("l1w1", [1024, 4096]); din("l1b1", [128, 32])
        din("l1w2", [4096, 1024]); din("l1b2", [128, 8]); din("l1sc", [1, 1])
    tabs = {}
    if "l0a" in stages:
        for nm in ("coshi", "coslo", "sinhi", "sinlo"):
            tabs[nm] = nc.dram_tensor(nm, [512, 2048], F32, kind="ExternalInput").ap()
    xin, xout, xro = [], [], []
    for j in range(BPC):
        xin.append(nc.dram_tensor(f"xin{j}", [1024, 2048], F32, kind="ExternalInput").ap())
        xout.append(nc.dram_tensor(f"xout{j}", [1024, 2048], F32, kind="ExternalOutput").ap())
        if "l0a" in stages:
            xro.append(nc.dram_tensor(f"xro{j}", [1024, 2048], F32, kind="ExternalInput").ap())
    scr = nc.dram_tensor("scr", [256, 8], F32, kind="Internal").ap()
    if "dump_att" in DBG:
        DBG_DRAM[0] = nc.dram_tensor("dbgatt", [128, 4200], F32, kind="ExternalOutput").ap()
    if "dump_ctx" in DBG:
        DBG_DRAM[0] = nc.dram_tensor("dbgatt", [1024, 2048], F32, kind="ExternalOutput").ap()
    if "dump_negm" in DBG:
        DBG_DRAM2[0] = nc.dram_tensor("dbgnegm", [128, 2128], F32, kind="ExternalOutput").ap()
    if "dumph" in DBG:
        DBG_DUMP[0] = nc.dram_tensor("dbgout", [1024, 2048], F32, kind="ExternalOutput").ap()
    scr2 = nc.dram_tensor("scr2", [16, 1024], BF16, kind="Internal").ap()

    need_bf = ("l1a" in stages) or ("l1f" in stages)
    wshapes = {"l0wq": (1024, 1024), "l0wk": (1024, 256), "l0wv": (1024, 256),
               "l0wo": (1024, 1024), "l0fw": (1024, 1024),
               "l1wq": (1024, 1024), "l1wk": (1024, 1024), "l1wv": (1024, 1024),
               "l1wo": (1024, 1024), "l1w1": (1024, 4096), "l1w2": (4096, 1024)}
    w16 = {}
    for nm, shp in wshapes.items():
        if nm in wd:
            w16[nm] = nc.dram_tensor(nm + "_16", list(shp), BF16, kind="Internal").ap()
    with ExitStack() as stack:
        blk = stack.enter_context(nc.Block())
        ctx = Ctx(nc, blk, stack)
        consts = _load_consts(ctx, cts)
        _fence(ctx)
        for nm in w16:
            emit_weight_prepass(ctx, wd[nm], w16[nm], *wshapes[nm], nm)
        for nm in w16:
            wd[nm] = w16[nm]
        xT = stack.enter_context(nc.sbuf_tensor(_u("xT"), [128, 8, 2048], F32))
        for j in range(BPC):
            with ExitStack() as bst:
                emit_load_stream(ctx, xin[j], xT, None)
                if "l0a" in stages:
                    wd2 = dict(wd)
                    wd2["xro"] = xro[j]
                    emit_l0_attn(ctx, consts, wd2, tabs, xT, scr)
                if "l0f" in stages:
                    emit_l0_ffn(ctx, consts, wd, xT)
                xbf = None
                if need_bf:
                    xbf = bst.enter_context(nc.sbuf_tensor(_u("xbf"), [128, 8, 2048], BF16))
                if "l1a" in stages:
                    emit_refresh_xbf(ctx, xT, xbf)
                    emit_l1_attn(ctx, consts, wd, xT, xbf, scr)
                if "l1f" in stages:
                    emit_refresh_xbf(ctx, xT, xbf)
                    emit_l1_ffn(ctx, consts, wd, xT, xbf, scr2)
                emit_store_stream(ctx, xT, xout[j])
    return nc


def _host_prep(inputs):
    d = {}
    d["l0wq"] = inputs["L0_attn_W_q"].T
    d["l0wk"] = inputs["L0_attn_W_k"].T
    d["l0wv"] = inputs["L0_attn_W_v"].T
    d["l0wo"] = inputs["L0_attn_W_o"].T
    d["l0fw"] = inputs["L0_ffn_weight_fp8"].T
    d["l0fb"] = inputs["L0_ffn_bias"].reshape(8, 128).T
    d["l0sc"] = np.stack([np.asarray(inputs["L0_ffn_input_scale"]).reshape(()),
                          np.asarray(inputs["L0_ffn_weight_scale"]).reshape(())]).reshape(1, 2)
    d["l1wq"] = inputs["L1_attn_Wq"].T
    d["l1wk"] = inputs["L1_attn_Wk"].T
    d["l1wv"] = inputs["L1_attn_Wv"].T
    d["l1wo"] = inputs["L1_attn_Wo"].T
    d["l1bq"] = inputs["L1_attn_bq"].reshape(8, 128).T
    d["l1bk"] = inputs["L1_attn_bk"].reshape(8, 128).T
    d["l1bv_row"] = inputs["L1_attn_bv"].reshape(8, 128)
    d["l1bo"] = inputs["L1_attn_bo"].reshape(8, 128).T
    d["l1w1"] = inputs["L1_ffn_W1"].T
    d["l1b1"] = inputs["L1_ffn_b1"].reshape(32, 128).T
    d["l1w2"] = inputs["L1_ffn_W2"].T
    d["l1b2"] = inputs["L1_ffn_b2"].reshape(8, 128).T
    d["l1sc"] = np.asarray(inputs["L1_ffn_scale"], np.float32).reshape(1, 1)
    f = np.arange(512, dtype=np.float64) / 512.0
    theta = 1.0 / (10000.0 ** f)
    ang = theta[:, None] * np.arange(S, dtype=np.float64)[None, :]
    for nm, fn in (("cos", np.cos), ("sin", np.sin)):
        v = fn(ang)
        hi = v.astype(np.float32)
        lo = (v - hi.astype(np.float64)).astype(np.float32)
        d[nm + "hi"] = hi
        d[nm + "lo"] = lo
    return {k: np.ascontiguousarray(v, dtype=np.float32) for k, v in d.items()}


def _get_program(stages):
    key = tuple(stages)
    if key not in _PROGRAM_CACHE:
        _PROGRAM_CACHE[key] = build_program(stages)
    return _PROGRAM_CACHE[key]


def _prog_input_names(nc):
    names = set()
    for a in nc.m.functions[0].allocations:
        if isinstance(a, mybir.MemoryLocationSet) and a.kind == "ExternalInput":
            for ml in a.memorylocations:
                names.add(ml.name)
    return names


def run_stage_group(stages, shared, streams, xros):
    nc = _get_program(stages)
    names = _prog_input_names(nc)
    in_maps = []
    for c in range(NC_N):
        m = {k: v for k, v in shared.items() if k in names}
        for j in range(BPC):
            m[f"xin{j}"] = streams[c][j]
            if f"xro{j}" in names:
                m[f"xro{j}"] = xros[c][j]
        in_maps.append(m)
    res = run_bass_kernel_spmd(nc, in_maps, core_ids=list(range(NC_N)))
    return [[np.asarray(res.results[c][f"xout{j}"]) for j in range(BPC)] for c in range(NC_N)]


STAGE_GROUPS = [("l0a", "l0f", "l1a", "l1f")]


def kernel(**inputs):
    shared = _host_prep(inputs)
    x = np.asarray(inputs["L0_attn_X"], np.float32)
    streams, xros = [], []
    for c in range(NC_N):
        ss, rr = [], []
        for j in range(BPC):
            xb = x[c * BPC + j]
            ss.append(np.ascontiguousarray(xb.T))
            rr.append(np.ascontiguousarray(
                np.concatenate([xb[:, 0::2], xb[:, 1::2]], axis=1).T))
        streams.append(ss)
        xros.append(rr)
    for stages in STAGE_GROUPS:
        streams = run_stage_group(stages, shared, streams, xros)
    out = np.empty((B, S, D), np.float32)
    for c in range(NC_N):
        for j in range(BPC):
            out[c * BPC + j] = streams[c][j].T
    return out


# version 09
# speedup vs baseline: 1.5368x; 1.2603x over previous
import numpy as np
import ml_dtypes
import concourse.bass as bass
import concourse.mybir as mybir
from concourse.bass_utils import run_bass_kernel_spmd
from contextlib import ExitStack

F32 = mybir.dt.float32
BF16 = mybir.dt.bfloat16
I32 = mybir.dt.int32
AF = mybir.ActivationFunctionType
ALU = mybir.AluOpType
AXL = mybir.AxisListType

B, S, D = 16, 2048, 1024
NC_N = 8
BPC = B // NC_N
NCH = D // 128
NKC = S // 128
NTG = S // 512
NBIG = -3.0e38

_PROGRAM_CACHE = {}
DBG = set()
DBG_DUMP = [None]
DBG_ATT = [None]
DBG_DRAM = [None]
DBG_ATT2 = [None]
DBG_DRAM2 = [None]

_NAME_SEQ = [0]


def _u(name):
    _NAME_SEQ[0] += 1
    return f"{name}_{_NAME_SEQ[0]}"




class Ctx:
    def __init__(self, nc, block, stack):
        self.nc = nc
        self.block = block
        self.stack = stack
        self.sems = {}
        self.cnt = {}
        for name in ("d", "p", "v", "a", "g", "t", "o", "s",
                     "w0", "w1", "w2", "w3"):
            self.sems[name] = stack.enter_context(nc.semaphore("sem_" + name))
            self.cnt[name] = 0

    def inc(self, inst, name, n=1):
        inst.then_inc(self.sems[name], n)
        self.cnt[name] += n
        return inst

    def dma(self, eng, out, in_, name="d"):
        return self.inc(eng.dma_start(out=out, in_=in_), name, 16)

    def wait(self, eng, name):
        eng.wait_ge(self.sems[name], self.cnt[name])

    def wait_all(self, eng):
        for name in ("d", "p", "v", "a", "g", "t", "o", "s",
                     "w0", "w1", "w2", "w3"):
            if self.cnt[name]:
                eng.wait_ge(self.sems[name], self.cnt[name])


def _fence(ctx):
    block = ctx.block

    @block.tensor
    def _(e):
        ctx.inc(e.drain(), "p")

    @block.vector
    def _(e):
        ctx.inc(e.drain(), "v")

    @block.scalar
    def _(e):
        ctx.inc(e.drain(), "a")

    @block.gpsimd
    def _(e):
        ctx.inc(e.drain(), "g")

    @block.sync
    def _(e):
        ctx.inc(e.drain(), "s")

    for emit in (block.tensor, block.vector, block.scalar, block.gpsimd, block.sync):
        @emit
        def _(e):
            ctx.wait_all(e)


def _make_consts(nc):
    c = {}
    c["ident"] = nc.inline_tensor(np.eye(128, dtype=ml_dtypes.bfloat16), "c_ident")
    c["ones_row_f"] = nc.inline_tensor(np.ones((1, 128), np.float32), "c_onesrf")
    c["ones_col_f"] = nc.inline_tensor(np.ones((128, 1), np.float32), "c_onescf")
    mask = np.triu(np.ones((128, 128), np.float32), k=1) * (-1e9)
    c["causal"] = nc.inline_tensor(mask, "c_causal")
    c["expmask"] = nc.inline_tensor(np.full((128, 1), 0x7F800000, np.int32), "c_expmask")
    return c


def _load_consts(ctx, cts):
    nc, block = ctx.nc, ctx.block
    sb = {}
    sb["ident"] = ctx.stack.enter_context(nc.sbuf_tensor(_u("cs_ident"), [128, 128], BF16))
    sb["ones_row_f"] = ctx.stack.enter_context(nc.sbuf_tensor(_u("cs_onesrf"), [1, 128], F32))
    sb["ones_col_f"] = ctx.stack.enter_context(nc.sbuf_tensor(_u("cs_onescf"), [128, 1], F32))
    sb["causal"] = ctx.stack.enter_context(nc.sbuf_tensor(_u("cs_causal"), [128, 128], F32))
    sb["expmask"] = ctx.stack.enter_context(nc.sbuf_tensor(_u("cs_expmask"), [128, 1], I32))
    sb["eps"] = ctx.stack.enter_context(nc.sbuf_tensor(_u("cs_eps"), [128, 1], F32))

    @block.gpsimd
    def _(g):
        for k in ("ident", "ones_row_f", "ones_col_f", "causal", "expmask"):
            ctx.dma(g, sb[k][:, :], cts[k].ap()[:, :])

    @block.vector
    def _(v):
        v.memset(sb["eps"][:, :], 1e-5)
        ctx.inc(v.drain(), "v")
    return sb


def emit_weight_prepass(ctx, w_dram, w16_dram, nin, nout, name):
    nc, block = ctx.nc, ctx.block
    sub = min(nout, 512)
    slices = [(ic, s0) for ic in range(nin // 128) for s0 in range(0, nout, sub)]
    with nc.sbuf_tensor(_u(name + "_pf"), [128, 2, sub], F32) as pf, \
         nc.sbuf_tensor(_u(name + "_pb"), [128, 2, sub], BF16) as pb:
        @block.gpsimd
        def _(g):
            ctx.wait_all(g)
            casts = [None] * len(slices)
            stores = [None] * len(slices)

            def cast(ci):
                slot = ci % 2
                g.wait_ge(ctx.sems["w%d" % slot], ctx.cnt["w%d" % slot])
                if ci >= 2:
                    g.wait_ge(ctx.sems["w%d" % (2 + slot)], stores[ci - 2])
                i = g.tensor_copy(out=pb[:, slot, :], in_=pf[:, slot, :])
                ctx.inc(i, "g")
                casts[ci] = ctx.cnt["g"]

            def store(ci):
                ic, s0 = slices[ci]
                slot = ci % 2
                g.wait_ge(ctx.sems["g"], casts[ci])
                i = g.dma_start(out=w16_dram[ic * 128:(ic + 1) * 128, s0:s0 + sub],
                                in_=pb[:, slot, :])
                ctx.inc(i, "w%d" % (2 + slot), 16)
                stores[ci] = ctx.cnt["w%d" % (2 + slot)]

            for si, (ic, s0) in enumerate(slices):
                slot = si % 2
                if si >= 2:
                    g.wait_ge(ctx.sems["g"], casts[si - 2])
                i = g.dma_start(out=pf[:, slot, :],
                                in_=w_dram[ic * 128:(ic + 1) * 128, s0:s0 + sub])
                ctx.inc(i, "w%d" % slot, 16)
                if si >= 1:
                    cast(si - 1)
                    store(si - 1)
            cast(len(slices) - 1)
            store(len(slices) - 1)
            for slot in (2, 3):
                g.wait_ge(ctx.sems["w%d" % slot], ctx.cnt["w%d" % slot])
            ctx.inc(g.drain(), "g")
        _fence(ctx)


def load_weight_bf16(ctx, w16_dram, nin, c0, c1, name):
    nc, block = ctx.nc, ctx.block
    nchk = nin // 128
    ncol = c1 - c0
    wbf = ctx.stack.enter_context(nc.sbuf_tensor(_u(name + "_bf"), [128, nchk, ncol], BF16))

    @block.gpsimd
    def _(g):
        ctx.wait_all(g)
        for ic in range(nchk):
            ctx.dma(g, wbf[:, ic, :], w16_dram[ic * 128:(ic + 1) * 128, c0:c1])
        ctx.wait(g, "d")
        ctx.inc(g.drain(), "g")
    _fence(ctx)
    return wbf


def emit_proj_featmajor(ctx, wbf, nin, nout, src_bf, evac, psname, oc0=0):
    nc, block = ctx.nc, ctx.block
    nic = nin // 128
    noc = nout // 128
    with nc.psum_tensor(_u(psname), [128, 2, 512], F32) as ps:
        for oc_ in range(noc):
            for tg in range(NTG):
                par = (oc_ * NTG + tg) % 2

                @block.tensor
                def _(te, oc=oc_, tg=tg, par=par):
                    if oc == 0 and tg == 0:
                        ctx.wait_all(te)
                    if oc * NTG + tg >= 2:
                        ctx.wait(te, "v")
                    for ic in range(nic):
                        i = te.matmul(ps[:, par, :], wbf[:, ic, oc * 128:(oc + 1) * 128],
                                      src_bf[:, ic, tg * 512:(tg + 1) * 512],
                                      start=(ic == 0), stop=(ic == nic - 1))
                    ctx.inc(i, "p")

                @block.vector
                def _(v, oc=oc_, tg=tg, par=par):
                    ctx.wait(v, "p")
                    evac(v, ps[:, par, :], oc0 + oc, tg)
                    ctx.inc(v.drain(), "v")
    _fence(ctx)


def emit_load_stream(ctx, x_dram, xT, xbf=None):
    nc, block = ctx.nc, ctx.block

    @block.gpsimd
    def _(g):
        ctx.wait_all(g)
        for c in range(NCH):
            ctx.dma(g, xT[:, c, :], x_dram[c * 128:(c + 1) * 128, :])
        if xbf is not None:
            ctx.wait(g, "d")
            for c in range(NCH):
                g.tensor_copy(out=xbf[:, c, :], in_=xT[:, c, :])
            ctx.inc(g.drain(), "g")
    _fence(ctx)


def emit_refresh_xbf(ctx, xT, xbf):
    @ctx.block.gpsimd
    def _(g):
        ctx.wait_all(g)
        for c in range(NCH):
            g.tensor_copy(out=xbf[:, c, :], in_=xT[:, c, :])
        ctx.inc(g.drain(), "g")
    _fence(ctx)


def emit_store_stream(ctx, xT, y_dram):
    nc, block = ctx.nc, ctx.block

    @block.gpsimd
    def _(g):
        ctx.wait_all(g)
        for c in range(NCH):
            ctx.dma(g, y_dram[c * 128:(c + 1) * 128, :], xT[:, c, :], name="o")
        ctx.wait(g, "o")
    _fence(ctx)


def emit_rope(ctx, xro_dram, tabs, xrbf):
    nc, block = ctx.nc, ctx.block
    with ExitStack() as st:
        xe = st.enter_context(nc.sbuf_tensor(_u("rp_xe"), [128, 2048], F32))
        xo = st.enter_context(nc.sbuf_tensor(_u("rp_xo"), [128, 2048], F32))
        tt = {k: st.enter_context(nc.sbuf_tensor(_u("rp_" + k), [128, 2048], F32))
              for k in ("ch", "cl", "sh", "sl")}
        m1 = st.enter_context(nc.sbuf_tensor(_u("rp_m1"), [128, 2048], F32))
        m2 = st.enter_context(nc.sbuf_tensor(_u("rp_m2"), [128, 2048], F32))
        p1 = st.enter_context(nc.sbuf_tensor(_u("rp_p1"), [128, 2048], F32))
        p2 = st.enter_context(nc.sbuf_tensor(_u("rp_p2"), [128, 2048], F32))
        for c in range(4):
            @block.gpsimd
            def _(g, c=c):
                if c == 0:
                    ctx.wait_all(g)
                else:
                    ctx.wait(g, "v")
                ctx.dma(g, xe[:, :], xro_dram[c * 128:(c + 1) * 128, :])
                ctx.dma(g, xo[:, :], xro_dram[512 + c * 128:512 + (c + 1) * 128, :])
                for k, nm in (("ch", "coshi"), ("cl", "coslo"), ("sh", "sinhi"), ("sl", "sinlo")):
                    ctx.dma(g, tt[k][:, :], tabs[nm][c * 128:(c + 1) * 128, :])

            @block.vector
            def _(v, c=c):
                ctx.wait(v, "d")
                v.tensor_tensor(out=m1[:, :], in0=xe[:, :], in1=tt["ch"][:, :], op=ALU.mult)
                v.tensor_tensor(out=m2[:, :], in0=xo[:, :], in1=tt["sh"][:, :], op=ALU.mult)
                v.tensor_tensor(out=p1[:, :], in0=m1[:, :], in1=m2[:, :], op=ALU.subtract)
                v.tensor_tensor(out=m1[:, :], in0=xe[:, :], in1=tt["cl"][:, :], op=ALU.mult)
                v.tensor_tensor(out=m2[:, :], in0=xo[:, :], in1=tt["sl"][:, :], op=ALU.mult)
                v.tensor_tensor(out=p2[:, :], in0=m1[:, :], in1=m2[:, :], op=ALU.subtract)
                v.tensor_tensor(out=p1[:, :], in0=p1[:, :], in1=p2[:, :], op=ALU.add)
                v.tensor_copy(out=xrbf[:, c, :], in_=p1[:, :])
                v.tensor_tensor(out=m1[:, :], in0=xe[:, :], in1=tt["sh"][:, :], op=ALU.mult)
                v.tensor_tensor(out=m2[:, :], in0=xo[:, :], in1=tt["ch"][:, :], op=ALU.mult)
                v.tensor_tensor(out=p1[:, :], in0=m1[:, :], in1=m2[:, :], op=ALU.add)
                v.tensor_tensor(out=m1[:, :], in0=xe[:, :], in1=tt["sl"][:, :], op=ALU.mult)
                v.tensor_tensor(out=m2[:, :], in0=xo[:, :], in1=tt["cl"][:, :], op=ALU.mult)
                v.tensor_tensor(out=p2[:, :], in0=m1[:, :], in1=m2[:, :], op=ALU.add)
                v.tensor_tensor(out=p1[:, :], in0=p1[:, :], in1=p2[:, :], op=ALU.add)
                v.tensor_copy(out=xrbf[:, 4 + c, :], in_=p1[:, :])
                ctx.inc(v.drain(), "v")
    _fence(ctx)


def emit_layernorm_inplace(ctx, consts, xT):
    nc, block = ctx.nc, ctx.block
    with ExitStack() as st:
        sq = st.enter_context(nc.sbuf_tensor(_u("ln_sq"), [128, 1024], F32))
        stats = st.enter_context(nc.sbuf_tensor(_u("ln_stats"), [1, 4, 1024], F32))
        bc = st.enter_context(nc.sbuf_tensor(_u("ln_bc"), [128, 2, 1024], F32))
        ps_mu = st.enter_context(nc.psum_tensor(_u("ln_psmu"), [1, 1024], F32))
        ps_sq = st.enter_context(nc.psum_tensor(_u("ln_pssq"), [1, 1024], F32))
        ps_bc = st.enter_context(nc.psum_tensor(_u("ln_psbc"), [128, 2, 1024], F32))
        for th in range(2):
            t0 = th * 1024

            for c in range(NCH):
                @block.scalar
                def _(a, th=th, t0=t0, c=c):
                    if th == 0 and c == 0:
                        ctx.wait_all(a)
                    elif c == 0:
                        ctx.wait(a, "v")
                    else:
                        ctx.wait(a, "p")
                    i = a.activation(out=sq[:, :], in_=xT[:, c, t0:t0 + 1024],
                                     func=AF.Square, bias=0.0, scale=1.0)
                    ctx.inc(i, "a")

                @block.tensor
                def _(te, th=th, t0=t0, c=c):
                    ctx.wait(te, "a")
                    for g2 in range(2):
                        te.matmul(ps_mu[:, g2 * 512:(g2 + 1) * 512], consts["ones_col_f"][:, :],
                                  xT[:, c, t0 + g2 * 512:t0 + (g2 + 1) * 512],
                                  start=(c == 0), stop=(c == NCH - 1))
                        i = te.matmul(ps_sq[:, g2 * 512:(g2 + 1) * 512], consts["ones_col_f"][:, :],
                                      sq[:, g2 * 512:(g2 + 1) * 512],
                                      start=(c == 0), stop=(c == NCH - 1))
                    ctx.inc(i, "p")

            @block.vector
            def _(v, th=th):
                ctx.wait(v, "p")
                v.tensor_scalar_mul(stats[0:1, 0, :], ps_mu[:, :], 1.0 / D)
                v.tensor_scalar_mul(stats[0:1, 1, :], ps_sq[:, :], 1.0 / D)
                v.tensor_tensor(out=stats[0:1, 3, :], in0=stats[0:1, 0, :],
                                in1=stats[0:1, 0, :], op=ALU.mult)
                v.tensor_tensor(out=stats[0:1, 1, :], in0=stats[0:1, 1, :],
                                in1=stats[0:1, 3, :], op=ALU.subtract)
                ctx.inc(v.drain(), "v")

            @block.scalar
            def _(a, th=th):
                ctx.wait(a, "v")
                i = a.activation(out=stats[0:1, 3, :], in_=stats[0:1, 1, :],
                                 func=AF.Sqrt, bias=consts["eps"][0:1, 0:1], scale=1.0)
                ctx.inc(i, "a")

            @block.vector
            def _(v, th=th):
                ctx.wait(v, "a")
                v.reciprocal(out=stats[0:1, 1, :], in_=stats[0:1, 3, :])
                v.tensor_tensor(out=stats[0:1, 2, :], in0=stats[0:1, 0, :],
                                in1=stats[0:1, 1, :], op=ALU.mult)
                v.tensor_scalar_mul(stats[0:1, 2, :], stats[0:1, 2, :], -1.0)
                ctx.inc(v.drain(), "v")

            @block.tensor
            def _(te, th=th):
                ctx.wait(te, "v")
                for g2 in range(2):
                    te.matmul(ps_bc[:, 0, g2 * 512:(g2 + 1) * 512], consts["ones_row_f"][0:1, :],
                              stats[0:1, 1, g2 * 512:(g2 + 1) * 512], start=True, stop=True)
                    i = te.matmul(ps_bc[:, 1, g2 * 512:(g2 + 1) * 512], consts["ones_row_f"][0:1, :],
                                  stats[0:1, 2, g2 * 512:(g2 + 1) * 512], start=True, stop=True)
                ctx.inc(i, "p")

            @block.vector
            def _(v, th=th, t0=t0):
                ctx.wait(v, "p")
                v.tensor_copy(out=bc[:, 0, :], in_=ps_bc[:, 0, :])
                v.tensor_copy(out=bc[:, 1, :], in_=ps_bc[:, 1, :])
                for c in range(NCH):
                    v.tensor_tensor(out=xT[:, c, t0:t0 + 1024], in0=xT[:, c, t0:t0 + 1024],
                                    in1=bc[:, 0, :], op=ALU.mult)
                    v.tensor_tensor(out=xT[:, c, t0:t0 + 1024], in0=xT[:, c, t0:t0 + 1024],
                                    in1=bc[:, 1, :], op=ALU.add)
                ctx.inc(v.drain(), "v")
    _fence(ctx)


def emit_negm(ctx, consts, qt_bf, kt_bf, layer, negm_bc, scr_dram, tag):
    nc, block = ctx.nc, ctx.block
    kch = 4 if layer == 0 else 8
    with ExitStack() as st:
        mq = st.enter_context(nc.sbuf_tensor(_u(tag + "_mq"), [128, 8], F32))
        mk = st.enter_context(nc.sbuf_tensor(_u(tag + "_mk"), [128, 8], F32))
        flat = st.enter_context(nc.sbuf_tensor(_u(tag + "_flat"), [1, 2048], F32))
        mrow = st.enter_context(nc.sbuf_tensor(_u(tag + "_mrow"), [1, 3, 16], F32))
        ps = st.enter_context(nc.psum_tensor(_u(tag + "_ps"), [128, 16], F32))
        if "dump_negm" in DBG and tag == "l0nm":
            DBG_ATT2[0] = st.enter_context(nc.sbuf_tensor(_u("nm_dbg"), [128, 2128], F32))

        @block.vector
        def _(v):
            ctx.wait_all(v)
            v.memset(mk[:, :], 0.0)
            for c in range(NCH):
                v.reduce_max(out=mq[:, c:c + 1], in_=qt_bf[:, c, :], axis=AXL.X,
                             apply_absolute_value=True)
            for c in range(kch):
                v.reduce_max(out=mk[:, c:c + 1], in_=kt_bf[:, c, :], axis=AXL.X,
                             apply_absolute_value=True)
            ctx.inc(v.drain(), "v")

        @block.gpsimd
        def _(g):
            ctx.wait(g, "v")
            ctx.dma(g, scr_dram[0:128, :], mq[:, :])
            ctx.dma(g, scr_dram[128:256, :], mk[:, :])

        @block.gpsimd
        def _(g):
            ctx.wait(g, "d")
            ctx.dma(g, flat[0:1, :], scr_dram.rearrange("(o p) c -> o (p c)", o=1)[0:1, :])

        @block.vector
        def _(v):
            ctx.wait(v, "d")
            for h in range(16):
                st_q = 64 * (h % 2) * 8 + (h // 2)
                v.reduce_max(out=mrow[0:1, 0, h:h + 1],
                             in_=flat[0:1, st_q:st_q + 505:8], axis=AXL.X)
            for h in range(16):
                if layer == 0:
                    gk = h // 4
                    st_k = 1024 + gk
                    v.reduce_max(out=mrow[0:1, 1, h:h + 1],
                                 in_=flat[0:1, st_k:st_k + 505:8], axis=AXL.X)
                else:
                    st_k = 1024 + 64 * (h % 2) * 8 + (h // 2)
                    v.reduce_max(out=mrow[0:1, 1, h:h + 1],
                                 in_=flat[0:1, st_k:st_k + 505:8], axis=AXL.X)
            ctx.inc(v.drain(), "v")
            ctx.wait(v, "v")
            v.scalar_tensor_tensor(out=mrow[0:1, 2, :], in0=mrow[0:1, 0, :],
                                   scalar=-64.0, in1=mrow[0:1, 1, :],
                                   op0=ALU.mult, op1=ALU.mult)
            ctx.inc(v.drain(), "v")

        @block.tensor
        def _(te):
            ctx.wait(te, "v")
            i = te.matmul(ps[:, :], consts["ones_row_f"][0:1, :], mrow[0:1, 2, :],
                          start=True, stop=True)
            ctx.inc(i, "p")

        @block.vector
        def _(v):
            ctx.wait(v, "p")
            v.tensor_copy(out=negm_bc[:, :], in_=ps[:, :])
            if "dump_negm" in DBG and tag == "l0nm":
                v.tensor_copy(out=DBG_ATT2[0][:, 0:8], in_=mq[:, :])
                v.tensor_copy(out=DBG_ATT2[0][:, 8:16], in_=mk[:, :])
                v.tensor_copy(out=DBG_ATT2[0][0:1, 16:2064], in_=flat[0:1, :])
                v.tensor_copy(out=DBG_ATT2[0][0:1, 2064:2112], in_=mrow[0:1, :, :])
                v.tensor_copy(out=DBG_ATT2[0][:, 2112:2128], in_=negm_bc[:, :])
            ctx.inc(v.drain(), "v")

        if "dump_negm" in DBG and tag == "l0nm":
            @block.gpsimd
            def _(g):
                ctx.wait(g, "v")
                ctx.dma(g, DBG_DRAM2[0][:, :], DBG_ATT2[0][:, :], name="o")
                ctx.wait(g, "o")
    _fence(ctx)


def emit_attention(ctx, consts, layer, qt_bf, kt_bf, v_bf, negm_bc, ctx_bf):
    nc, block = ctx.nc, ctx.block
    units = [(h, qi) for h in range(16) for qi in range(NKC)]
    nun = len(units)
    marks = {k: [0] * nun for k in ("S", "RM", "EXP", "SC", "TR", "AV", "EV")}

    def live_of(qi):
        return (qi + 1) if layer == 0 else NKC

    with ExitStack() as st:
        E = st.enter_context(nc.sbuf_tensor(_u("at_E"), [128, 2048], BF16))
        W = st.enter_context(nc.sbuf_tensor(_u("at_W"), [128, 2048], BF16))
        PT = st.enter_context(nc.sbuf_tensor(_u("at_PT"), [128, 16, 128], BF16))
        m8 = st.enter_context(nc.sbuf_tensor(_u("at_m8"), [128, 8], F32))
        rs = st.enter_context(nc.sbuf_tensor(_u("at_rs"), [128, 2], F32))
        rc = st.enter_context(nc.sbuf_tensor(_u("at_rc"), [128, 1], F32))
        rm = st.enter_context(nc.sbuf_tensor(_u("at_rm"), [128, 3], F32))
        ps_s = st.enter_context(nc.psum_tensor(_u("at_pss"), [128, 2, 1024], F32))
        ps_c = st.enter_context(nc.psum_tensor(_u("at_psc"), [128, 128], F32))

        def emit_scores(u):
            h, qi = units[u]
            live = live_of(qi)
            qch, qba = h // 2, 64 * (h % 2)
            grp = h // 4

            @block.tensor
            def _(te):
                if u == 0:
                    ctx.wait_all(te)
                else:
                    te.wait_ge(ctx.sems["a"], marks["EXP"][u - 1])
                lhsT = qt_bf[qba:qba + 64, qch, 128 * qi:128 * (qi + 1)]
                rhs_t = kt_bf[qba:qba + 64, grp if layer == 0 else qch, :]
                for half in range(2):
                    k0 = half * 8
                    nliv = min(8, max(0, live - k0))
                    if nliv == 0:
                        continue
                    for nb in range((nliv * 128 + 511) // 512):
                        n0 = k0 * 128 + nb * 512
                        nn = min(512, (k0 + nliv) * 128 - n0)
                        i = te.matmul(ps_s[:, half, nb * 512:nb * 512 + nn],
                                      lhsT, rhs_t[:, n0:n0 + nn],
                                      start=True, stop=True, tile_position=(qba, 0))
                ctx.inc(i, "p")
                marks["S"][u] = ctx.cnt["p"]

        def emit_av(u):
            h, qi = units[u]
            live = live_of(qi)
            qba = 64 * (h % 2)
            grp = h // 4

            @block.tensor
            def _(te):
                te.wait_ge(ctx.sems["t"], marks["TR"][u])
                if u > 0:
                    te.wait_ge(ctx.sems["a"], marks["EV"][u - 1])
                for c in range(live):
                    va = v_bf[:, c, 64 * (grp if layer == 0 else h):
                              64 * (grp if layer == 0 else h) + 64]
                    i = te.matmul(ps_c[qba:qba + 64, :], va, PT[:, c, :],
                                  start=(c == 0), stop=(c == live - 1),
                                  tile_position=(0, qba), skip_group_check=True)
                ctx.inc(i, "p")
                marks["AV"][u] = ctx.cnt["p"]

        emit_scores(0)
        for u in range(nun):
            h, qi = units[u]
            live = live_of(qi)
            nlive = live * 128

            @block.vector
            def _(v, u=u, h=h, qi=qi, live=live):
                v.wait_ge(ctx.sems["p"], marks["S"][u])
                if layer == 0:
                    half, cp = qi // 8, (qi % 8) * 128
                    v.tensor_tensor(out=ps_s[:, half, cp:cp + 128],
                                    in0=ps_s[:, half, cp:cp + 128],
                                    in1=consts["causal"][:, :], op=ALU.add)
                for half in range(2):
                    nliv = min(8, max(0, live - half * 8))
                    if nliv == 0:
                        continue
                    v.reduce_max(out=rm[:, half:half + 1],
                                 in_=ps_s[:, half, 0:nliv * 128],
                                 axis=AXL.X, negate=True)
                i = v.drain()
                ctx.inc(i, "v")
                ctx.wait(v, "v")
                if live > 8:
                    i = v.tensor_tensor(out=rm[:, 2:3], in0=rm[:, 0:1],
                                        in1=rm[:, 1:2], op=ALU.min)
                else:
                    i = v.tensor_copy(out=rm[:, 2:3], in_=rm[:, 0:1])
                ctx.inc(i, "v")
                marks["RM"][u] = ctx.cnt["v"]

            @block.scalar
            def _(a, u=u, h=h, live=live):
                a.wait_ge(ctx.sems["v"], marks["RM"][u])
                if u > 0:
                    a.wait_ge(ctx.sems["t"], marks["TR"][u - 1])
                for half in range(2):
                    k0 = half * 8
                    nliv = min(8, max(0, live - k0))
                    if nliv == 0:
                        continue
                    i = a.activation(out=E[:, k0 * 128:(k0 + nliv) * 128],
                                     in_=ps_s[:, half, 0:nliv * 128], func=AF.Exp,
                                     bias=rm[:, 2:3], scale=1.0,
                                     accum_out=rs[:, half:half + 1])
                ctx.inc(i, "a")
                marks["EXP"][u] = ctx.cnt["a"]

            if u > 0:
                h1, qi1 = units[u - 1]
                qba1 = 64 * (h1 % 2)

                @block.scalar
                def _(a, u=u, h1=h1, qi1=qi1, qba1=qba1):
                    a.wait_ge(ctx.sems["p"], marks["AV"][u - 1])
                    i = a.activation(out=ctx_bf[qba1:qba1 + 64, h1 // 2,
                                                128 * qi1:128 * (qi1 + 1)],
                                     in_=ps_c[qba1:qba1 + 64, :], func=AF.Copy,
                                     bias=0.0, scale=1.0)
                    ctx.inc(i, "a")
                    marks["EV"][u - 1] = ctx.cnt["a"]

            @block.vector
            def _(v, u=u, live=live, nlive=nlive):
                v.wait_ge(ctx.sems["a"], marks["EXP"][u])
                if layer == 1:
                    for rnd in range(5):
                        src_t = E if rnd == 0 else W
                        i = v.max(out=m8[:, :], in_=src_t[:, 0:nlive])
                        ctx.inc(i, "v")
                        ctx.wait(v, "v")
                        if rnd < 4:
                            v.scalar_tensor_tensor(out=W[:, 0:nlive],
                                                   in0=src_t[:, 0:nlive],
                                                   scalar=m8[:, 7:8],
                                                   in1=src_t[:, 0:nlive],
                                                   op0=ALU.is_lt, op1=ALU.mult)
                    i = v.scalar_tensor_tensor(out=E[:, 0:nlive], in0=E[:, 0:nlive],
                                               scalar=m8[:, 5:6], in1=E[:, 0:nlive],
                                               op0=ALU.is_ge, op1=ALU.mult,
                                               accum_out=rs[:, 0:1])
                    ctx.inc(i, "v")
                    ctx.wait(v, "v")
                elif live > 8:
                    i = v.tensor_tensor(out=rs[:, 0:1], in0=rs[:, 0:1],
                                        in1=rs[:, 1:2], op=ALU.add)
                    ctx.inc(i, "v")
                    ctx.wait(v, "v")
                i = v.reciprocal(out=rc[:, 0:1], in_=rs[:, 0:1])
                ctx.inc(i, "v")
                ctx.wait(v, "v")
                v.tensor_scalar_mul(E[:, 0:nlive], E[:, 0:nlive], rc[:, 0:1])
                i = v.drain()
                ctx.inc(i, "v")
                marks["SC"][u] = ctx.cnt["v"]

            @block.scalar
            def _(a, u=u, live=live):
                a.wait_ge(ctx.sems["v"], marks["SC"][u])
                if u > 0:
                    a.wait_ge(ctx.sems["p"], marks["AV"][u - 1])
                    a.wait_ge(ctx.sems["t"], marks["TR"][u - 1])
                i = a.dma_start(out=PT[:, 0:live, :], in_=E[:, 0:live * 128],
                                transpose=True)
                ctx.inc(i, "t", 16)
                marks["TR"][u] = ctx.cnt["t"]

            if u + 1 < nun:
                emit_scores(u + 1)
            emit_av(u)

        hL, qiL = units[-1]
        qbaL = 64 * (hL % 2)

        @block.scalar
        def _(a):
            a.wait_ge(ctx.sems["p"], marks["AV"][nun - 1])
            i = a.activation(out=ctx_bf[qbaL:qbaL + 64, hL // 2,
                                        128 * qiL:128 * (qiL + 1)],
                             in_=ps_c[qbaL:qbaL + 64, :], func=AF.Copy,
                             bias=0.0, scale=1.0)
            ctx.inc(i, "a")
            marks["EV"][nun - 1] = ctx.cnt["a"]
    _fence(ctx)


def emit_l0_attn(ctx, consts, wd, tabs, xT, scr_dram):
    nc, block = ctx.nc, ctx.block
    with ExitStack() as st:
        xr_bf = st.enter_context(nc.sbuf_tensor(_u("l0_xr"), [128, 8, 2048], BF16))
        emit_rope(ctx, wd["xro"], tabs, xr_bf)

        qt_bf = st.enter_context(nc.sbuf_tensor(_u("l0_qt"), [128, 8, 2048], BF16))
        ktd_bf = st.enter_context(nc.sbuf_tensor(_u("l0_kt"), [128, 4, 2048], BF16))
        v_bf = st.enter_context(nc.sbuf_tensor(_u("l0_v"), [128, 16, 256], BF16))

        with ExitStack() as st2:
            saved, ctx.stack = ctx.stack, st2
            wq = load_weight_bf16(ctx, wd["l0wq"], 1024, 0, 1024, "l0wq")

            def evq(v, psap, oc, tg):
                v.tensor_copy(out=qt_bf[:, oc, tg * 512:(tg + 1) * 512], in_=psap)
            emit_proj_featmajor(ctx, wq, 1024, 1024, xr_bf, evq, "l0psq")
            ctx.stack = saved

        with ExitStack() as st2:
            saved, ctx.stack = ctx.stack, st2
            wk = load_weight_bf16(ctx, wd["l0wk"], 1024, 0, 256, "l0wk")

            def evk(v, psap, oc, tg):
                for gg in range(2):
                    gidx = 2 * oc + gg
                    for hf in range(2):
                        v.tensor_scalar(out=ktd_bf[64 * hf:64 * hf + 64, gidx,
                                                   tg * 512:(tg + 1) * 512],
                                        in0=psap[64 * gg:64 * gg + 64, :], scalar1=0.125,
                                        scalar2=None, op0=ALU.mult)
            emit_proj_featmajor(ctx, wk, 1024, 256, xr_bf, evk, "l0psk")
            ctx.stack = saved

        with ExitStack() as st2:
            saved, ctx.stack = ctx.stack, st2
            wv = load_weight_bf16(ctx, wd["l0wv"], 1024, 0, 256, "l0wv")
            with nc.psum_tensor(_u("l0psv"), [128, 2, 512], F32) as psv:
                for tc in range(NKC):
                    par = tc % 2

                    @block.tensor
                    def _(te, tc=tc, par=par):
                        if tc == 0:
                            ctx.wait_all(te)
                        if tc >= 2:
                            ctx.wait(te, "v")
                        for ic in range(NCH):
                            i = te.matmul(psv[:, par, 0:256],
                                          xr_bf[:, ic, tc * 128:(tc + 1) * 128],
                                          wv[:, ic, :], start=(ic == 0), stop=(ic == NCH - 1))
                        ctx.inc(i, "p")

                    @block.vector
                    def _(v, tc=tc, par=par):
                        ctx.wait(v, "p")
                        v.tensor_copy(out=v_bf[:, tc, :], in_=psv[:, par, 0:256])
                        ctx.inc(v.drain(), "v")
            _fence(ctx)
            ctx.stack = saved

        emit_attention(ctx, consts, 0, qt_bf, ktd_bf, v_bf, None, xr_bf)

        with ExitStack() as st2:
            saved, ctx.stack = ctx.stack, st2
            wo = load_weight_bf16(ctx, wd["l0wo"], 1024, 0, 1024, "l0wo")

            def evo(v, psap, oc, tg):
                v.tensor_tensor(out=xT[:, oc, tg * 512:(tg + 1) * 512],
                                in0=psap, in1=xT[:, oc, tg * 512:(tg + 1) * 512], op=ALU.add)
            emit_proj_featmajor(ctx, wo, 1024, 1024, xr_bf, evo, "l0pso")
            ctx.stack = saved
    emit_layernorm_inplace(ctx, consts, xT)


def emit_l0_ffn(ctx, consts, wd, xT):
    nc, block = ctx.nc, ctx.block
    with ExitStack() as st:
        q_bf = st.enter_context(nc.sbuf_tensor(_u("f0_q"), [128, 8, 2048], BF16))
        sc2 = st.enter_context(nc.sbuf_tensor(_u("f0_sc2"), [128, 2], F32))
        scs = st.enter_context(nc.sbuf_tensor(_u("f0_scs"), [1, 4], F32))
        bias = st.enter_context(nc.sbuf_tensor(_u("f0_bias"), [128, 8], F32))

        @block.gpsimd
        def _(g):
            ctx.wait_all(g)
            ctx.dma(g, scs[0:1, 0:2], wd["l0sc"][0:1, 0:2])
            ctx.dma(g, bias[:, :], wd["l0fb"][:, :])

        @block.vector
        def _(v):
            ctx.wait(v, "d")
            i = v.reciprocal(out=scs[0:1, 2:3], in_=scs[0:1, 0:1])
            ctx.inc(i, "v")
            ctx.wait(v, "v")
            v.tensor_tensor(out=scs[0:1, 3:4], in0=scs[0:1, 0:1],
                            in1=scs[0:1, 1:2], op=ALU.mult)
            ctx.inc(v.drain(), "v")

        with nc.psum_tensor(_u("f0psb"), [128, 2], F32) as psb:
            @block.tensor
            def _(te):
                ctx.wait(te, "v")
                i = te.matmul(psb[:, :], consts["ones_row_f"][0:1, :], scs[0:1, 2:4],
                              start=True, stop=True)
                ctx.inc(i, "p")

            @block.vector
            def _(v):
                ctx.wait(v, "p")
                v.tensor_copy(out=sc2[:, :], in_=psb[:, :])
                ctx.inc(v.drain(), "v")
        _fence(ctx)

        with ExitStack() as stq:
            t = stq.enter_context(nc.sbuf_tensor(_u("f0_t"), [128, 2048], F32))
            sg = stq.enter_context(nc.sbuf_tensor(_u("f0_sg"), [128, 2048], F32))
            aa = stq.enter_context(nc.sbuf_tensor(_u("f0_a"), [128, 2048], F32))
            ib = stq.enter_context(nc.sbuf_tensor(_u("f0_ib"), [128, 2048], I32))
            sp = stq.enter_context(nc.sbuf_tensor(_u("f0_sp"), [128, 2048], F32))
            uu = stq.enter_context(nc.sbuf_tensor(_u("f0_u"), [128, 2048], F32))
            vv = stq.enter_context(nc.sbuf_tensor(_u("f0_v"), [128, 2048], F32))
            bb = stq.enter_context(nc.sbuf_tensor(_u("f0_b"), [128, 2048], F32))
            for c in range(NCH):
                @block.vector
                def _(v, c=c):
                    if c == 0:
                        ctx.wait_all(v)
                    else:
                        ctx.wait(v, "g")
                    v.tensor_scalar(out=t[:, :], in0=xT[:, c, :], scalar1=sc2[:, 0:1],
                                    scalar2=448.0, op0=ALU.mult, op1=ALU.min)
                    v.tensor_scalar_max(t[:, :], t[:, :], -448.0)
                    ctx.inc(v.drain(), "v")

                @block.scalar
                def _(a, c=c):
                    ctx.wait(a, "v")
                    ctx.wait(a, "g")
                    a.activation(out=sg[:, :], in_=t[:, :], func=AF.Sign, bias=0.0, scale=1.0)
                    i = a.activation(out=aa[:, :], in_=t[:, :], func=AF.Abs, bias=0.0, scale=1.0)
                    ctx.inc(i, "a")

                @block.vector
                def _(v, c=c):
                    ctx.wait(v, "a")
                    v.tensor_scalar(out=ib[:, :], in0=aa[:, :].bitcast(I32),
                                    scalar1=consts["expmask"][:, 0:1], scalar2=None,
                                    op0=ALU.bitwise_and)
                    v.tensor_scalar(out=sp[:, :], in0=ib[:, :].bitcast(F32),
                                    scalar1=0.125, scalar2=2.0 ** -9,
                                    op0=ALU.mult, op1=ALU.max)
                    v.reciprocal(out=uu[:, :], in_=sp[:, :])
                    v.tensor_tensor(out=uu[:, :], in0=aa[:, :], in1=uu[:, :], op=ALU.mult)
                    v.tensor_scalar_min(uu[:, :], uu[:, :], 16.0)
                    v.tensor_scalar(out=vv[:, :], in0=uu[:, :], scalar1=0.5,
                                    scalar2=None, op0=ALU.add)
                    for cbit in (8.0, 8.0, 4.0, 2.0, 1.0):
                        v.tensor_scalar(out=bb[:, :], in0=vv[:, :], scalar1=cbit,
                                        scalar2=None, op0=ALU.is_ge)
                        v.scalar_tensor_tensor(out=vv[:, :], in0=bb[:, :], scalar=-cbit,
                                               in1=vv[:, :], op0=ALU.mult, op1=ALU.add)
                    v.tensor_tensor(out=uu[:, :], in0=uu[:, :], in1=vv[:, :], op=ALU.subtract)
                    v.tensor_scalar(out=uu[:, :], in0=uu[:, :], scalar1=0.5,
                                    scalar2=None, op0=ALU.add)
                    v.tensor_tensor(out=uu[:, :], in0=uu[:, :], in1=sp[:, :], op=ALU.mult)
                    ctx.inc(v.drain(), "v")

                @block.gpsimd
                def _(g, c=c):
                    ctx.wait(g, "v")
                    g.tensor_tensor(out=q_bf[:, c, :], in0=uu[:, :], in1=sg[:, :], op=ALU.mult)
                    ctx.inc(g.drain(), "g")
        _fence(ctx)

        with ExitStack() as st2:
            saved, ctx.stack = ctx.stack, st2
            fw = load_weight_bf16(ctx, wd["l0fw"], 1024, 0, 1024, "l0fw")
            with nc.sbuf_tensor(_u("f0_ev"), [128, 512], F32) as ev:
                def evf(v, psap, oc, tg):
                    v.tensor_scalar(out=ev[:, :], in0=psap, scalar1=sc2[:, 1:2],
                                    scalar2=bias[:, oc:oc + 1], op0=ALU.mult, op1=ALU.add)
                    v.tensor_tensor(out=xT[:, oc, tg * 512:(tg + 1) * 512],
                                    in0=ev[:, :], in1=xT[:, oc, tg * 512:(tg + 1) * 512],
                                    op=ALU.add)
                emit_proj_featmajor(ctx, fw, 1024, 1024, q_bf, evf, "f0ps")
            ctx.stack = saved
    emit_layernorm_inplace(ctx, consts, xT)


def emit_l1_attn(ctx, consts, wd, xT, xbf, scr_dram):
    nc, block = ctx.nc, ctx.block
    with ExitStack() as st:
        qt_bf = st.enter_context(nc.sbuf_tensor(_u("l1_qt"), [128, 8, 2048], BF16))
        kt_bf = st.enter_context(nc.sbuf_tensor(_u("l1_kt"), [128, 8, 2048], BF16))
        v_bf = st.enter_context(nc.sbuf_tensor(_u("l1_v"), [128, 16, 1024], BF16))
        bq = st.enter_context(nc.sbuf_tensor(_u("l1_bq"), [128, 8], F32))
        bk = st.enter_context(nc.sbuf_tensor(_u("l1_bk"), [128, 8], F32))
        bo = st.enter_context(nc.sbuf_tensor(_u("l1_bo"), [128, 8], F32))
        bv_scope = ExitStack()
        bv_bc = bv_scope.enter_context(nc.sbuf_tensor(_u("l1_bvbc"), [128, 1024], BF16))
        bvr_scope = ExitStack()
        bvr = bvr_scope.enter_context(nc.sbuf_tensor(_u("l1_bvr"), [1, 1024], F32))

        @block.gpsimd
        def _(g):
            ctx.wait_all(g)
            ctx.dma(g, bq[:, :], wd["l1bq"][:, :])
            ctx.dma(g, bk[:, :], wd["l1bk"][:, :])
            ctx.dma(g, bo[:, :], wd["l1bo"][:, :])
            ctx.dma(g, bvr[0:1, :], wd["l1bv_row"].rearrange("(o a) b -> o (a b)", o=1)[0:1, :])

        with nc.psum_tensor(_u("l1psbv"), [128, 2, 512], F32) as psbv:
            @block.tensor
            def _(te):
                ctx.wait(te, "d")
                for half in range(2):
                    i = te.matmul(psbv[:, half, :], consts["ones_row_f"][0:1, :],
                                  bvr[0:1, half * 512:(half + 1) * 512],
                                  start=True, stop=True)
                ctx.inc(i, "p")

            @block.vector
            def _(v):
                ctx.wait(v, "p")
                v.tensor_copy(out=bv_bc[:, 0:512], in_=psbv[:, 0, :])
                v.tensor_copy(out=bv_bc[:, 512:1024], in_=psbv[:, 1, :])
                ctx.inc(v.drain(), "v")
        _fence(ctx)
        bvr_scope.close()

        def evq(v, psap, oc, tg):
            v.tensor_scalar(out=qt_bf[:, oc, tg * 512:(tg + 1) * 512], in0=psap,
                            scalar1=bq[:, oc:oc + 1], scalar2=None, op0=ALU.add)
        for half in range(2):
            with ExitStack() as st2:
                saved, ctx.stack = ctx.stack, st2
                wq = load_weight_bf16(ctx, wd["l1wq"], 1024, half * 512,
                                      (half + 1) * 512, "l1wq")
                emit_proj_featmajor(ctx, wq, 1024, 512, xbf, evq, "l1psq", oc0=half * 4)
                ctx.stack = saved

        def evk(v, psap, oc, tg):
            v.tensor_scalar(out=kt_bf[:, oc, tg * 512:(tg + 1) * 512], in0=psap,
                            scalar1=bk[:, oc:oc + 1], scalar2=0.125,
                            op0=ALU.add, op1=ALU.mult)
        for half in range(2):
            with ExitStack() as st2:
                saved, ctx.stack = ctx.stack, st2
                wk = load_weight_bf16(ctx, wd["l1wk"], 1024, half * 512,
                                      (half + 1) * 512, "l1wk")
                emit_proj_featmajor(ctx, wk, 1024, 512, xbf, evk, "l1psk", oc0=half * 4)
                ctx.stack = saved

        for vhalf in range(2):
          with ExitStack() as st2:
            saved, ctx.stack = ctx.stack, st2
            wv = load_weight_bf16(ctx, wd["l1wv"], 1024, vhalf * 512,
                                  (vhalf + 1) * 512, "l1wv")
            with nc.psum_tensor(_u("l1psv"), [128, 2, 512], F32) as psv:
                for tc in range(NKC):
                    for vh in [vhalf]:
                        par = tc % 2

                        @block.tensor
                        def _(te, tc=tc, vh=vh, par=par):
                            if tc == 0:
                                ctx.wait_all(te)
                            else:
                                ctx.wait(te, "v")
                            for ic in range(NCH):
                                i = te.matmul(psv[:, par, :],
                                              xbf[:, ic, tc * 128:(tc + 1) * 128],
                                              wv[:, ic, :],
                                              start=(ic == 0), stop=(ic == NCH - 1))
                            ctx.inc(i, "p")

                        @block.vector
                        def _(v, tc=tc, vh=vh, par=par):
                            ctx.wait(v, "p")
                            v.tensor_tensor(out=v_bf[:, tc, vh * 512:(vh + 1) * 512],
                                            in0=psv[:, par, :],
                                            in1=bv_bc[:, vh * 512:(vh + 1) * 512], op=ALU.add)
                            ctx.inc(v.drain(), "v")
            _fence(ctx)
            ctx.stack = saved

        bv_scope.close()
        emit_attention(ctx, consts, 1, qt_bf, kt_bf, v_bf, None, xbf)

        for half in range(2):
            with ExitStack() as st2:
                saved, ctx.stack = ctx.stack, st2
                wo = load_weight_bf16(ctx, wd["l1wo"], 1024, half * 512,
                                      (half + 1) * 512, "l1wo")
                with nc.sbuf_tensor(_u("l1_ev"), [128, 512], F32) as ev:
                    def evo(v, psap, oc, tg):
                        v.tensor_scalar(out=ev[:, :], in0=psap, scalar1=bo[:, oc:oc + 1],
                                        scalar2=None, op0=ALU.add)
                        v.tensor_tensor(out=xT[:, oc, tg * 512:(tg + 1) * 512],
                                        in0=ev[:, :],
                                        in1=xT[:, oc, tg * 512:(tg + 1) * 512],
                                        op=ALU.add)
                    emit_proj_featmajor(ctx, wo, 1024, 512, xbf, evo, "l1pso",
                                        oc0=half * 4)
                ctx.stack = saved
    emit_layernorm_inplace(ctx, consts, xT)


def emit_l1_ffn(ctx, consts, wd, xT, xbf, scr2):
    nc, block = ctx.nc, ctx.block
    with ExitStack() as st:
        b1 = st.enter_context(nc.sbuf_tensor(_u("f1_b1"), [128, 32], F32))
        b2 = st.enter_context(nc.sbuf_tensor(_u("f1_b2"), [128, 8], F32))
        scl = st.enter_context(nc.sbuf_tensor(_u("f1_scl"), [1, 1], F32))
        se = st.enter_context(nc.sbuf_tensor(_u("f1_se"), [1, 8, 1024], BF16))
        se2q = st.enter_context(nc.sbuf_tensor(_u("f1_se2q"), [1, 16, 256], BF16))
        p_sb = st.enter_context(nc.sbuf_tensor(_u("f1_p"), [1, 4, 2048], BF16))
        r_sb = st.enter_context(nc.sbuf_tensor(_u("f1_r"), [1, 16, 512], BF16))
        hT = st.enter_context(nc.sbuf_tensor(_u("f1_h"), [128, 32, 512], BF16))

        @block.gpsimd
        def _(g):
            ctx.wait_all(g)
            ctx.dma(g, b1[:, :], wd["l1b1"][:, :])
            ctx.dma(g, b2[:, :], wd["l1b2"][:, :])
            ctx.dma(g, scl[0:1, 0:1], wd["l1sc"][0:1, 0:1])

        do_prep = "noprep" not in DBG
        lt_scope = ExitStack()
        lt = lt_scope.enter_context(nc.sbuf_tensor(_u("f1_lt"), [1, 8, 1024], BF16))
        with nc.sbuf_tensor(_u("f1_ptt"), [128, 128], BF16) as ptt:
          if do_prep:
              for c in range(NCH):
                  @block.scalar
                  def _(a, c=c):
                      if c == 0:
                          ctx.wait_all(a)
                      else:
                          ctx.wait(a, "d")
                      i = a.dma_start(out=ptt[:, :], in_=xbf[:, c, 0:128], transpose=True)
                      ctx.inc(i, "t", 16)

                  @block.gpsimd
                  def _(g, c=c):
                      ctx.wait(g, "t")
                      ctx.dma(g, scr2[:, c * 128:(c + 1) * 128], ptt[1:17, :])

              @block.gpsimd
              def _(g):
                  ctx.wait(g, "d")
                  i = g.dma_start(out=lt[0:1, :, :],
                                  in_=scr2.rearrange("(o r) p -> o r p", o=1)[0:1, 0:16:2, :])
                  ctx.inc(i, "d", 16)
          _fence(ctx)

        @block.vector
        def _(v):
            if not do_prep:
                ctx.inc(v.drain(), "v")
                return
            ctx.wait(v, "v")
            for i8 in range(8):
                v.tensor_scalar_mul(se[0:1, i8, :], lt[0:1, i8, :], scl[0:1, 0:1])
            ctx.inc(v.drain(), "v")
            ctx.wait(v, "v")
            for i4 in range(4):
                for q4 in range(4):
                    for half in range(2):
                        v.tensor_copy(out=se2q[0:1, i4 * 4 + q4, half * 128:(half + 1) * 128],
                                      in_=se[0:1, 4 + i4, half * 512 + q4:(half + 1) * 512:4])
            ctx.inc(v.drain(), "v")
        _fence(ctx)
        lt_scope.close()

        with nc.psum_tensor(_u("f1psp"), [1, 4, 512], F32) as psp:
            for tg in (range(NTG) if do_prep else []):
                @block.tensor
                def _(te, tg=tg):
                    if tg == 0:
                        ctx.wait_all(te)
                    else:
                        ctx.wait(te, "v")
                    for i4 in range(4):
                        for ic in range(NCH):
                            i = te.matmul(psp[:, i4, :],
                                          xbf[:, ic, 2 + 2 * i4:3 + 2 * i4],
                                          xbf[:, ic, tg * 512:(tg + 1) * 512],
                                          start=(ic == 0), stop=(ic == NCH - 1))
                    ctx.inc(i, "p")

                @block.vector
                def _(v, tg=tg):
                    ctx.wait(v, "p")
                    v.tensor_copy(out=p_sb[0:1, :, tg * 512:(tg + 1) * 512], in_=psp[0:1, :, :])
                    ctx.inc(v.drain(), "v")
        _fence(ctx)

        for tg in range(NTG):
            for hs in range(4):
                with ExitStack() as st2:
                    saved, ctx.stack = ctx.stack, st2
                    w1 = load_weight_bf16(ctx, wd["l1w1"], 1024, hs * 1024, (hs + 1) * 1024,
                                          "f1w1")
                    with nc.psum_tensor(_u("f1ps1"), [128, 2, 512], F32) as ps1:
                        for hb8 in range(8):
                            hb = hs * 8 + hb8
                            par = hb8 % 2

                            @block.tensor
                            def _(te, tg=tg, hs=hs, hb8=hb8, hb=hb, par=par):
                                if hb8 == 0:
                                    ctx.wait_all(te)
                                if hb8 >= 2:
                                    ctx.wait(te, "a")
                                for ic in range(NCH):
                                    i = te.matmul(ps1[:, par, :],
                                              w1[:, ic, hb8 * 128:(hb8 + 1) * 128],
                                              xbf[:, ic, tg * 512:(tg + 1) * 512],
                                              start=(ic == 0),
                                              stop=(ic == NCH - 1 and "nolora" in DBG))
                                if "nolora" not in DBG:
                                    i4 = hb // 8
                                    jsl = (hb % 8) * 128
                                    i = te.matmul(ps1[:, par, :], se[0:1, i4, jsl:jsl + 128],
                                                  p_sb[0:1, i4, tg * 512:(tg + 1) * 512],
                                                  start=False, stop=True)
                                ctx.inc(i, "p")

                            @block.scalar
                            def _(a, hb=hb, par=par):
                                ctx.wait(a, "p")
                                i = a.activation(out=hT[:, hb, :], in_=ps1[:, par, :],
                                                 func=AF.Gelu, bias=b1[:, hb:hb + 1],
                                                 scale=1.0)
                                ctx.inc(i, "a")
                    _fence(ctx)
                    ctx.stack = saved

            if "dumph" in DBG and tg == 0:
                with nc.sbuf_tensor(_u("dmp"), [128, 512], F32) as dmp:
                    for hc in range(32):
                        @block.gpsimd
                        def _(g, hc=hc):
                            ctx.wait_all(g)
                            i = g.tensor_copy(out=dmp[:, :], in_=hT[:, hc, :])
                            ctx.inc(i, "g")
                            ctx.wait(g, "g")
                            ctx.dma(g, DBG_DUMP[0][(hc % 8) * 128:(hc % 8 + 1) * 128,
                                                   (hc // 8) * 512:(hc // 8 + 1) * 512],
                                    dmp[:, :], name="o")
                            ctx.wait(g, "o")
                _fence(ctx)

            for i4h in (range(2) if do_prep else []):
                with nc.psum_tensor(_u("f1psr"), [1, 8, 512], F32) as psr:
                    @block.tensor
                    def _(te, tg=tg, i4h=i4h):
                        ctx.wait_all(te)
                        for i4x in range(2):
                            i4 = 2 * i4h + i4x
                            for q4 in range(4):
                                for kc in range(8):
                                    i = te.matmul(psr[:, i4x * 4 + q4, :],
                                                  xbf[:, kc, 10 + 2 * i4:11 + 2 * i4],
                                                  hT[:, q4 * 8 + kc, :],
                                                  start=(kc == 0), stop=(kc == 7))
                        ctx.inc(i, "p")

                    @block.vector
                    def _(v, tg=tg, i4h=i4h):
                        ctx.wait(v, "p")
                        v.tensor_copy(out=r_sb[0:1, i4h * 8:(i4h + 1) * 8, :],
                                      in_=psr[0:1, :, :])
                        ctx.inc(v.drain(), "v")
                _fence(ctx)

            with nc.sbuf_tensor(_u("f1_ev2"), [128, 512], F32) as ev2, \
                 nc.psum_tensor(_u("f1ps2"), [128, 2, 512], F32) as ps2:
                for oc in range(NCH):
                    par = oc % 2
                    i4 = oc // 2
                    with ExitStack() as st2:
                        saved, ctx.stack = ctx.stack, st2
                        w2 = load_weight_bf16(ctx, wd["l1w2"], 4096, oc * 128,
                                              (oc + 1) * 128, "f1w2")

                        @block.tensor
                        def _(te, tg=tg, oc=oc, par=par, i4=i4, w2=w2):
                            ctx.wait_all(te)
                            for hc in range(32):
                                i = te.matmul(ps2[:, par, :], w2[:, hc, :],
                                          hT[:, hc, :], start=(hc == 0),
                                          stop=(hc == 31 and "nolora" in DBG))
                            if "nolora" not in DBG:
                                for q4 in range(4):
                                    i = te.matmul(ps2[:, par, :],
                                                  se2q[0:1, i4 * 4 + q4,
                                                       (oc % 2) * 128:(oc % 2) * 128 + 128],
                                                  r_sb[0:1, i4 * 4 + q4, :],
                                                  start=False, stop=(q4 == 3))
                            ctx.inc(i, "p")

                        @block.vector
                        def _(v, tg=tg, oc=oc, par=par):
                            ctx.wait(v, "p")
                            v.tensor_scalar(out=ev2[:, :], in0=ps2[:, par, :],
                                            scalar1=b2[:, oc:oc + 1], scalar2=None,
                                            op0=ALU.add)
                            v.tensor_tensor(out=xT[:, oc, tg * 512:(tg + 1) * 512],
                                            in0=ev2[:, :],
                                            in1=xT[:, oc, tg * 512:(tg + 1) * 512],
                                            op=ALU.add)
                            ctx.inc(v.drain(), "v")
                        _fence(ctx)
                        ctx.stack = saved
    if "noln" not in DBG:
        emit_layernorm_inplace(ctx, consts, xT)


def build_program(stages):
    nc = bass.Bass("TRN2", target_bir_lowering=False, detect_race_conditions=False)
    cts = _make_consts(nc)
    wd = {}

    def din(nm, shp):
        wd[nm] = nc.dram_tensor(nm, shp, F32, kind="ExternalInput").ap()

    if "l0a" in stages:
        din("l0wq", [1024, 1024]); din("l0wk", [1024, 256])
        din("l0wv", [1024, 256]); din("l0wo", [1024, 1024])
    if "l0f" in stages:
        din("l0fw", [1024, 1024]); din("l0fb", [128, 8]); din("l0sc", [1, 2])
    if "l1a" in stages:
        for nm in ("l1wq", "l1wk", "l1wv", "l1wo"):
            din(nm, [1024, 1024])
        din("l1bq", [128, 8]); din("l1bk", [128, 8])
        din("l1bv_row", [8, 128]); din("l1bo", [128, 8])
    if "l1f" in stages:
        din("l1w1", [1024, 4096]); din("l1b1", [128, 32])
        din("l1w2", [4096, 1024]); din("l1b2", [128, 8]); din("l1sc", [1, 1])
    tabs = {}
    if "l0a" in stages:
        for nm in ("coshi", "coslo", "sinhi", "sinlo"):
            tabs[nm] = nc.dram_tensor(nm, [512, 2048], F32, kind="ExternalInput").ap()
    xin, xout, xro = [], [], []
    for j in range(BPC):
        xin.append(nc.dram_tensor(f"xin{j}", [1024, 2048], F32, kind="ExternalInput").ap())
        xout.append(nc.dram_tensor(f"xout{j}", [1024, 2048], F32, kind="ExternalOutput").ap())
        if "l0a" in stages:
            xro.append(nc.dram_tensor(f"xro{j}", [1024, 2048], F32, kind="ExternalInput").ap())
    scr = nc.dram_tensor("scr", [256, 8], F32, kind="Internal").ap()
    if "dump_att" in DBG:
        DBG_DRAM[0] = nc.dram_tensor("dbgatt", [128, 4200], F32, kind="ExternalOutput").ap()
    if "dump_ctx" in DBG:
        DBG_DRAM[0] = nc.dram_tensor("dbgatt", [1024, 2048], F32, kind="ExternalOutput").ap()
    if "dump_negm" in DBG:
        DBG_DRAM2[0] = nc.dram_tensor("dbgnegm", [128, 2128], F32, kind="ExternalOutput").ap()
    if "dumph" in DBG:
        DBG_DUMP[0] = nc.dram_tensor("dbgout", [1024, 2048], F32, kind="ExternalOutput").ap()
    scr2 = nc.dram_tensor("scr2", [16, 1024], BF16, kind="Internal").ap()

    need_bf = ("l1a" in stages) or ("l1f" in stages)
    wshapes = {"l0wq": (1024, 1024), "l0wk": (1024, 256), "l0wv": (1024, 256),
               "l0wo": (1024, 1024), "l0fw": (1024, 1024),
               "l1wq": (1024, 1024), "l1wk": (1024, 1024), "l1wv": (1024, 1024),
               "l1wo": (1024, 1024), "l1w1": (1024, 4096), "l1w2": (4096, 1024)}
    w16 = {}
    for nm, shp in wshapes.items():
        if nm in wd:
            w16[nm] = nc.dram_tensor(nm + "_16", list(shp), BF16, kind="Internal").ap()
    with ExitStack() as stack:
        blk = stack.enter_context(nc.Block())
        ctx = Ctx(nc, blk, stack)
        consts = _load_consts(ctx, cts)
        _fence(ctx)
        for nm in w16:
            emit_weight_prepass(ctx, wd[nm], w16[nm], *wshapes[nm], nm)
        for nm in w16:
            wd[nm] = w16[nm]
        xT = stack.enter_context(nc.sbuf_tensor(_u("xT"), [128, 8, 2048], F32))
        for j in range(BPC):
            with ExitStack() as bst:
                emit_load_stream(ctx, xin[j], xT, None)
                if "l0a" in stages:
                    wd2 = dict(wd)
                    wd2["xro"] = xro[j]
                    emit_l0_attn(ctx, consts, wd2, tabs, xT, scr)
                if "l0f" in stages:
                    emit_l0_ffn(ctx, consts, wd, xT)
                xbf = None
                if need_bf:
                    xbf = bst.enter_context(nc.sbuf_tensor(_u("xbf"), [128, 8, 2048], BF16))
                if "l1a" in stages:
                    emit_refresh_xbf(ctx, xT, xbf)
                    emit_l1_attn(ctx, consts, wd, xT, xbf, scr)
                if "l1f" in stages:
                    emit_refresh_xbf(ctx, xT, xbf)
                    emit_l1_ffn(ctx, consts, wd, xT, xbf, scr2)
                emit_store_stream(ctx, xT, xout[j])
    return nc


def _host_prep(inputs):
    d = {}
    d["l0wq"] = inputs["L0_attn_W_q"].T
    d["l0wk"] = inputs["L0_attn_W_k"].T
    d["l0wv"] = inputs["L0_attn_W_v"].T
    d["l0wo"] = inputs["L0_attn_W_o"].T
    d["l0fw"] = inputs["L0_ffn_weight_fp8"].T
    d["l0fb"] = inputs["L0_ffn_bias"].reshape(8, 128).T
    d["l0sc"] = np.stack([np.asarray(inputs["L0_ffn_input_scale"]).reshape(()),
                          np.asarray(inputs["L0_ffn_weight_scale"]).reshape(())]).reshape(1, 2)
    d["l1wq"] = inputs["L1_attn_Wq"].T
    d["l1wk"] = inputs["L1_attn_Wk"].T
    d["l1wv"] = inputs["L1_attn_Wv"].T
    d["l1wo"] = inputs["L1_attn_Wo"].T
    d["l1bq"] = inputs["L1_attn_bq"].reshape(8, 128).T
    d["l1bk"] = inputs["L1_attn_bk"].reshape(8, 128).T
    d["l1bv_row"] = inputs["L1_attn_bv"].reshape(8, 128)
    d["l1bo"] = inputs["L1_attn_bo"].reshape(8, 128).T
    d["l1w1"] = inputs["L1_ffn_W1"].T
    d["l1b1"] = inputs["L1_ffn_b1"].reshape(32, 128).T
    d["l1w2"] = inputs["L1_ffn_W2"].T
    d["l1b2"] = inputs["L1_ffn_b2"].reshape(8, 128).T
    d["l1sc"] = np.asarray(inputs["L1_ffn_scale"], np.float32).reshape(1, 1)
    f = np.arange(512, dtype=np.float64) / 512.0
    theta = 1.0 / (10000.0 ** f)
    ang = theta[:, None] * np.arange(S, dtype=np.float64)[None, :]
    for nm, fn in (("cos", np.cos), ("sin", np.sin)):
        v = fn(ang)
        hi = v.astype(np.float32)
        lo = (v - hi.astype(np.float64)).astype(np.float32)
        d[nm + "hi"] = hi
        d[nm + "lo"] = lo
    return {k: np.ascontiguousarray(v, dtype=np.float32) for k, v in d.items()}


def _get_program(stages):
    key = tuple(stages)
    if key not in _PROGRAM_CACHE:
        _PROGRAM_CACHE[key] = build_program(stages)
    return _PROGRAM_CACHE[key]


def _prog_input_names(nc):
    names = set()
    for a in nc.m.functions[0].allocations:
        if isinstance(a, mybir.MemoryLocationSet) and a.kind == "ExternalInput":
            for ml in a.memorylocations:
                names.add(ml.name)
    return names


def run_stage_group(stages, shared, streams, xros):
    nc = _get_program(stages)
    names = _prog_input_names(nc)
    in_maps = []
    for c in range(NC_N):
        m = {k: v for k, v in shared.items() if k in names}
        for j in range(BPC):
            m[f"xin{j}"] = streams[c][j]
            if f"xro{j}" in names:
                m[f"xro{j}"] = xros[c][j]
        in_maps.append(m)
    res = run_bass_kernel_spmd(nc, in_maps, core_ids=list(range(NC_N)))
    return [[np.asarray(res.results[c][f"xout{j}"]) for j in range(BPC)] for c in range(NC_N)]


STAGE_GROUPS = [("l0a", "l0f", "l1a", "l1f")]


def kernel(**inputs):
    shared = _host_prep(inputs)
    x = np.asarray(inputs["L0_attn_X"], np.float32)
    streams, xros = [], []
    for c in range(NC_N):
        ss, rr = [], []
        for j in range(BPC):
            xb = x[c * BPC + j]
            ss.append(np.ascontiguousarray(xb.T))
            rr.append(np.ascontiguousarray(
                np.concatenate([xb[:, 0::2], xb[:, 1::2]], axis=1).T))
        streams.append(ss)
        xros.append(rr)
    for stages in STAGE_GROUPS:
        streams = run_stage_group(stages, shared, streams, xros)
    out = np.empty((B, S, D), np.float32)
    for c in range(NC_N):
        for j in range(BPC):
            out[c * BPC + j] = streams[c][j].T
    return out


# version 10 (final)
# speedup vs baseline: 1.5572x; 1.0133x over previous
import numpy as np
import ml_dtypes
import concourse.bass as bass
import concourse.mybir as mybir
from concourse.bass_utils import run_bass_kernel_spmd
from contextlib import ExitStack

F32 = mybir.dt.float32
BF16 = mybir.dt.bfloat16
I32 = mybir.dt.int32
AF = mybir.ActivationFunctionType
ALU = mybir.AluOpType
AXL = mybir.AxisListType

B, S, D = 16, 2048, 1024
NC_N = 8
BPC = B // NC_N
NCH = D // 128
NKC = S // 128
NTG = S // 512
NBIG = -3.0e38

_PROGRAM_CACHE = {}
DBG = set()
DBG_DUMP = [None]
DBG_ATT = [None]
DBG_DRAM = [None]
DBG_ATT2 = [None]
DBG_DRAM2 = [None]

_NAME_SEQ = [0]


def _u(name):
    _NAME_SEQ[0] += 1
    return f"{name}_{_NAME_SEQ[0]}"




class Ctx:
    def __init__(self, nc, block, stack):
        self.nc = nc
        self.block = block
        self.stack = stack
        self.sems = {}
        self.cnt = {}
        for name in ("d", "p", "v", "a", "g", "t", "o", "s",
                     "w0", "w1", "w2", "w3"):
            self.sems[name] = stack.enter_context(nc.semaphore("sem_" + name))
            self.cnt[name] = 0

    def inc(self, inst, name, n=1):
        inst.then_inc(self.sems[name], n)
        self.cnt[name] += n
        return inst

    def dma(self, eng, out, in_, name="d"):
        return self.inc(eng.dma_start(out=out, in_=in_), name, 16)

    def wait(self, eng, name):
        eng.wait_ge(self.sems[name], self.cnt[name])

    def wait_all(self, eng):
        for name in ("d", "p", "v", "a", "g", "t", "o", "s",
                     "w0", "w1", "w2", "w3"):
            if self.cnt[name]:
                eng.wait_ge(self.sems[name], self.cnt[name])


def _fence(ctx):
    block = ctx.block

    @block.tensor
    def _(e):
        ctx.inc(e.drain(), "p")

    @block.vector
    def _(e):
        ctx.inc(e.drain(), "v")

    @block.scalar
    def _(e):
        ctx.inc(e.drain(), "a")

    @block.gpsimd
    def _(e):
        ctx.inc(e.drain(), "g")

    @block.sync
    def _(e):
        ctx.inc(e.drain(), "s")

    for emit in (block.tensor, block.vector, block.scalar, block.gpsimd, block.sync):
        @emit
        def _(e):
            ctx.wait_all(e)


def _make_consts(nc):
    c = {}
    c["ident"] = nc.inline_tensor(np.eye(128, dtype=ml_dtypes.bfloat16), "c_ident")
    c["ones_row_f"] = nc.inline_tensor(np.ones((1, 128), np.float32), "c_onesrf")
    c["ones_col_f"] = nc.inline_tensor(np.ones((128, 1), np.float32), "c_onescf")
    mask = np.triu(np.ones((128, 128), np.float32), k=1) * (-1e9)
    c["causal"] = nc.inline_tensor(mask, "c_causal")
    c["expmask"] = nc.inline_tensor(np.full((128, 1), 0x7F800000, np.int32), "c_expmask")
    return c


def _load_consts(ctx, cts):
    nc, block = ctx.nc, ctx.block
    sb = {}
    sb["ident"] = ctx.stack.enter_context(nc.sbuf_tensor(_u("cs_ident"), [128, 128], BF16))
    sb["ones_row_f"] = ctx.stack.enter_context(nc.sbuf_tensor(_u("cs_onesrf"), [1, 128], F32))
    sb["ones_col_f"] = ctx.stack.enter_context(nc.sbuf_tensor(_u("cs_onescf"), [128, 1], F32))
    sb["causal"] = ctx.stack.enter_context(nc.sbuf_tensor(_u("cs_causal"), [128, 128], F32))
    sb["expmask"] = ctx.stack.enter_context(nc.sbuf_tensor(_u("cs_expmask"), [128, 1], I32))
    sb["eps"] = ctx.stack.enter_context(nc.sbuf_tensor(_u("cs_eps"), [128, 1], F32))

    @block.gpsimd
    def _(g):
        for k in ("ident", "ones_row_f", "ones_col_f", "causal", "expmask"):
            ctx.dma(g, sb[k][:, :], cts[k].ap()[:, :])

    @block.vector
    def _(v):
        v.memset(sb["eps"][:, :], 1e-5)
        ctx.inc(v.drain(), "v")
    return sb


def emit_weight_prepass(ctx, w_dram, w16_dram, nin, nout, name):
    nc, block = ctx.nc, ctx.block
    sub = min(nout, 512)
    slices = [(ic, s0) for ic in range(nin // 128) for s0 in range(0, nout, sub)]
    with nc.sbuf_tensor(_u(name + "_pf"), [128, 2, sub], F32) as pf, \
         nc.sbuf_tensor(_u(name + "_pb"), [128, 2, sub], BF16) as pb:
        @block.gpsimd
        def _(g):
            ctx.wait_all(g)
            casts = [None] * len(slices)
            stores = [None] * len(slices)

            def cast(ci):
                slot = ci % 2
                g.wait_ge(ctx.sems["w%d" % slot], ctx.cnt["w%d" % slot])
                if ci >= 2:
                    g.wait_ge(ctx.sems["w%d" % (2 + slot)], stores[ci - 2])
                i = g.tensor_copy(out=pb[:, slot, :], in_=pf[:, slot, :])
                ctx.inc(i, "g")
                casts[ci] = ctx.cnt["g"]

            def store(ci):
                ic, s0 = slices[ci]
                slot = ci % 2
                g.wait_ge(ctx.sems["g"], casts[ci])
                i = g.dma_start(out=w16_dram[ic * 128:(ic + 1) * 128, s0:s0 + sub],
                                in_=pb[:, slot, :])
                ctx.inc(i, "w%d" % (2 + slot), 16)
                stores[ci] = ctx.cnt["w%d" % (2 + slot)]

            for si, (ic, s0) in enumerate(slices):
                slot = si % 2
                if si >= 2:
                    g.wait_ge(ctx.sems["g"], casts[si - 2])
                i = g.dma_start(out=pf[:, slot, :],
                                in_=w_dram[ic * 128:(ic + 1) * 128, s0:s0 + sub])
                ctx.inc(i, "w%d" % slot, 16)
                if si >= 1:
                    cast(si - 1)
                    store(si - 1)
            cast(len(slices) - 1)
            store(len(slices) - 1)
            for slot in (2, 3):
                g.wait_ge(ctx.sems["w%d" % slot], ctx.cnt["w%d" % slot])
            ctx.inc(g.drain(), "g")
        _fence(ctx)


def load_weight_bf16(ctx, w16_dram, nin, c0, c1, name):
    nc, block = ctx.nc, ctx.block
    nchk = nin // 128
    ncol = c1 - c0
    wbf = ctx.stack.enter_context(nc.sbuf_tensor(_u(name + "_bf"), [128, nchk, ncol], BF16))

    @block.gpsimd
    def _(g):
        ctx.wait_all(g)
        for ic in range(nchk):
            ctx.dma(g, wbf[:, ic, :], w16_dram[ic * 128:(ic + 1) * 128, c0:c1])
        ctx.wait(g, "d")
        ctx.inc(g.drain(), "g")
    _fence(ctx)
    return wbf


def emit_proj_featmajor(ctx, wbf, nin, nout, src_bf, evac, psname, oc0=0):
    nc, block = ctx.nc, ctx.block
    nic = nin // 128
    noc = nout // 128
    with nc.psum_tensor(_u(psname), [128, 2, 512], F32) as ps:
        for oc_ in range(noc):
            for tg in range(NTG):
                par = (oc_ * NTG + tg) % 2

                @block.tensor
                def _(te, oc=oc_, tg=tg, par=par):
                    if oc == 0 and tg == 0:
                        ctx.wait_all(te)
                    if oc * NTG + tg >= 2:
                        ctx.wait(te, "v")
                    for ic in range(nic):
                        i = te.matmul(ps[:, par, :], wbf[:, ic, oc * 128:(oc + 1) * 128],
                                      src_bf[:, ic, tg * 512:(tg + 1) * 512],
                                      start=(ic == 0), stop=(ic == nic - 1))
                    ctx.inc(i, "p")

                @block.vector
                def _(v, oc=oc_, tg=tg, par=par):
                    ctx.wait(v, "p")
                    evac(v, ps[:, par, :], oc0 + oc, tg)
                    ctx.inc(v.drain(), "v")
    _fence(ctx)


def emit_load_stream(ctx, x_dram, xT, xbf=None):
    nc, block = ctx.nc, ctx.block

    @block.gpsimd
    def _(g):
        ctx.wait_all(g)
        for c in range(NCH):
            ctx.dma(g, xT[:, c, :], x_dram[c * 128:(c + 1) * 128, :])
        if xbf is not None:
            ctx.wait(g, "d")
            for c in range(NCH):
                g.tensor_copy(out=xbf[:, c, :], in_=xT[:, c, :])
            ctx.inc(g.drain(), "g")
    _fence(ctx)


def emit_refresh_xbf(ctx, xT, xbf):
    @ctx.block.gpsimd
    def _(g):
        ctx.wait_all(g)
        for c in range(NCH):
            g.tensor_copy(out=xbf[:, c, :], in_=xT[:, c, :])
        ctx.inc(g.drain(), "g")
    _fence(ctx)


def emit_store_stream(ctx, xT, y_dram):
    nc, block = ctx.nc, ctx.block

    @block.gpsimd
    def _(g):
        ctx.wait_all(g)
        for c in range(NCH):
            ctx.dma(g, y_dram[c * 128:(c + 1) * 128, :], xT[:, c, :], name="o")
        ctx.wait(g, "o")
    _fence(ctx)


def emit_rope(ctx, xro_dram, tabs, xrbf):
    nc, block = ctx.nc, ctx.block
    with ExitStack() as st:
        xe = st.enter_context(nc.sbuf_tensor(_u("rp_xe"), [128, 2048], F32))
        xo = st.enter_context(nc.sbuf_tensor(_u("rp_xo"), [128, 2048], F32))
        tt = {k: st.enter_context(nc.sbuf_tensor(_u("rp_" + k), [128, 2048], F32))
              for k in ("ch", "cl", "sh", "sl")}
        m1 = st.enter_context(nc.sbuf_tensor(_u("rp_m1"), [128, 2048], F32))
        m2 = st.enter_context(nc.sbuf_tensor(_u("rp_m2"), [128, 2048], F32))
        p1 = st.enter_context(nc.sbuf_tensor(_u("rp_p1"), [128, 2048], F32))
        p2 = st.enter_context(nc.sbuf_tensor(_u("rp_p2"), [128, 2048], F32))
        for c in range(4):
            @block.gpsimd
            def _(g, c=c):
                if c == 0:
                    ctx.wait_all(g)
                else:
                    ctx.wait(g, "v")
                ctx.dma(g, xe[:, :], xro_dram[c * 128:(c + 1) * 128, :])
                ctx.dma(g, xo[:, :], xro_dram[512 + c * 128:512 + (c + 1) * 128, :])
                for k, nm in (("ch", "coshi"), ("cl", "coslo"), ("sh", "sinhi"), ("sl", "sinlo")):
                    ctx.dma(g, tt[k][:, :], tabs[nm][c * 128:(c + 1) * 128, :])

            @block.vector
            def _(v, c=c):
                ctx.wait(v, "d")
                v.tensor_tensor(out=m1[:, :], in0=xe[:, :], in1=tt["ch"][:, :], op=ALU.mult)
                v.tensor_tensor(out=m2[:, :], in0=xo[:, :], in1=tt["sh"][:, :], op=ALU.mult)
                v.tensor_tensor(out=p1[:, :], in0=m1[:, :], in1=m2[:, :], op=ALU.subtract)
                v.tensor_tensor(out=m1[:, :], in0=xe[:, :], in1=tt["cl"][:, :], op=ALU.mult)
                v.tensor_tensor(out=m2[:, :], in0=xo[:, :], in1=tt["sl"][:, :], op=ALU.mult)
                v.tensor_tensor(out=p2[:, :], in0=m1[:, :], in1=m2[:, :], op=ALU.subtract)
                v.tensor_tensor(out=p1[:, :], in0=p1[:, :], in1=p2[:, :], op=ALU.add)
                v.tensor_copy(out=xrbf[:, c, :], in_=p1[:, :])
                v.tensor_tensor(out=m1[:, :], in0=xe[:, :], in1=tt["sh"][:, :], op=ALU.mult)
                v.tensor_tensor(out=m2[:, :], in0=xo[:, :], in1=tt["ch"][:, :], op=ALU.mult)
                v.tensor_tensor(out=p1[:, :], in0=m1[:, :], in1=m2[:, :], op=ALU.add)
                v.tensor_tensor(out=m1[:, :], in0=xe[:, :], in1=tt["sl"][:, :], op=ALU.mult)
                v.tensor_tensor(out=m2[:, :], in0=xo[:, :], in1=tt["cl"][:, :], op=ALU.mult)
                v.tensor_tensor(out=p2[:, :], in0=m1[:, :], in1=m2[:, :], op=ALU.add)
                v.tensor_tensor(out=p1[:, :], in0=p1[:, :], in1=p2[:, :], op=ALU.add)
                v.tensor_copy(out=xrbf[:, 4 + c, :], in_=p1[:, :])
                ctx.inc(v.drain(), "v")
    _fence(ctx)


def emit_layernorm_inplace(ctx, consts, xT):
    nc, block = ctx.nc, ctx.block
    with ExitStack() as st:
        sq = st.enter_context(nc.sbuf_tensor(_u("ln_sq"), [128, 1024], F32))
        stats = st.enter_context(nc.sbuf_tensor(_u("ln_stats"), [1, 4, 1024], F32))
        bc = st.enter_context(nc.sbuf_tensor(_u("ln_bc"), [128, 2, 1024], F32))
        ps_mu = st.enter_context(nc.psum_tensor(_u("ln_psmu"), [1, 1024], F32))
        ps_sq = st.enter_context(nc.psum_tensor(_u("ln_pssq"), [1, 1024], F32))
        ps_bc = st.enter_context(nc.psum_tensor(_u("ln_psbc"), [128, 2, 1024], F32))
        for th in range(2):
            t0 = th * 1024

            for c in range(NCH):
                @block.scalar
                def _(a, th=th, t0=t0, c=c):
                    if th == 0 and c == 0:
                        ctx.wait_all(a)
                    elif c == 0:
                        ctx.wait(a, "v")
                    else:
                        ctx.wait(a, "p")
                    i = a.activation(out=sq[:, :], in_=xT[:, c, t0:t0 + 1024],
                                     func=AF.Square, bias=0.0, scale=1.0)
                    ctx.inc(i, "a")

                @block.tensor
                def _(te, th=th, t0=t0, c=c):
                    ctx.wait(te, "a")
                    for g2 in range(2):
                        te.matmul(ps_mu[:, g2 * 512:(g2 + 1) * 512], consts["ones_col_f"][:, :],
                                  xT[:, c, t0 + g2 * 512:t0 + (g2 + 1) * 512],
                                  start=(c == 0), stop=(c == NCH - 1))
                        i = te.matmul(ps_sq[:, g2 * 512:(g2 + 1) * 512], consts["ones_col_f"][:, :],
                                      sq[:, g2 * 512:(g2 + 1) * 512],
                                      start=(c == 0), stop=(c == NCH - 1))
                    ctx.inc(i, "p")

            @block.vector
            def _(v, th=th):
                ctx.wait(v, "p")
                v.tensor_scalar_mul(stats[0:1, 0, :], ps_mu[:, :], 1.0 / D)
                v.tensor_scalar_mul(stats[0:1, 1, :], ps_sq[:, :], 1.0 / D)
                v.tensor_tensor(out=stats[0:1, 3, :], in0=stats[0:1, 0, :],
                                in1=stats[0:1, 0, :], op=ALU.mult)
                v.tensor_tensor(out=stats[0:1, 1, :], in0=stats[0:1, 1, :],
                                in1=stats[0:1, 3, :], op=ALU.subtract)
                ctx.inc(v.drain(), "v")

            @block.scalar
            def _(a, th=th):
                ctx.wait(a, "v")
                i = a.activation(out=stats[0:1, 3, :], in_=stats[0:1, 1, :],
                                 func=AF.Sqrt, bias=consts["eps"][0:1, 0:1], scale=1.0)
                ctx.inc(i, "a")

            @block.vector
            def _(v, th=th):
                ctx.wait(v, "a")
                v.reciprocal(out=stats[0:1, 1, :], in_=stats[0:1, 3, :])
                v.tensor_tensor(out=stats[0:1, 2, :], in0=stats[0:1, 0, :],
                                in1=stats[0:1, 1, :], op=ALU.mult)
                v.tensor_scalar_mul(stats[0:1, 2, :], stats[0:1, 2, :], -1.0)
                ctx.inc(v.drain(), "v")

            @block.tensor
            def _(te, th=th):
                ctx.wait(te, "v")
                for g2 in range(2):
                    te.matmul(ps_bc[:, 0, g2 * 512:(g2 + 1) * 512], consts["ones_row_f"][0:1, :],
                              stats[0:1, 1, g2 * 512:(g2 + 1) * 512], start=True, stop=True)
                    i = te.matmul(ps_bc[:, 1, g2 * 512:(g2 + 1) * 512], consts["ones_row_f"][0:1, :],
                                  stats[0:1, 2, g2 * 512:(g2 + 1) * 512], start=True, stop=True)
                ctx.inc(i, "p")

            @block.vector
            def _(v, th=th, t0=t0):
                ctx.wait(v, "p")
                v.tensor_copy(out=bc[:, 0, :], in_=ps_bc[:, 0, :])
                v.tensor_copy(out=bc[:, 1, :], in_=ps_bc[:, 1, :])
                for c in range(NCH):
                    v.tensor_tensor(out=xT[:, c, t0:t0 + 1024], in0=xT[:, c, t0:t0 + 1024],
                                    in1=bc[:, 0, :], op=ALU.mult)
                    v.tensor_tensor(out=xT[:, c, t0:t0 + 1024], in0=xT[:, c, t0:t0 + 1024],
                                    in1=bc[:, 1, :], op=ALU.add)
                ctx.inc(v.drain(), "v")
    _fence(ctx)


def emit_negm(ctx, consts, qt_bf, kt_bf, layer, negm_bc, scr_dram, tag):
    nc, block = ctx.nc, ctx.block
    kch = 4 if layer == 0 else 8
    with ExitStack() as st:
        mq = st.enter_context(nc.sbuf_tensor(_u(tag + "_mq"), [128, 8], F32))
        mk = st.enter_context(nc.sbuf_tensor(_u(tag + "_mk"), [128, 8], F32))
        flat = st.enter_context(nc.sbuf_tensor(_u(tag + "_flat"), [1, 2048], F32))
        mrow = st.enter_context(nc.sbuf_tensor(_u(tag + "_mrow"), [1, 3, 16], F32))
        ps = st.enter_context(nc.psum_tensor(_u(tag + "_ps"), [128, 16], F32))
        if "dump_negm" in DBG and tag == "l0nm":
            DBG_ATT2[0] = st.enter_context(nc.sbuf_tensor(_u("nm_dbg"), [128, 2128], F32))

        @block.vector
        def _(v):
            ctx.wait_all(v)
            v.memset(mk[:, :], 0.0)
            for c in range(NCH):
                v.reduce_max(out=mq[:, c:c + 1], in_=qt_bf[:, c, :], axis=AXL.X,
                             apply_absolute_value=True)
            for c in range(kch):
                v.reduce_max(out=mk[:, c:c + 1], in_=kt_bf[:, c, :], axis=AXL.X,
                             apply_absolute_value=True)
            ctx.inc(v.drain(), "v")

        @block.gpsimd
        def _(g):
            ctx.wait(g, "v")
            ctx.dma(g, scr_dram[0:128, :], mq[:, :])
            ctx.dma(g, scr_dram[128:256, :], mk[:, :])

        @block.gpsimd
        def _(g):
            ctx.wait(g, "d")
            ctx.dma(g, flat[0:1, :], scr_dram.rearrange("(o p) c -> o (p c)", o=1)[0:1, :])

        @block.vector
        def _(v):
            ctx.wait(v, "d")
            for h in range(16):
                st_q = 64 * (h % 2) * 8 + (h // 2)
                v.reduce_max(out=mrow[0:1, 0, h:h + 1],
                             in_=flat[0:1, st_q:st_q + 505:8], axis=AXL.X)
            for h in range(16):
                if layer == 0:
                    gk = h // 4
                    st_k = 1024 + gk
                    v.reduce_max(out=mrow[0:1, 1, h:h + 1],
                                 in_=flat[0:1, st_k:st_k + 505:8], axis=AXL.X)
                else:
                    st_k = 1024 + 64 * (h % 2) * 8 + (h // 2)
                    v.reduce_max(out=mrow[0:1, 1, h:h + 1],
                                 in_=flat[0:1, st_k:st_k + 505:8], axis=AXL.X)
            ctx.inc(v.drain(), "v")
            ctx.wait(v, "v")
            v.scalar_tensor_tensor(out=mrow[0:1, 2, :], in0=mrow[0:1, 0, :],
                                   scalar=-64.0, in1=mrow[0:1, 1, :],
                                   op0=ALU.mult, op1=ALU.mult)
            ctx.inc(v.drain(), "v")

        @block.tensor
        def _(te):
            ctx.wait(te, "v")
            i = te.matmul(ps[:, :], consts["ones_row_f"][0:1, :], mrow[0:1, 2, :],
                          start=True, stop=True)
            ctx.inc(i, "p")

        @block.vector
        def _(v):
            ctx.wait(v, "p")
            v.tensor_copy(out=negm_bc[:, :], in_=ps[:, :])
            if "dump_negm" in DBG and tag == "l0nm":
                v.tensor_copy(out=DBG_ATT2[0][:, 0:8], in_=mq[:, :])
                v.tensor_copy(out=DBG_ATT2[0][:, 8:16], in_=mk[:, :])
                v.tensor_copy(out=DBG_ATT2[0][0:1, 16:2064], in_=flat[0:1, :])
                v.tensor_copy(out=DBG_ATT2[0][0:1, 2064:2112], in_=mrow[0:1, :, :])
                v.tensor_copy(out=DBG_ATT2[0][:, 2112:2128], in_=negm_bc[:, :])
            ctx.inc(v.drain(), "v")

        if "dump_negm" in DBG and tag == "l0nm":
            @block.gpsimd
            def _(g):
                ctx.wait(g, "v")
                ctx.dma(g, DBG_DRAM2[0][:, :], DBG_ATT2[0][:, :], name="o")
                ctx.wait(g, "o")
    _fence(ctx)


def emit_attention(ctx, consts, layer, qt_bf, kt_bf, v_bf, negm_bc, ctx_bf):
    nc, block = ctx.nc, ctx.block
    units = [(h, qi) for h in range(16) for qi in range(NKC)]
    nun = len(units)
    marks = {k: [0] * nun for k in ("S", "RM", "EXP", "SC", "TR", "AV", "EV")}

    def live_of(qi):
        return (qi + 1) if layer == 0 else NKC

    with ExitStack() as st:
        npar = 2 if layer == 0 else 1
        E = st.enter_context(nc.sbuf_tensor(_u("at_E"), [128, npar, 2048], BF16))
        W = (st.enter_context(nc.sbuf_tensor(_u("at_W"), [128, 2048], BF16))
             if layer == 1 else None)
        PT = st.enter_context(nc.sbuf_tensor(_u("at_PT"), [128, 16, 128], BF16))
        m8 = st.enter_context(nc.sbuf_tensor(_u("at_m8"), [128, 8], BF16))
        rs = st.enter_context(nc.sbuf_tensor(_u("at_rs"), [128, 2], F32))
        rc = st.enter_context(nc.sbuf_tensor(_u("at_rc"), [128, 1], F32))
        rm = st.enter_context(nc.sbuf_tensor(_u("at_rm"), [128, 3], F32))
        ps_s = st.enter_context(nc.psum_tensor(_u("at_pss"), [128, 2, 1024], F32))
        ps_c = st.enter_context(nc.psum_tensor(_u("at_psc"), [128, 128], F32))

        def emit_scores(u):
            h, qi = units[u]
            live = live_of(qi)
            qch, qba = h // 2, 64 * (h % 2)
            grp = h // 4

            @block.tensor
            def _(te):
                if u == 0:
                    ctx.wait_all(te)
                else:
                    te.wait_ge(ctx.sems["a"], marks["EXP"][u - 1])
                lhsT = qt_bf[qba:qba + 64, qch, 128 * qi:128 * (qi + 1)]
                rhs_t = kt_bf[qba:qba + 64, grp if layer == 0 else qch, :]
                for half in range(2):
                    k0 = half * 8
                    nliv = min(8, max(0, live - k0))
                    if nliv == 0:
                        continue
                    for nb in range((nliv * 128 + 511) // 512):
                        n0 = k0 * 128 + nb * 512
                        nn = min(512, (k0 + nliv) * 128 - n0)
                        i = te.matmul(ps_s[:, half, nb * 512:nb * 512 + nn],
                                      lhsT, rhs_t[:, n0:n0 + nn],
                                      start=True, stop=True, tile_position=(qba, 0))
                ctx.inc(i, "p")
                marks["S"][u] = ctx.cnt["p"]

        def emit_av(u):
            h, qi = units[u]
            live = live_of(qi)
            qba = 64 * (h % 2)
            grp = h // 4

            @block.tensor
            def _(te):
                te.wait_ge(ctx.sems["t"], marks["TR"][u])
                if u > 0:
                    te.wait_ge(ctx.sems["a"], marks["EV"][u - 1])
                for c in range(live):
                    va = v_bf[:, c, 64 * (grp if layer == 0 else h):
                              64 * (grp if layer == 0 else h) + 64]
                    i = te.matmul(ps_c[qba:qba + 64, :], va, PT[:, c, :],
                                  start=(c == 0), stop=(c == live - 1),
                                  tile_position=(0, qba), skip_group_check=True)
                ctx.inc(i, "p")
                marks["AV"][u] = ctx.cnt["p"]

        emit_scores(0)
        for u in range(nun):
            h, qi = units[u]
            live = live_of(qi)
            nlive = live * 128

            @block.vector
            def _(v, u=u, h=h, qi=qi, live=live):
                v.wait_ge(ctx.sems["p"], marks["S"][u])
                if layer == 0:
                    half, cp = qi // 8, (qi % 8) * 128
                    v.tensor_tensor(out=ps_s[:, half, cp:cp + 128],
                                    in0=ps_s[:, half, cp:cp + 128],
                                    in1=consts["causal"][:, :], op=ALU.add)
                for half in range(2):
                    nliv = min(8, max(0, live - half * 8))
                    if nliv == 0:
                        continue
                    v.reduce_max(out=rm[:, half:half + 1],
                                 in_=ps_s[:, half, 0:nliv * 128],
                                 axis=AXL.X, negate=True)
                i = v.drain()
                ctx.inc(i, "v")
                ctx.wait(v, "v")
                if live > 8:
                    i = v.tensor_tensor(out=rm[:, 2:3], in0=rm[:, 0:1],
                                        in1=rm[:, 1:2], op=ALU.min)
                else:
                    i = v.tensor_copy(out=rm[:, 2:3], in_=rm[:, 0:1])
                ctx.inc(i, "v")
                marks["RM"][u] = ctx.cnt["v"]

            @block.scalar
            def _(a, u=u, h=h, live=live):
                a.wait_ge(ctx.sems["v"], marks["RM"][u])
                if u >= npar:
                    a.wait_ge(ctx.sems["t"], marks["TR"][u - npar])
                for half in range(2):
                    k0 = half * 8
                    nliv = min(8, max(0, live - k0))
                    if nliv == 0:
                        continue
                    i = a.activation(out=E[:, u % npar, k0 * 128:(k0 + nliv) * 128],
                                     in_=ps_s[:, half, 0:nliv * 128], func=AF.Exp,
                                     bias=rm[:, 2:3], scale=1.0,
                                     accum_out=rs[:, half:half + 1])
                ctx.inc(i, "a")
                marks["EXP"][u] = ctx.cnt["a"]

            if u > 0:
                h1, qi1 = units[u - 1]
                qba1 = 64 * (h1 % 2)

                @block.scalar
                def _(a, u=u, h1=h1, qi1=qi1, qba1=qba1):
                    a.wait_ge(ctx.sems["p"], marks["AV"][u - 1])
                    i = a.activation(out=ctx_bf[qba1:qba1 + 64, h1 // 2,
                                                128 * qi1:128 * (qi1 + 1)],
                                     in_=ps_c[qba1:qba1 + 64, :], func=AF.Copy,
                                     bias=0.0, scale=1.0)
                    ctx.inc(i, "a")
                    marks["EV"][u - 1] = ctx.cnt["a"]

            @block.vector
            def _(v, u=u, live=live, nlive=nlive):
                v.wait_ge(ctx.sems["a"], marks["EXP"][u])
                Eu = E[:, u % npar, :]
                if layer == 1:
                    for rnd in range(5):
                        src_t = Eu if rnd == 0 else W[:, :]
                        i = v.max(out=m8[:, :], in_=src_t[:, 0:nlive])
                        ctx.inc(i, "v")
                        ctx.wait(v, "v")
                        if rnd < 4:
                            v.scalar_tensor_tensor(out=W[:, 0:nlive],
                                                   in0=src_t[:, 0:nlive],
                                                   scalar=m8[:, 7:8],
                                                   in1=src_t[:, 0:nlive],
                                                   op0=ALU.is_lt, op1=ALU.mult)
                    i = v.scalar_tensor_tensor(out=Eu[:, 0:nlive], in0=Eu[:, 0:nlive],
                                               scalar=m8[:, 5:6], in1=Eu[:, 0:nlive],
                                               op0=ALU.is_ge, op1=ALU.mult,
                                               accum_out=rs[:, 0:1])
                    ctx.inc(i, "v")
                    ctx.wait(v, "v")
                elif live > 8:
                    i = v.tensor_tensor(out=rs[:, 0:1], in0=rs[:, 0:1],
                                        in1=rs[:, 1:2], op=ALU.add)
                    ctx.inc(i, "v")
                    ctx.wait(v, "v")
                i = v.reciprocal(out=rc[:, 0:1], in_=rs[:, 0:1])
                ctx.inc(i, "v")
                ctx.wait(v, "v")
                v.tensor_scalar_mul(Eu[:, 0:nlive], Eu[:, 0:nlive], rc[:, 0:1])
                i = v.drain()
                ctx.inc(i, "v")
                marks["SC"][u] = ctx.cnt["v"]

            @block.scalar
            def _(a, u=u, live=live):
                a.wait_ge(ctx.sems["v"], marks["SC"][u])
                if u > 0:
                    a.wait_ge(ctx.sems["p"], marks["AV"][u - 1])
                    a.wait_ge(ctx.sems["t"], marks["TR"][u - 1])
                i = a.dma_start(out=PT[:, 0:live, :],
                                in_=E[:, u % npar, 0:live * 128], transpose=True)
                ctx.inc(i, "t", 16)
                marks["TR"][u] = ctx.cnt["t"]

            if u + 1 < nun:
                emit_scores(u + 1)
            emit_av(u)

        hL, qiL = units[-1]
        qbaL = 64 * (hL % 2)

        @block.scalar
        def _(a):
            a.wait_ge(ctx.sems["p"], marks["AV"][nun - 1])
            i = a.activation(out=ctx_bf[qbaL:qbaL + 64, hL // 2,
                                        128 * qiL:128 * (qiL + 1)],
                             in_=ps_c[qbaL:qbaL + 64, :], func=AF.Copy,
                             bias=0.0, scale=1.0)
            ctx.inc(i, "a")
            marks["EV"][nun - 1] = ctx.cnt["a"]
    _fence(ctx)


def emit_l0_attn(ctx, consts, wd, tabs, xT, scr_dram):
    nc, block = ctx.nc, ctx.block
    with ExitStack() as st:
        xr_bf = st.enter_context(nc.sbuf_tensor(_u("l0_xr"), [128, 8, 2048], BF16))
        emit_rope(ctx, wd["xro"], tabs, xr_bf)

        qt_bf = st.enter_context(nc.sbuf_tensor(_u("l0_qt"), [128, 8, 2048], BF16))
        ktd_bf = st.enter_context(nc.sbuf_tensor(_u("l0_kt"), [128, 4, 2048], BF16))
        v_bf = st.enter_context(nc.sbuf_tensor(_u("l0_v"), [128, 16, 256], BF16))

        with ExitStack() as st2:
            saved, ctx.stack = ctx.stack, st2
            wq = load_weight_bf16(ctx, wd["l0wq"], 1024, 0, 1024, "l0wq")

            def evq(v, psap, oc, tg):
                v.tensor_copy(out=qt_bf[:, oc, tg * 512:(tg + 1) * 512], in_=psap)
            emit_proj_featmajor(ctx, wq, 1024, 1024, xr_bf, evq, "l0psq")
            ctx.stack = saved

        with ExitStack() as st2:
            saved, ctx.stack = ctx.stack, st2
            wk = load_weight_bf16(ctx, wd["l0wk"], 1024, 0, 256, "l0wk")

            def evk(v, psap, oc, tg):
                for gg in range(2):
                    gidx = 2 * oc + gg
                    for hf in range(2):
                        v.tensor_scalar(out=ktd_bf[64 * hf:64 * hf + 64, gidx,
                                                   tg * 512:(tg + 1) * 512],
                                        in0=psap[64 * gg:64 * gg + 64, :], scalar1=0.125,
                                        scalar2=None, op0=ALU.mult)
            emit_proj_featmajor(ctx, wk, 1024, 256, xr_bf, evk, "l0psk")
            ctx.stack = saved

        with ExitStack() as st2:
            saved, ctx.stack = ctx.stack, st2
            wv = load_weight_bf16(ctx, wd["l0wv"], 1024, 0, 256, "l0wv")
            with nc.psum_tensor(_u("l0psv"), [128, 2, 512], F32) as psv:
                for tc in range(NKC):
                    par = tc % 2

                    @block.tensor
                    def _(te, tc=tc, par=par):
                        if tc == 0:
                            ctx.wait_all(te)
                        if tc >= 2:
                            ctx.wait(te, "v")
                        for ic in range(NCH):
                            i = te.matmul(psv[:, par, 0:256],
                                          xr_bf[:, ic, tc * 128:(tc + 1) * 128],
                                          wv[:, ic, :], start=(ic == 0), stop=(ic == NCH - 1))
                        ctx.inc(i, "p")

                    @block.vector
                    def _(v, tc=tc, par=par):
                        ctx.wait(v, "p")
                        v.tensor_copy(out=v_bf[:, tc, :], in_=psv[:, par, 0:256])
                        ctx.inc(v.drain(), "v")
            _fence(ctx)
            ctx.stack = saved

        emit_attention(ctx, consts, 0, qt_bf, ktd_bf, v_bf, None, xr_bf)

        with ExitStack() as st2:
            saved, ctx.stack = ctx.stack, st2
            wo = load_weight_bf16(ctx, wd["l0wo"], 1024, 0, 1024, "l0wo")

            def evo(v, psap, oc, tg):
                v.tensor_tensor(out=xT[:, oc, tg * 512:(tg + 1) * 512],
                                in0=psap, in1=xT[:, oc, tg * 512:(tg + 1) * 512], op=ALU.add)
            emit_proj_featmajor(ctx, wo, 1024, 1024, xr_bf, evo, "l0pso")
            ctx.stack = saved
    emit_layernorm_inplace(ctx, consts, xT)


def emit_l0_ffn(ctx, consts, wd, xT):
    nc, block = ctx.nc, ctx.block
    with ExitStack() as st:
        q_bf = st.enter_context(nc.sbuf_tensor(_u("f0_q"), [128, 8, 2048], BF16))
        sc2 = st.enter_context(nc.sbuf_tensor(_u("f0_sc2"), [128, 2], F32))
        scs = st.enter_context(nc.sbuf_tensor(_u("f0_scs"), [1, 4], F32))
        bias = st.enter_context(nc.sbuf_tensor(_u("f0_bias"), [128, 8], F32))

        @block.gpsimd
        def _(g):
            ctx.wait_all(g)
            ctx.dma(g, scs[0:1, 0:2], wd["l0sc"][0:1, 0:2])
            ctx.dma(g, bias[:, :], wd["l0fb"][:, :])

        @block.vector
        def _(v):
            ctx.wait(v, "d")
            i = v.reciprocal(out=scs[0:1, 2:3], in_=scs[0:1, 0:1])
            ctx.inc(i, "v")
            ctx.wait(v, "v")
            v.tensor_tensor(out=scs[0:1, 3:4], in0=scs[0:1, 0:1],
                            in1=scs[0:1, 1:2], op=ALU.mult)
            ctx.inc(v.drain(), "v")

        with nc.psum_tensor(_u("f0psb"), [128, 2], F32) as psb:
            @block.tensor
            def _(te):
                ctx.wait(te, "v")
                i = te.matmul(psb[:, :], consts["ones_row_f"][0:1, :], scs[0:1, 2:4],
                              start=True, stop=True)
                ctx.inc(i, "p")

            @block.vector
            def _(v):
                ctx.wait(v, "p")
                v.tensor_copy(out=sc2[:, :], in_=psb[:, :])
                ctx.inc(v.drain(), "v")
        _fence(ctx)

        with ExitStack() as stq:
            t = stq.enter_context(nc.sbuf_tensor(_u("f0_t"), [128, 2048], F32))
            sg = stq.enter_context(nc.sbuf_tensor(_u("f0_sg"), [128, 2048], F32))
            aa = stq.enter_context(nc.sbuf_tensor(_u("f0_a"), [128, 2048], F32))
            ib = stq.enter_context(nc.sbuf_tensor(_u("f0_ib"), [128, 2048], I32))
            sp = stq.enter_context(nc.sbuf_tensor(_u("f0_sp"), [128, 2048], F32))
            uu = stq.enter_context(nc.sbuf_tensor(_u("f0_u"), [128, 2048], F32))
            vv = stq.enter_context(nc.sbuf_tensor(_u("f0_v"), [128, 2048], F32))
            bb = stq.enter_context(nc.sbuf_tensor(_u("f0_b"), [128, 2048], F32))
            for c in range(NCH):
                @block.vector
                def _(v, c=c):
                    if c == 0:
                        ctx.wait_all(v)
                    else:
                        ctx.wait(v, "g")
                    v.tensor_scalar(out=t[:, :], in0=xT[:, c, :], scalar1=sc2[:, 0:1],
                                    scalar2=448.0, op0=ALU.mult, op1=ALU.min)
                    v.tensor_scalar_max(t[:, :], t[:, :], -448.0)
                    ctx.inc(v.drain(), "v")

                @block.scalar
                def _(a, c=c):
                    ctx.wait(a, "v")
                    ctx.wait(a, "g")
                    a.activation(out=sg[:, :], in_=t[:, :], func=AF.Sign, bias=0.0, scale=1.0)
                    i = a.activation(out=aa[:, :], in_=t[:, :], func=AF.Abs, bias=0.0, scale=1.0)
                    ctx.inc(i, "a")

                @block.vector
                def _(v, c=c):
                    ctx.wait(v, "a")
                    v.tensor_scalar(out=ib[:, :], in0=aa[:, :].bitcast(I32),
                                    scalar1=consts["expmask"][:, 0:1], scalar2=None,
                                    op0=ALU.bitwise_and)
                    v.tensor_scalar(out=sp[:, :], in0=ib[:, :].bitcast(F32),
                                    scalar1=0.125, scalar2=2.0 ** -9,
                                    op0=ALU.mult, op1=ALU.max)
                    v.reciprocal(out=uu[:, :], in_=sp[:, :])
                    v.tensor_tensor(out=uu[:, :], in0=aa[:, :], in1=uu[:, :], op=ALU.mult)
                    v.tensor_scalar_min(uu[:, :], uu[:, :], 16.0)
                    v.tensor_scalar(out=vv[:, :], in0=uu[:, :], scalar1=0.5,
                                    scalar2=None, op0=ALU.add)
                    for cbit in (8.0, 8.0, 4.0, 2.0, 1.0):
                        v.tensor_scalar(out=bb[:, :], in0=vv[:, :], scalar1=cbit,
                                        scalar2=None, op0=ALU.is_ge)
                        v.scalar_tensor_tensor(out=vv[:, :], in0=bb[:, :], scalar=-cbit,
                                               in1=vv[:, :], op0=ALU.mult, op1=ALU.add)
                    v.tensor_tensor(out=uu[:, :], in0=uu[:, :], in1=vv[:, :], op=ALU.subtract)
                    v.tensor_scalar(out=uu[:, :], in0=uu[:, :], scalar1=0.5,
                                    scalar2=None, op0=ALU.add)
                    v.tensor_tensor(out=uu[:, :], in0=uu[:, :], in1=sp[:, :], op=ALU.mult)
                    ctx.inc(v.drain(), "v")

                @block.gpsimd
                def _(g, c=c):
                    ctx.wait(g, "v")
                    g.tensor_tensor(out=q_bf[:, c, :], in0=uu[:, :], in1=sg[:, :], op=ALU.mult)
                    ctx.inc(g.drain(), "g")
        _fence(ctx)

        with ExitStack() as st2:
            saved, ctx.stack = ctx.stack, st2
            fw = load_weight_bf16(ctx, wd["l0fw"], 1024, 0, 1024, "l0fw")
            with nc.sbuf_tensor(_u("f0_ev"), [128, 512], F32) as ev:
                def evf(v, psap, oc, tg):
                    v.tensor_scalar(out=ev[:, :], in0=psap, scalar1=sc2[:, 1:2],
                                    scalar2=bias[:, oc:oc + 1], op0=ALU.mult, op1=ALU.add)
                    v.tensor_tensor(out=xT[:, oc, tg * 512:(tg + 1) * 512],
                                    in0=ev[:, :], in1=xT[:, oc, tg * 512:(tg + 1) * 512],
                                    op=ALU.add)
                emit_proj_featmajor(ctx, fw, 1024, 1024, q_bf, evf, "f0ps")
            ctx.stack = saved
    emit_layernorm_inplace(ctx, consts, xT)


def emit_l1_attn(ctx, consts, wd, xT, xbf, scr_dram):
    nc, block = ctx.nc, ctx.block
    with ExitStack() as st:
        qt_bf = st.enter_context(nc.sbuf_tensor(_u("l1_qt"), [128, 8, 2048], BF16))
        kt_bf = st.enter_context(nc.sbuf_tensor(_u("l1_kt"), [128, 8, 2048], BF16))
        v_bf = st.enter_context(nc.sbuf_tensor(_u("l1_v"), [128, 16, 1024], BF16))
        bq = st.enter_context(nc.sbuf_tensor(_u("l1_bq"), [128, 8], F32))
        bk = st.enter_context(nc.sbuf_tensor(_u("l1_bk"), [128, 8], F32))
        bo = st.enter_context(nc.sbuf_tensor(_u("l1_bo"), [128, 8], F32))
        bv_scope = ExitStack()
        bv_bc = bv_scope.enter_context(nc.sbuf_tensor(_u("l1_bvbc"), [128, 1024], BF16))
        bvr_scope = ExitStack()
        bvr = bvr_scope.enter_context(nc.sbuf_tensor(_u("l1_bvr"), [1, 1024], F32))

        @block.gpsimd
        def _(g):
            ctx.wait_all(g)
            ctx.dma(g, bq[:, :], wd["l1bq"][:, :])
            ctx.dma(g, bk[:, :], wd["l1bk"][:, :])
            ctx.dma(g, bo[:, :], wd["l1bo"][:, :])
            ctx.dma(g, bvr[0:1, :], wd["l1bv_row"].rearrange("(o a) b -> o (a b)", o=1)[0:1, :])

        with nc.psum_tensor(_u("l1psbv"), [128, 2, 512], F32) as psbv:
            @block.tensor
            def _(te):
                ctx.wait(te, "d")
                for half in range(2):
                    i = te.matmul(psbv[:, half, :], consts["ones_row_f"][0:1, :],
                                  bvr[0:1, half * 512:(half + 1) * 512],
                                  start=True, stop=True)
                ctx.inc(i, "p")

            @block.vector
            def _(v):
                ctx.wait(v, "p")
                v.tensor_copy(out=bv_bc[:, 0:512], in_=psbv[:, 0, :])
                v.tensor_copy(out=bv_bc[:, 512:1024], in_=psbv[:, 1, :])
                ctx.inc(v.drain(), "v")
        _fence(ctx)
        bvr_scope.close()

        def evq(v, psap, oc, tg):
            v.tensor_scalar(out=qt_bf[:, oc, tg * 512:(tg + 1) * 512], in0=psap,
                            scalar1=bq[:, oc:oc + 1], scalar2=None, op0=ALU.add)
        for half in range(2):
            with ExitStack() as st2:
                saved, ctx.stack = ctx.stack, st2
                wq = load_weight_bf16(ctx, wd["l1wq"], 1024, half * 512,
                                      (half + 1) * 512, "l1wq")
                emit_proj_featmajor(ctx, wq, 1024, 512, xbf, evq, "l1psq", oc0=half * 4)
                ctx.stack = saved

        def evk(v, psap, oc, tg):
            v.tensor_scalar(out=kt_bf[:, oc, tg * 512:(tg + 1) * 512], in0=psap,
                            scalar1=bk[:, oc:oc + 1], scalar2=0.125,
                            op0=ALU.add, op1=ALU.mult)
        for half in range(2):
            with ExitStack() as st2:
                saved, ctx.stack = ctx.stack, st2
                wk = load_weight_bf16(ctx, wd["l1wk"], 1024, half * 512,
                                      (half + 1) * 512, "l1wk")
                emit_proj_featmajor(ctx, wk, 1024, 512, xbf, evk, "l1psk", oc0=half * 4)
                ctx.stack = saved

        for vhalf in range(2):
          with ExitStack() as st2:
            saved, ctx.stack = ctx.stack, st2
            wv = load_weight_bf16(ctx, wd["l1wv"], 1024, vhalf * 512,
                                  (vhalf + 1) * 512, "l1wv")
            with nc.psum_tensor(_u("l1psv"), [128, 2, 512], F32) as psv:
                for tc in range(NKC):
                    for vh in [vhalf]:
                        par = tc % 2

                        @block.tensor
                        def _(te, tc=tc, vh=vh, par=par):
                            if tc == 0:
                                ctx.wait_all(te)
                            else:
                                ctx.wait(te, "v")
                            for ic in range(NCH):
                                i = te.matmul(psv[:, par, :],
                                              xbf[:, ic, tc * 128:(tc + 1) * 128],
                                              wv[:, ic, :],
                                              start=(ic == 0), stop=(ic == NCH - 1))
                            ctx.inc(i, "p")

                        @block.vector
                        def _(v, tc=tc, vh=vh, par=par):
                            ctx.wait(v, "p")
                            v.tensor_tensor(out=v_bf[:, tc, vh * 512:(vh + 1) * 512],
                                            in0=psv[:, par, :],
                                            in1=bv_bc[:, vh * 512:(vh + 1) * 512], op=ALU.add)
                            ctx.inc(v.drain(), "v")
            _fence(ctx)
            ctx.stack = saved

        bv_scope.close()
        emit_attention(ctx, consts, 1, qt_bf, kt_bf, v_bf, None, xbf)

        for half in range(2):
            with ExitStack() as st2:
                saved, ctx.stack = ctx.stack, st2
                wo = load_weight_bf16(ctx, wd["l1wo"], 1024, half * 512,
                                      (half + 1) * 512, "l1wo")
                with nc.sbuf_tensor(_u("l1_ev"), [128, 512], F32) as ev:
                    def evo(v, psap, oc, tg):
                        v.tensor_scalar(out=ev[:, :], in0=psap, scalar1=bo[:, oc:oc + 1],
                                        scalar2=None, op0=ALU.add)
                        v.tensor_tensor(out=xT[:, oc, tg * 512:(tg + 1) * 512],
                                        in0=ev[:, :],
                                        in1=xT[:, oc, tg * 512:(tg + 1) * 512],
                                        op=ALU.add)
                    emit_proj_featmajor(ctx, wo, 1024, 512, xbf, evo, "l1pso",
                                        oc0=half * 4)
                ctx.stack = saved
    emit_layernorm_inplace(ctx, consts, xT)


def emit_l1_ffn(ctx, consts, wd, xT, xbf, scr2):
    nc, block = ctx.nc, ctx.block
    with ExitStack() as st:
        b1 = st.enter_context(nc.sbuf_tensor(_u("f1_b1"), [128, 32], F32))
        b2 = st.enter_context(nc.sbuf_tensor(_u("f1_b2"), [128, 8], F32))
        scl = st.enter_context(nc.sbuf_tensor(_u("f1_scl"), [1, 1], F32))
        se = st.enter_context(nc.sbuf_tensor(_u("f1_se"), [1, 8, 1024], BF16))
        se2q = st.enter_context(nc.sbuf_tensor(_u("f1_se2q"), [1, 16, 256], BF16))
        p_sb = st.enter_context(nc.sbuf_tensor(_u("f1_p"), [1, 4, 2048], BF16))
        r_sb = st.enter_context(nc.sbuf_tensor(_u("f1_r"), [1, 16, 512], BF16))
        hT = st.enter_context(nc.sbuf_tensor(_u("f1_h"), [128, 32, 512], BF16))

        @block.gpsimd
        def _(g):
            ctx.wait_all(g)
            ctx.dma(g, b1[:, :], wd["l1b1"][:, :])
            ctx.dma(g, b2[:, :], wd["l1b2"][:, :])
            ctx.dma(g, scl[0:1, 0:1], wd["l1sc"][0:1, 0:1])

        do_prep = "noprep" not in DBG
        lt_scope = ExitStack()
        lt = lt_scope.enter_context(nc.sbuf_tensor(_u("f1_lt"), [1, 8, 1024], BF16))
        with nc.sbuf_tensor(_u("f1_ptt"), [128, 128], BF16) as ptt:
          if do_prep:
              for c in range(NCH):
                  @block.scalar
                  def _(a, c=c):
                      if c == 0:
                          ctx.wait_all(a)
                      else:
                          ctx.wait(a, "d")
                      i = a.dma_start(out=ptt[:, :], in_=xbf[:, c, 0:128], transpose=True)
                      ctx.inc(i, "t", 16)

                  @block.gpsimd
                  def _(g, c=c):
                      ctx.wait(g, "t")
                      ctx.dma(g, scr2[:, c * 128:(c + 1) * 128], ptt[1:17, :])

              @block.gpsimd
              def _(g):
                  ctx.wait(g, "d")
                  i = g.dma_start(out=lt[0:1, :, :],
                                  in_=scr2.rearrange("(o r) p -> o r p", o=1)[0:1, 0:16:2, :])
                  ctx.inc(i, "d", 16)
          _fence(ctx)

        @block.vector
        def _(v):
            if not do_prep:
                ctx.inc(v.drain(), "v")
                return
            ctx.wait(v, "v")
            for i8 in range(8):
                v.tensor_scalar_mul(se[0:1, i8, :], lt[0:1, i8, :], scl[0:1, 0:1])
            ctx.inc(v.drain(), "v")
            ctx.wait(v, "v")
            for i4 in range(4):
                for q4 in range(4):
                    for half in range(2):
                        v.tensor_copy(out=se2q[0:1, i4 * 4 + q4, half * 128:(half + 1) * 128],
                                      in_=se[0:1, 4 + i4, half * 512 + q4:(half + 1) * 512:4])
            ctx.inc(v.drain(), "v")
        _fence(ctx)
        lt_scope.close()

        with nc.psum_tensor(_u("f1psp"), [1, 4, 512], F32) as psp:
            for tg in (range(NTG) if do_prep else []):
                @block.tensor
                def _(te, tg=tg):
                    if tg == 0:
                        ctx.wait_all(te)
                    else:
                        ctx.wait(te, "v")
                    for i4 in range(4):
                        for ic in range(NCH):
                            i = te.matmul(psp[:, i4, :],
                                          xbf[:, ic, 2 + 2 * i4:3 + 2 * i4],
                                          xbf[:, ic, tg * 512:(tg + 1) * 512],
                                          start=(ic == 0), stop=(ic == NCH - 1))
                    ctx.inc(i, "p")

                @block.vector
                def _(v, tg=tg):
                    ctx.wait(v, "p")
                    v.tensor_copy(out=p_sb[0:1, :, tg * 512:(tg + 1) * 512], in_=psp[0:1, :, :])
                    ctx.inc(v.drain(), "v")
        _fence(ctx)

        for tg in range(NTG):
            for hs in range(4):
                with ExitStack() as st2:
                    saved, ctx.stack = ctx.stack, st2
                    w1 = load_weight_bf16(ctx, wd["l1w1"], 1024, hs * 1024, (hs + 1) * 1024,
                                          "f1w1")
                    with nc.psum_tensor(_u("f1ps1"), [128, 2, 512], F32) as ps1:
                        for hb8 in range(8):
                            hb = hs * 8 + hb8
                            par = hb8 % 2

                            @block.tensor
                            def _(te, tg=tg, hs=hs, hb8=hb8, hb=hb, par=par):
                                if hb8 == 0:
                                    ctx.wait_all(te)
                                if hb8 >= 2:
                                    ctx.wait(te, "a")
                                for ic in range(NCH):
                                    i = te.matmul(ps1[:, par, :],
                                              w1[:, ic, hb8 * 128:(hb8 + 1) * 128],
                                              xbf[:, ic, tg * 512:(tg + 1) * 512],
                                              start=(ic == 0),
                                              stop=(ic == NCH - 1 and "nolora" in DBG))
                                if "nolora" not in DBG:
                                    i4 = hb // 8
                                    jsl = (hb % 8) * 128
                                    i = te.matmul(ps1[:, par, :], se[0:1, i4, jsl:jsl + 128],
                                                  p_sb[0:1, i4, tg * 512:(tg + 1) * 512],
                                                  start=False, stop=True)
                                ctx.inc(i, "p")

                            @block.scalar
                            def _(a, hb=hb, par=par):
                                ctx.wait(a, "p")
                                i = a.activation(out=hT[:, hb, :], in_=ps1[:, par, :],
                                                 func=AF.Gelu, bias=b1[:, hb:hb + 1],
                                                 scale=1.0)
                                ctx.inc(i, "a")
                    _fence(ctx)
                    ctx.stack = saved

            if "dumph" in DBG and tg == 0:
                with nc.sbuf_tensor(_u("dmp"), [128, 512], F32) as dmp:
                    for hc in range(32):
                        @block.gpsimd
                        def _(g, hc=hc):
                            ctx.wait_all(g)
                            i = g.tensor_copy(out=dmp[:, :], in_=hT[:, hc, :])
                            ctx.inc(i, "g")
                            ctx.wait(g, "g")
                            ctx.dma(g, DBG_DUMP[0][(hc % 8) * 128:(hc % 8 + 1) * 128,
                                                   (hc // 8) * 512:(hc // 8 + 1) * 512],
                                    dmp[:, :], name="o")
                            ctx.wait(g, "o")
                _fence(ctx)

            for i4h in (range(2) if do_prep else []):
                with nc.psum_tensor(_u("f1psr"), [1, 8, 512], F32) as psr:
                    @block.tensor
                    def _(te, tg=tg, i4h=i4h):
                        ctx.wait_all(te)
                        for i4x in range(2):
                            i4 = 2 * i4h + i4x
                            for q4 in range(4):
                                for kc in range(8):
                                    i = te.matmul(psr[:, i4x * 4 + q4, :],
                                                  xbf[:, kc, 10 + 2 * i4:11 + 2 * i4],
                                                  hT[:, q4 * 8 + kc, :],
                                                  start=(kc == 0), stop=(kc == 7))
                        ctx.inc(i, "p")

                    @block.vector
                    def _(v, tg=tg, i4h=i4h):
                        ctx.wait(v, "p")
                        v.tensor_copy(out=r_sb[0:1, i4h * 8:(i4h + 1) * 8, :],
                                      in_=psr[0:1, :, :])
                        ctx.inc(v.drain(), "v")
                _fence(ctx)

            with nc.sbuf_tensor(_u("f1_ev2"), [128, 512], F32) as ev2, \
                 nc.psum_tensor(_u("f1ps2"), [128, 2, 512], F32) as ps2:
                for oc in range(NCH):
                    par = oc % 2
                    i4 = oc // 2
                    with ExitStack() as st2:
                        saved, ctx.stack = ctx.stack, st2
                        w2 = load_weight_bf16(ctx, wd["l1w2"], 4096, oc * 128,
                                              (oc + 1) * 128, "f1w2")

                        @block.tensor
                        def _(te, tg=tg, oc=oc, par=par, i4=i4, w2=w2):
                            ctx.wait_all(te)
                            for hc in range(32):
                                i = te.matmul(ps2[:, par, :], w2[:, hc, :],
                                          hT[:, hc, :], start=(hc == 0),
                                          stop=(hc == 31 and "nolora" in DBG))
                            if "nolora" not in DBG:
                                for q4 in range(4):
                                    i = te.matmul(ps2[:, par, :],
                                                  se2q[0:1, i4 * 4 + q4,
                                                       (oc % 2) * 128:(oc % 2) * 128 + 128],
                                                  r_sb[0:1, i4 * 4 + q4, :],
                                                  start=False, stop=(q4 == 3))
                            ctx.inc(i, "p")

                        @block.vector
                        def _(v, tg=tg, oc=oc, par=par):
                            ctx.wait(v, "p")
                            v.tensor_scalar(out=ev2[:, :], in0=ps2[:, par, :],
                                            scalar1=b2[:, oc:oc + 1], scalar2=None,
                                            op0=ALU.add)
                            v.tensor_tensor(out=xT[:, oc, tg * 512:(tg + 1) * 512],
                                            in0=ev2[:, :],
                                            in1=xT[:, oc, tg * 512:(tg + 1) * 512],
                                            op=ALU.add)
                            ctx.inc(v.drain(), "v")
                        _fence(ctx)
                        ctx.stack = saved
    if "noln" not in DBG:
        emit_layernorm_inplace(ctx, consts, xT)


def build_program(stages):
    nc = bass.Bass("TRN2", target_bir_lowering=False, detect_race_conditions=False)
    cts = _make_consts(nc)
    wd = {}

    def din(nm, shp):
        wd[nm] = nc.dram_tensor(nm, shp, F32, kind="ExternalInput").ap()

    if "l0a" in stages:
        din("l0wq", [1024, 1024]); din("l0wk", [1024, 256])
        din("l0wv", [1024, 256]); din("l0wo", [1024, 1024])
    if "l0f" in stages:
        din("l0fw", [1024, 1024]); din("l0fb", [128, 8]); din("l0sc", [1, 2])
    if "l1a" in stages:
        for nm in ("l1wq", "l1wk", "l1wv", "l1wo"):
            din(nm, [1024, 1024])
        din("l1bq", [128, 8]); din("l1bk", [128, 8])
        din("l1bv_row", [8, 128]); din("l1bo", [128, 8])
    if "l1f" in stages:
        din("l1w1", [1024, 4096]); din("l1b1", [128, 32])
        din("l1w2", [4096, 1024]); din("l1b2", [128, 8]); din("l1sc", [1, 1])
    tabs = {}
    if "l0a" in stages:
        for nm in ("coshi", "coslo", "sinhi", "sinlo"):
            tabs[nm] = nc.dram_tensor(nm, [512, 2048], F32, kind="ExternalInput").ap()
    xin, xout, xro = [], [], []
    for j in range(BPC):
        xin.append(nc.dram_tensor(f"xin{j}", [1024, 2048], F32, kind="ExternalInput").ap())
        xout.append(nc.dram_tensor(f"xout{j}", [1024, 2048], F32, kind="ExternalOutput").ap())
        if "l0a" in stages:
            xro.append(nc.dram_tensor(f"xro{j}", [1024, 2048], F32, kind="ExternalInput").ap())
    scr = nc.dram_tensor("scr", [256, 8], F32, kind="Internal").ap()
    if "dump_att" in DBG:
        DBG_DRAM[0] = nc.dram_tensor("dbgatt", [128, 4200], F32, kind="ExternalOutput").ap()
    if "dump_ctx" in DBG:
        DBG_DRAM[0] = nc.dram_tensor("dbgatt", [1024, 2048], F32, kind="ExternalOutput").ap()
    if "dump_negm" in DBG:
        DBG_DRAM2[0] = nc.dram_tensor("dbgnegm", [128, 2128], F32, kind="ExternalOutput").ap()
    if "dumph" in DBG:
        DBG_DUMP[0] = nc.dram_tensor("dbgout", [1024, 2048], F32, kind="ExternalOutput").ap()
    scr2 = nc.dram_tensor("scr2", [16, 1024], BF16, kind="Internal").ap()

    need_bf = ("l1a" in stages) or ("l1f" in stages)
    wshapes = {"l0wq": (1024, 1024), "l0wk": (1024, 256), "l0wv": (1024, 256),
               "l0wo": (1024, 1024), "l0fw": (1024, 1024),
               "l1wq": (1024, 1024), "l1wk": (1024, 1024), "l1wv": (1024, 1024),
               "l1wo": (1024, 1024), "l1w1": (1024, 4096), "l1w2": (4096, 1024)}
    w16 = {}
    for nm, shp in wshapes.items():
        if nm in wd:
            w16[nm] = nc.dram_tensor(nm + "_16", list(shp), BF16, kind="Internal").ap()
    with ExitStack() as stack:
        blk = stack.enter_context(nc.Block())
        ctx = Ctx(nc, blk, stack)
        consts = _load_consts(ctx, cts)
        _fence(ctx)
        for nm in w16:
            emit_weight_prepass(ctx, wd[nm], w16[nm], *wshapes[nm], nm)
        for nm in w16:
            wd[nm] = w16[nm]
        xT = stack.enter_context(nc.sbuf_tensor(_u("xT"), [128, 8, 2048], F32))
        for j in range(BPC):
            with ExitStack() as bst:
                emit_load_stream(ctx, xin[j], xT, None)
                if "l0a" in stages:
                    wd2 = dict(wd)
                    wd2["xro"] = xro[j]
                    emit_l0_attn(ctx, consts, wd2, tabs, xT, scr)
                if "l0f" in stages:
                    emit_l0_ffn(ctx, consts, wd, xT)
                xbf = None
                if need_bf:
                    xbf = bst.enter_context(nc.sbuf_tensor(_u("xbf"), [128, 8, 2048], BF16))
                if "l1a" in stages:
                    emit_refresh_xbf(ctx, xT, xbf)
                    emit_l1_attn(ctx, consts, wd, xT, xbf, scr)
                if "l1f" in stages:
                    emit_refresh_xbf(ctx, xT, xbf)
                    emit_l1_ffn(ctx, consts, wd, xT, xbf, scr2)
                emit_store_stream(ctx, xT, xout[j])
    return nc


def _host_prep(inputs):
    d = {}
    d["l0wq"] = inputs["L0_attn_W_q"].T
    d["l0wk"] = inputs["L0_attn_W_k"].T
    d["l0wv"] = inputs["L0_attn_W_v"].T
    d["l0wo"] = inputs["L0_attn_W_o"].T
    d["l0fw"] = inputs["L0_ffn_weight_fp8"].T
    d["l0fb"] = inputs["L0_ffn_bias"].reshape(8, 128).T
    d["l0sc"] = np.stack([np.asarray(inputs["L0_ffn_input_scale"]).reshape(()),
                          np.asarray(inputs["L0_ffn_weight_scale"]).reshape(())]).reshape(1, 2)
    d["l1wq"] = inputs["L1_attn_Wq"].T
    d["l1wk"] = inputs["L1_attn_Wk"].T
    d["l1wv"] = inputs["L1_attn_Wv"].T
    d["l1wo"] = inputs["L1_attn_Wo"].T
    d["l1bq"] = inputs["L1_attn_bq"].reshape(8, 128).T
    d["l1bk"] = inputs["L1_attn_bk"].reshape(8, 128).T
    d["l1bv_row"] = inputs["L1_attn_bv"].reshape(8, 128)
    d["l1bo"] = inputs["L1_attn_bo"].reshape(8, 128).T
    d["l1w1"] = inputs["L1_ffn_W1"].T
    d["l1b1"] = inputs["L1_ffn_b1"].reshape(32, 128).T
    d["l1w2"] = inputs["L1_ffn_W2"].T
    d["l1b2"] = inputs["L1_ffn_b2"].reshape(8, 128).T
    d["l1sc"] = np.asarray(inputs["L1_ffn_scale"], np.float32).reshape(1, 1)
    f = np.arange(512, dtype=np.float64) / 512.0
    theta = 1.0 / (10000.0 ** f)
    ang = theta[:, None] * np.arange(S, dtype=np.float64)[None, :]
    for nm, fn in (("cos", np.cos), ("sin", np.sin)):
        v = fn(ang)
        hi = v.astype(np.float32)
        lo = (v - hi.astype(np.float64)).astype(np.float32)
        d[nm + "hi"] = hi
        d[nm + "lo"] = lo
    return {k: np.ascontiguousarray(v, dtype=np.float32) for k, v in d.items()}


def _get_program(stages):
    key = tuple(stages)
    if key not in _PROGRAM_CACHE:
        _PROGRAM_CACHE[key] = build_program(stages)
    return _PROGRAM_CACHE[key]


def _prog_input_names(nc):
    names = set()
    for a in nc.m.functions[0].allocations:
        if isinstance(a, mybir.MemoryLocationSet) and a.kind == "ExternalInput":
            for ml in a.memorylocations:
                names.add(ml.name)
    return names


def run_stage_group(stages, shared, streams, xros):
    nc = _get_program(stages)
    names = _prog_input_names(nc)
    in_maps = []
    for c in range(NC_N):
        m = {k: v for k, v in shared.items() if k in names}
        for j in range(BPC):
            m[f"xin{j}"] = streams[c][j]
            if f"xro{j}" in names:
                m[f"xro{j}"] = xros[c][j]
        in_maps.append(m)
    res = run_bass_kernel_spmd(nc, in_maps, core_ids=list(range(NC_N)))
    return [[np.asarray(res.results[c][f"xout{j}"]) for j in range(BPC)] for c in range(NC_N)]


STAGE_GROUPS = [("l0a", "l0f", "l1a", "l1f")]


def kernel(**inputs):
    shared = _host_prep(inputs)
    x = np.asarray(inputs["L0_attn_X"], np.float32)
    streams, xros = [], []
    for c in range(NC_N):
        ss, rr = [], []
        for j in range(BPC):
            xb = x[c * BPC + j]
            ss.append(np.ascontiguousarray(xb.T))
            rr.append(np.ascontiguousarray(
                np.concatenate([xb[:, 0::2], xb[:, 1::2]], axis=1).T))
        streams.append(ss)
        xros.append(rr)
    for stages in STAGE_GROUPS:
        streams = run_stage_group(stages, shared, streams, xros)
    out = np.empty((B, S, D), np.float32)
    for c in range(NC_N):
        for j in range(BPC):
            out[c * BPC + j] = streams[c][j].T
    return out
